# version 01
# baseline (speedup 1.0000x reference)
import os
import numpy as np
import concourse.bass as bass
import concourse.mybir as mybir
from concourse.bass_utils import run_bass_kernel_spmd

F32 = mybir.dt.float32
BF16 = mybir.dt.bfloat16
AF = mybir.ActivationFunctionType
ALU = mybir.AluOpType
AX = mybir.AxisListType

T = 2048
NT = 16
D = 1024
NK = 8
DI = 2048
NM = 16
DS = 16
CH = 128
NCH = 16
E0, H0 = 8, 2048
E1, H1 = 16, 128
NHEAD, DK, DC, DHR = 8, 128, 32, 32
SCALE = 1.0 / float(np.sqrt(DK + DHR))
LN_EPS = 1e-5
N512 = 512

ARENA_BYTES = 49152


class Sem:
    def __init__(self, name):
        self.name = name
        self.total = 0
        self.h = None

    def inc(self, n=1):
        self.total += n
        return self.total


class Plan:
    ENGINES = ("tensor", "vector", "scalar", "gpsimd", "sync")

    def __init__(self):
        self.items = {e: [] for e in self.ENGINES}
        self.sems = {}

    def sem(self, name):
        s = self.sems.get(name)
        if s is None:
            s = self.sems[name] = Sem(name)
        return s

    def emit(self, eng, fn):
        self.items[eng].append(fn)

    def wait(self, eng, sem, val=None):
        v = sem.total if val is None else val
        if v > 0:
            self.emit(eng, lambda E, s=sem, vv=v: E.wait_ge(s.h, vv))

    def op(self, eng, fn, inc=None, inc_n=1):
        if inc is not None:
            inc.inc(inc_n)
            self.emit(eng, lambda E, f=fn, s=inc, n=inc_n: f(E).then_inc(s.h, n))
        else:
            self.emit(eng, fn)


def build_nc(debug_taps=(), stop=None):
    stop = stop if stop is not None else os.environ.get("MK_STOP", "")
    nc = bass.Bass("TRN2", target_bir_lowering=False, debug=False)
    p = Plan()
    taps = set(debug_taps)

    def din(name, shape):
        return nc.dram_tensor(name, list(shape), F32, kind="ExternalInput").ap()

    X_hbm = din("x", (T, D))
    ID_hbm = din("idm", (128, 128))
    J_hbm = din("jm", (128, 128))
    WIN = din("win", (D, DI))
    BINT = din("binT", (1, DI))
    CONVW = din("convw", (128, NM, 4))
    CONVB = din("convb", (128, NM))
    A_in = din("amat", (DS, DS))
    BP = din("bp", (DS, 1))
    CP = din("cp", (DS, 1))
    DP = din("dp", (1, 1))
    WOUT = din("wout", (DI, D))
    BOUT = din("boutr", (1, D))
    RW1 = din("rw1", (D, D))
    RB1C = din("rb1c", (128, NK))
    RW2 = din("rw2", (D, E0))
    RB2C = din("rb2c", (E0, 1))
    EW1 = din("ew1", (E0, D, H0))
    EB1C = din("eb1c", (128, E0, NM))
    EW2 = din("ew2", (E0, H0, D))
    EB2R = din("eb2r", (E0, D))
    WDQ = din("wdq", (D, DC))
    BDQC = din("bdqc", (DC, 1))
    WUQ = din("wuq", (DC + 1, NHEAD * DK))
    WDKV = din("wdkv", (D, DC))
    BDKVC = din("bdkvc", (DC, 1))
    WUK = din("wuk", (DC + 1, NHEAD * DK))
    WUV = din("wuv", (DC + 1, NHEAD * DK))
    WQR = din("wqr", (DC + 1, NHEAD * DHR))
    WKR = din("wkr", (D, DHR))
    BKRC = din("bkrc", (DHR, 1))
    WO = din("wo", (NHEAD * DK, D))
    BOR = din("bor", (1, D))
    ESEL = din("esel", (E1, D))
    KEYS = din("keys", (E1, D, H1))
    VALUES = din("values", (E1, H1, D))

    OUT = nc.dram_tensor("out", [T, D], F32, kind="ExternalOutput").ap()

    tap_outs = {}

    def tap(name, shape, dtype=F32):
        if name in taps:
            tap_outs[name] = nc.dram_tensor(
                "tap_" + name, list(shape), dtype, kind="ExternalOutput").ap()
            return tap_outs[name]
        return None

    sb = nc.alloc_sbuf_tensor

    X = sb("X", [128, NT, D], F32)
    XNT = sb("XNT", [128, NK, T], BF16)
    BIGT = sb("BIG", [128, 16384], BF16)
    ARENA = sb("ARENA", [128, ARENA_BYTES // 2], BF16)

    ARB = ARENA.ap()
    ARF = ARENA.ap().bitcast(F32)

    def arn_b(off_bytes, *shape):
        n = int(np.prod(shape))
        assert off_bytes % 2 == 0 and off_bytes + 2 * n <= ARENA_BYTES, (off_bytes, shape)
        v = ARB[:, off_bytes // 2: off_bytes // 2 + n]
        if len(shape) > 1:
            dims = " ".join(f"d{i}" for i in range(len(shape)))
            v = v.rearrange(f"p ({dims}) -> p {dims}",
                            **{f"d{i}": s for i, s in enumerate(shape)})
        return v

    def arn_f(off_bytes, *shape):
        n = int(np.prod(shape))
        assert off_bytes % 4 == 0 and off_bytes + 4 * n <= ARENA_BYTES, (off_bytes, shape)
        v = ARF[:, off_bytes // 4: off_bytes // 4 + n]
        if len(shape) > 1:
            dims = " ".join(f"d{i}" for i in range(len(shape)))
            v = v.rearrange(f"p ({dims}) -> p {dims}",
                            **{f"d{i}": s for i, s in enumerate(shape)})
        return v

    IDB = sb("IDB", [128, 128], BF16)
    IDF = sb("IDF", [128, 128], F32)
    JB = sb("JB", [128, 128], BF16)
    ONESC = sb("ONESC", [128, 1], BF16)
    ONESR = sb("ONESR", [1, N512], BF16)

    LN_SUM = sb("LN_SUM", [128, NT], F32)
    LN_SQ = sb("LN_SQ", [128, NT], F32)
    LN_MEAN = sb("LN_MEAN", [128, NT], F32)
    LN_RSTD = sb("LN_RSTD", [128, NT], F32)
    SCR8 = sb("SCR8", [128, D], F32)
    XNB = sb("XNB", [128, D], BF16)

    PQH = sb("PQH", [DS, 16 * DS], BF16)
    AMS = sb("AMS", [DS, DS], F32)
    ATS = sb("ATS", [DS, DS], F32)
    ATB = sb("ATB", [DS, DS], BF16)
    BPC = sb("BPC", [DS, 1], BF16)
    CPC = sb("CPC", [DS, 1], BF16)
    DPS = sb("DPS", [1, 1], F32)
    BT_CH = sb("BT_CH", [DS, 128], BF16)
    CT_CH = sb("CT_CH", [DS, 128], BF16)
    BTT = sb("BTT", [128, DS], BF16)
    WB_M = sb("WB_M", [128, DS], BF16)
    WCT = sb("WCT", [DS, 128], BF16)
    ATLT = sb("ATLT", [DS, DS], BF16)
    KROW = sb("KROW", [1, 128], F32)
    KROWB = sb("KROWB", [1, 128], BF16)
    TTOEP = sb("TTOEP", [128, 128], BF16)
    HT = sb("HT", [DS, 2, D], BF16)

    BINB = sb("BINB", [1, DI], BF16)
    CONVWS = sb("CONVWS", [128, NM, 4], F32)
    CONVBS = sb("CONVBS", [128, NM], F32)
    RB1S = sb("RB1S", [128, NK], F32)
    RW2B = sb("RW2B", [128, NK, E0], BF16)
    RB2S = sb("RB2S", [E0, 1], F32)
    LGTM = sb("LGTM", [128, NT, E0], F32)
    TOP_M1 = sb("TOP_M1", [128, NT], F32)
    TOP_M1N = sb("TOP_M1N", [128, NT], F32)
    TOP_M2 = sb("TOP_M2", [128, NT], F32)
    TOP_E1 = sb("TOP_E1", [128, NT, E0], F32)
    TOP_SCR = sb("TOP_SCR", [128, E0], F32)
    TOP_WS = sb("TOP_WS", [128, NT], F32)
    WTM = sb("WTM", [128, NT, E0], BF16)
    EB1S = sb("EB1S", [128, E0, NM], F32)

    TOP_MSK = sb("TOP_MSK", [128, NT, E0], F32)
    WROW = sb("WROW", [1, T], BF16)

    BDQS = sb("BDQS", [DC, 1], F32)
    BDKVS = sb("BDKVS", [DC, 1], F32)
    BKRS = sb("BKRS", [DHR, 1], F32)
    KMAX = sb("KMAX", [1, 2], F32)

    WIN_H = arn_b(0, NK, D)
    WOUT_H = arn_b(16384, NK, D)
    UC = arn_b(32768, 2, NK, 132)
    VCF = arn_b(36992, NK, 128)
    VC = arn_b(39040, 2, D)
    SYC = arn_b(43136, NK, 128)
    BOUTB = arn_b(45184, D)[0:1, :]

    PS = [nc.alloc_psum_tensor(f"PS{i}", [128, N512], F32) for i in range(8)]

    dma_x = p.sem("dma_x")
    dma_c = p.sem("dma_c")
    dma_w = p.sem("dma_w")
    s_mm = p.sem("s_mm")
    s_ev = p.sem("s_ev")
    s_vv = p.sem("s_vv")
    s_pr = p.sem("s_pr")
    s_tr = p.sem("s_tr")
    s_st = p.sem("s_st")
    s_gp = p.sem("s_gp")
    s_fb = p.sem("s_fb")
    s_ex = p.sem("s_ex")

    def dma(eng, out_ap, in_ap, sem):
        p.op(eng, lambda E, o=out_ap, i=in_ap: E.dma_start(out=o, in_=i),
             inc=sem, inc_n=16)

    dma("sync", X.ap(), X_hbm.rearrange("(n p) d -> p n d", p=128), dma_x)

    dma("sync", IDF.ap(), ID_hbm, dma_c)
    dma("gpsimd", IDB.ap(), ID_hbm, dma_c)
    dma("gpsimd", JB.ap(), J_hbm, dma_c)
    dma("sync", CONVWS.ap(), CONVW, dma_c)
    dma("sync", CONVBS.ap(), CONVB, dma_c)
    dma("sync", AMS.ap(), A_in, dma_c)
    dma("gpsimd", BPC.ap(), BP, dma_c)
    dma("gpsimd", CPC.ap(), CP, dma_c)
    dma("sync", DPS.ap(), DP, dma_c)
    dma("gpsimd", BINB.ap(), BINT, dma_c)
    dma("gpsimd", BOUTB, BOUT, dma_c)
    dma("sync", RB1S.ap(), RB1C, dma_c)
    dma("gpsimd", RW2B.ap(), RW2.rearrange("(k p) e -> p k e", p=128), dma_c)
    dma("sync", RB2S.ap(), RB2C, dma_c)
    dma("sync", EB1S.ap(), EB1C, dma_c)
    dma("sync", BDQS.ap(), BDQC, dma_c)
    dma("sync", BDKVS.ap(), BDKVC, dma_c)
    dma("sync", BKRS.ap(), BKRC, dma_c)
    CONST_N = dma_c.total

    p.emit("vector", lambda E: E.memset(ONESC.ap(), 1.0))
    p.emit("vector", lambda E: E.memset(ONESR.ap(), 1.0))
    p.emit("vector", lambda E: E.memset(TTOEP.ap(), 0.0))
    p.op("vector", lambda E: E.memset(HT.ap(), 0.0), inc=s_vv)
    V_MEMSET = s_vv.total

    def ssm_prep():
        p.wait("scalar", dma_c, CONST_N)
        p.op("scalar", lambda E: E.activation(ATS.ap(), AMS.ap(), AF.Tanh),
             inc=s_pr)
        p.wait("vector", s_pr, s_pr.total)
        p.op("vector", lambda E: E.tensor_copy(ATB.ap(), ATS.ap()))
        p.op("vector", lambda E: E.tensor_copy(PQH[0:DS, 0:DS], ATS.ap()),
             inc=s_pr)
        p.wait("tensor", s_pr, s_pr.total)
        p.wait("tensor", dma_c, CONST_N)
        p.op("tensor", lambda E: E.transpose(PS[0].ap().bitcast(BF16)[0:DS, 0:DS],
                                             ATB.ap(), IDB[0:DS, 0:DS]), inc=s_mm)
        p.wait("vector", s_mm, s_mm.total)
        p.op("vector", lambda E: E.tensor_copy(
            PQH[0:DS, DS:2 * DS], PS[0].ap().bitcast(BF16)[0:DS, 0:DS]), inc=s_pr)

        def slotP(j):
            return PQH[0:DS, (2 * j) * DS:(2 * j + 1) * DS]

        def slotQ(j):
            return PQH[0:DS, (2 * j + 1) * DS:(2 * j + 2) * DS]

        for j in range(7):
            p.wait("tensor", s_pr, s_pr.total)
            p.op("tensor", lambda E, a=slotQ(j), b=slotP(j): E.matmul(
                PS[1][0:DS, 0:DS], a, b, start=True, stop=True))
            p.op("tensor", lambda E, a=slotP(j), b=slotQ(j): E.matmul(
                PS[1][0:DS, DS:2 * DS], a, b, start=True,
                stop=True), inc=s_mm)
            p.wait("vector", s_mm, s_mm.total)
            p.op("vector", lambda E, d=slotP(j + 1): E.tensor_copy(
                d, PS[1][0:DS, 0:DS]))
            p.op("vector", lambda E, d=slotQ(j + 1): E.tensor_copy(
                d, PS[1][0:DS, DS:2 * DS]), inc=s_pr)
        P128, Q128 = slotP(7), slotQ(7)
        p.wait("vector", s_pr, s_pr.total)
        p.op("vector", lambda E: E.tensor_copy(ATLT.ap(), Q128))
        p.op("vector", lambda E: E.tensor_copy(BT_CH[0:DS, 0:1], BPC.ap()))
        p.op("vector", lambda E: E.tensor_copy(CT_CH[0:DS, 0:1], CPC.ap()),
             inc=s_pr)
        for j in range(7):
            m = 1 << j
            p.wait("tensor", s_pr, s_pr.total)
            p.op("tensor", lambda E, q=slotQ(j), mm=m: E.matmul(
                PS[1][0:DS, 0:mm], q, BT_CH[0:DS, 0:mm],
                start=True, stop=True))
            p.op("tensor", lambda E, pp=slotP(j), mm=m: E.matmul(
                PS[1][0:DS, 128:128 + mm], pp, CT_CH[0:DS, 0:mm],
                start=True, stop=True), inc=s_mm)
            p.wait("vector", s_mm, s_mm.total)
            p.op("vector", lambda E, mm=m: E.tensor_copy(
                BT_CH[0:DS, mm:2 * mm], PS[1][0:DS, 0:mm]))
            p.op("vector", lambda E, mm=m: E.tensor_copy(
                CT_CH[0:DS, mm:2 * mm], PS[1][0:DS, 128:128 + mm]),
                inc=s_pr)

        p.wait("tensor", s_pr, s_pr.total)
        p.op("tensor", lambda E: E.matmul(PS[1][0:1, 0:128], BPC.ap(),
                                          CT_CH[0:DS, 0:128], start=True,
                                          stop=True), inc=s_mm)
        p.wait("vector", s_mm, s_mm.total)
        p.op("vector", lambda E: E.tensor_copy(KROW.ap(), PS[1][0:1, 0:128]),
             inc=s_vv)
        p.wait("scalar", s_vv, s_vv.total)
        p.op("scalar", lambda E: E.add(KROW[0:1, 0:1], KROW[0:1, 0:1],
                                       DPS.ap()), inc=s_ev)
        p.wait("vector", s_ev, s_ev.total)
        p.op("vector", lambda E: E.tensor_copy(KROWB.ap(), KROW.ap()),
             inc=s_pr)
        p.wait("scalar", s_pr, s_pr.total)
        p.wait("scalar", s_vv, V_MEMSET)
        for s in range(128):
            p.op("scalar",
                 lambda E, ss=s: E.dma_start(out=TTOEP[ss:ss + 1, ss:128],
                                             in_=KROWB[0:1, 0:128 - ss]),
                 inc=s_pr, inc_n=16)
        t_ready = s_pr.total

        p.emit("vector", lambda E: E.drain())
        p.op("vector", lambda E: E.tensor_copy(WCT[0:DS, 0:127],
                                               CT_CH[0:DS, 1:128]), inc=s_pr)
        p.wait("tensor", s_pr, s_pr.total)
        p.op("tensor", lambda E: E.matmul(PS[1][0:DS, 0:1],
                                          P128, CPC.ap(), start=True,
                                          stop=True), inc=s_mm)
        p.wait("vector", s_mm, s_mm.total)
        p.op("vector", lambda E: E.tensor_copy(
            WCT[0:DS, 127:128], PS[1][0:DS, 0:1]), inc=s_pr)
        p.wait("tensor", s_pr, s_pr.total)
        p.op("tensor", lambda E: E.transpose(PS[2].ap().bitcast(BF16)[:, 0:DS],
                                             BT_CH[0:DS, 0:128],
                                             IDB[0:DS, 0:DS]), inc=s_mm)
        p.wait("vector", s_mm, s_mm.total)
        p.op("vector", lambda E: E.tensor_copy(
            BTT.ap(), PS[2].ap().bitcast(BF16)[:, 0:DS]), inc=s_pr)
        p.wait("tensor", s_pr, s_pr.total)
        p.op("tensor", lambda E: E.matmul(PS[2][:, 0:DS],
                                          JB.ap(), BTT.ap(), start=True,
                                          stop=True), inc=s_mm)
        p.wait("vector", s_mm, s_mm.total)
        p.op("vector", lambda E: E.tensor_copy(
            WB_M.ap(), PS[2][:, 0:DS]), inc=s_pr)
        return t_ready

    T_READY = ssm_prep()
    PREP_DONE = s_pr.total

    def tap_store(name, sbuf_ap, shape, dtype=F32, wait=()):
        t_ap = tap(name, shape, dtype)
        if t_ap is None:
            return
        for sem, val in wait:
            p.wait("sync", sem, val)
        p.op("sync", lambda E, o=t_ap, i=sbuf_ap: E.dma_start(out=o, in_=i),
             inc=s_st, inc_n=16)

    tap_store("ttoep", TTOEP.ap(), (128, 128), BF16, wait=[(s_pr, T_READY)])
    tap_store("wb", WB_M.ap(), (128, DS), BF16, wait=[(s_pr, PREP_DONE)])
    tap_store("wct", WCT.ap(), (DS, 128), BF16, wait=[(s_pr, PREP_DONE)])
    tap_store("atlt", ATLT.ap(), (DS, DS), BF16, wait=[(s_pr, PREP_DONE)])
    tap_store("krow", KROW.ap(), (1, 128), F32, wait=[(s_pr, PREP_DONE)])

    def layernorm(add_xn_to_x):
        p.wait("scalar", dma_x, 16)
        p.wait("vector", dma_x, 16)
        p.op("vector", lambda E: E.memset(LN_SUM.ap(), 0.0))
        p.op("vector", lambda E: E.memset(LN_MEAN.ap(), 0.0), inc=s_vv)
        p.op("vector", lambda E: E.memset(LN_SQ.ap(), 0.0), inc=s_vv)
        p.wait("scalar", s_vv, s_vv.total)
        for t in range(NT):
            p.op("scalar", lambda E, tt=t: E.activation(
                SCR8.ap(), X[:, tt, :], AF.Square,
                accum_out=LN_SQ[:, tt:tt + 1]), inc=s_ev)
            p.op("vector", lambda E, tt=t: E.tensor_scalar(
                XNB.ap().bitcast(F32)[:, 0:512], X[:, tt, 0:512], 1.0, 0.0,
                ALU.mult, ALU.add, accum_out=LN_SUM[:, tt:tt + 1]), inc=s_vv)
            p.op("vector", lambda E, tt=t: E.tensor_scalar(
                XNB.ap().bitcast(F32)[:, 0:512], X[:, tt, 512:1024], 1.0, 0.0,
                ALU.mult, ALU.add, accum_out=LN_MEAN[:, tt:tt + 1]), inc=s_vv)
        p.wait("vector", s_ev, s_ev.total)
        p.op("vector", lambda E: E.tensor_tensor(
            LN_SUM.ap(), LN_SUM.ap(), LN_MEAN.ap(), ALU.add))
        p.emit("vector", lambda E: E.drain())
        p.op("vector", lambda E: E.tensor_scalar(
            LN_MEAN.ap(), LN_SUM.ap(), 1.0 / D, 0.0, ALU.mult, ALU.add))
        p.op("vector", lambda E: E.tensor_scalar(
            LN_SQ.ap(), LN_SQ.ap(), 1.0 / D, 0.0, ALU.mult, ALU.add))
        p.emit("vector", lambda E: E.drain())
        p.op("vector", lambda E: E.tensor_tensor(
            LN_SUM.ap(), LN_MEAN.ap(), LN_MEAN.ap(), ALU.mult))
        p.emit("vector", lambda E: E.drain())
        p.op("vector", lambda E: E.tensor_tensor(
            LN_SQ.ap(), LN_SQ.ap(), LN_SUM.ap(), ALU.subtract))
        p.emit("vector", lambda E: E.drain())
        p.op("vector", lambda E: E.tensor_scalar(
            LN_SQ.ap(), LN_SQ.ap(), LN_EPS, 0.0, ALU.add, ALU.add))
        p.emit("vector", lambda E: E.drain())
        p.op("vector", lambda E: E.reciprocal(LN_SUM.ap(), LN_SQ.ap()),
             inc=s_vv)
        p.wait("scalar", s_vv, s_vv.total)
        p.op("scalar", lambda E: E.activation(LN_RSTD.ap(), LN_SUM.ap(),
                                              AF.Sqrt), inc=s_ev)
        p.wait("vector", s_ev, s_ev.total)
        for t in range(NT):
            if t >= 1:
                p.wait("vector", s_gp, s_gp.total)
                p.wait("vector", s_tr, s_tr.total)
            p.op("vector", lambda E, tt=t: E.tensor_scalar(
                SCR8.ap(), X[:, tt, :], LN_MEAN[:, tt:tt + 1],
                LN_RSTD[:, tt:tt + 1], ALU.subtract, ALU.mult))
            p.op("vector", lambda E: E.tensor_copy(XNB.ap(), SCR8.ap()),
                 inc=s_vv)
            if add_xn_to_x:
                p.wait("gpsimd", s_vv, s_vv.total)
                p.op("gpsimd", lambda E, tt=t: E.tensor_tensor(
                    X[:, tt, :], X[:, tt, :], SCR8.ap(), ALU.add), inc=s_gp)
            else:
                s_gp.inc(0)
            p.wait("tensor", s_vv, s_vv.total)
            for k in range(NK):
                bank = 6 + (k // 4)
                off = (k % 4) * 128
                p.op("tensor", lambda E, kk=k, b=bank, o=off: E.transpose(
                    PS[b].ap().bitcast(BF16)[:, o:o + 128],
                    XNB[:, kk * 128:(kk + 1) * 128], IDB.ap()),
                    inc=(s_mm if k in (3, 7) else None))
            mmv = s_mm.total
            p.wait("vector", s_mm, mmv - 1)
            for k in range(4):
                p.op("vector", lambda E, tt=t, kk=k: E.tensor_copy(
                    XNT[:, kk, tt * 128:(tt + 1) * 128],
                    PS[6].ap().bitcast(BF16)[:, kk * 128:(kk + 1) * 128]))
            p.wait("vector", s_mm, mmv)
            for k in range(4):
                p.op("vector", lambda E, tt=t, kk=k: E.tensor_copy(
                    XNT[:, 4 + kk, tt * 128:(tt + 1) * 128],
                    PS[7].ap().bitcast(BF16)[:, kk * 128:(kk + 1) * 128]),
                    inc=(s_tr if k == 3 else None))
        return s_tr.total, s_gp.total

    XNT_L0, GP_L0 = layernorm(add_xn_to_x=True)
    tap_store("ln_mean", LN_MEAN.ap(), (128, NT), F32, wait=[(s_tr, s_tr.total)])
    tap_store("ln_rstd", LN_RSTD.ap(), (128, NT), F32, wait=[(s_tr, s_tr.total)])
    tap_store("ln_sum", LN_SUM.ap(), (128, NT), F32, wait=[(s_tr, s_tr.total)])
    tap_store("ln_sq", LN_SQ.ap(), (128, NT), F32, wait=[(s_tr, s_tr.total)])
    tap_store("x0", X[:, 0, :], (128, D), F32, wait=[(dma_x, 16)])
    tap_store("xnt", XNT.ap(), (128, NK, T), BF16, wait=[(s_tr, XNT_L0)])


    def stream_mamba_weights(half):
        dma("gpsimd", WIN_H,
            WIN[:, half * 1024:(half + 1) * 1024].rearrange(
                "(k p) m -> p k m", p=128), dma_w)
        dma("gpsimd", WOUT_H,
            WOUT[half * 1024:(half + 1) * 1024, :].rearrange(
                "(k p) m -> p k m", p=128), dma_w)
        return dma_w.total

    tap_xu = tap("xu", (128, NCH, NK, 132), BF16)
    tap_vc = tap("vc", (128, NCH, D), BF16)
    tap_sy = tap("sy", (128, NCH, NK, 128), BF16)

    def mamba_half(half):
        p.wait("gpsimd", s_fb, s_fb.total)
        wv = stream_mamba_weights(half)
        p.wait("vector", s_fb, s_fb.total)
        p.op("vector", lambda E: E.memset(HT[0:DS, 0, :], 0.0))
        p.op("vector", lambda E: E.memset(UC[:, 0, :, 0:4], 0.0))
        p.op("vector", lambda E: E.memset(UC[:, 1, :, 0:4], 0.0), inc=s_vv)
        zs = s_vv.total

        p.wait("tensor", dma_w, wv - 16)
        p.wait("tensor", s_tr, XNT_L0)
        p.wait("tensor", s_pr, PREP_DONE)
        p.wait("tensor", s_vv, zs)
        p.wait("scalar", s_vv, zs)

        cv_done = {}
        for c in range(NCH):
            cur, prv = c % 2, 1 - c % 2
            for mg in range(2):
                bank = mg
                for mi in range(4):
                    m = mg * 4 + mi
                    for k in range(NK):
                        p.op("tensor", lambda E, kk=k, mm=m, b=bank, o=mi * 128, cc=c:
                             E.matmul(PS[b][:, o:o + 128],
                                      WIN_H[:, kk, mm * 128:(mm + 1) * 128],
                                      XNT[:, kk, cc * 128:(cc + 1) * 128],
                                      start=(kk == 0), stop=False))
                    gm0 = (half * NK + m) * 128
                    p.op("tensor", lambda E, b=bank, o=mi * 128, g0=gm0:
                         E.matmul(PS[b][:, o:o + 128],
                                  BINB[0:1, g0:g0 + 128],
                                  ONESR[0:1, 0:128], start=False, stop=True),
                         inc=(s_mm if mi == 3 else None))
            mm_inp = s_mm.total
            if c >= 2:
                p.wait("scalar", s_vv, cv_done[c - 2])
            p.wait("scalar", s_mm, mm_inp - 1)
            p.op("scalar", lambda E, cu=cur: E.copy(
                UC[:, cu, 0:4, 4:132],
                PS[0].ap().rearrange("p (m n) -> p m n", n=128)))
            p.wait("scalar", s_mm, mm_inp)
            p.op("scalar", lambda E, cu=cur: E.copy(
                UC[:, cu, 4:8, 4:132],
                PS[1].ap().rearrange("p (m n) -> p m n", n=128)), inc=s_ev)
            ev_u = s_ev.total
            p.wait("vector", s_ev, ev_u)
            if c > 0:
                p.op("vector", lambda E, cu=cur, pv=prv: E.tensor_copy(
                    UC[:, cu, :, 0:4], UC[:, pv, :, 128:132]))
            for m in range(NK):
                g = half * NK + m
                p.op("vector", lambda E, mm=m, gg=g, cu=cur: E.tensor_scalar(
                    VCF[:, mm, :], UC[:, cu, mm, 1:129], CONVWS[:, gg, 0:1],
                    CONVBS[:, gg:gg + 1], ALU.mult, ALU.add))
            for k in range(1, 4):
                for m in range(NK):
                    g = half * NK + m
                    p.op("vector",
                         lambda E, mm=m, gg=g, kk=k, cu=cur: E.scalar_tensor_tensor(
                             VCF[:, mm, :], UC[:, cu, mm, 1 + kk:129 + kk],
                             CONVWS[:, gg, kk:kk + 1], VCF[:, mm, :],
                             ALU.mult, ALU.add),
                         inc=(s_vv if (m == NK - 1 and k == 3) else None))
            cv_done[c] = s_vv.total
            if tap_xu is not None and half == 0:
                p.wait("sync", s_vv, cv_done[c])
                p.op("sync", lambda E, cc=c, cu=cur: E.dma_start(
                    out=tap_xu[:, cc, :, :], in_=UC[:, cu, :, :]),
                    inc=s_st, inc_n=16)
            p.wait("tensor", s_vv, cv_done[c])
            for m in range(NK):
                bank = 2 + (m // 4)
                off = (m % 4) * 128
                p.op("tensor", lambda E, mm=m, b=bank, o=off: E.transpose(
                    PS[b].ap().bitcast(BF16)[:, o:o + 128],
                    VCF[:, mm, :], IDB.ap()),
                    inc=(s_mm if m in (3, 7) else None))
            mm_tr = s_mm.total
            p.wait("vector", s_mm, mm_tr - 1)
            p.op("vector", lambda E, cu=cur: E.tensor_copy(
                VC[:, cu, 0:512], PS[2].ap().bitcast(BF16)[:, 0:512]))
            p.wait("vector", s_mm, mm_tr)
            p.op("vector", lambda E, cu=cur: E.tensor_copy(
                VC[:, cu, 512:1024], PS[3].ap().bitcast(BF16)[:, 0:512]),
                inc=s_vv)
            vc_ok = s_vv.total
            if tap_vc is not None and half == 0:
                p.wait("sync", s_vv, vc_ok)
                p.op("sync", lambda E, cc=c, cu=cur: E.dma_start(
                    out=tap_vc[:, cc, :], in_=VC[:, cu, :]), inc=s_st, inc_n=16)
            p.wait("tensor", s_vv, vc_ok)
            hcur, hnxt = c % 2, 1 - c % 2
            for m in range(NK):
                bank = 4 + (m // 4)
                off = (m % 4) * 128
                p.op("tensor", lambda E, mm=m, b=bank, o=off, cu=cur: E.matmul(
                    PS[b][:, o:o + 128],
                    VC[:, cu, mm * 128:(mm + 1) * 128], TTOEP.ap(),
                    start=True, stop=False))
                p.op("tensor", lambda E, mm=m, b=bank, o=off, hc=hcur: E.matmul(
                    PS[b][:, o:o + 128],
                    HT[0:DS, hc, mm * 128:(mm + 1) * 128], WCT.ap(),
                    start=False, stop=True),
                    inc=(s_mm if m in (3, 7) else None))
            mm_y = s_mm.total
            for n in range(2):
                p.op("tensor", lambda E, nn=n, hc=hcur: E.matmul(
                    PS[6][0:DS, 0:N512], ATLT.ap(),
                    HT[0:DS, hc, nn * 512:(nn + 1) * 512],
                    start=True, stop=False))
                p.op("tensor", lambda E, nn=n, cu=cur: E.matmul(
                    PS[6][0:DS, 0:N512], WB_M.ap(),
                    VC[:, cu, nn * 512:(nn + 1) * 512],
                    start=False, stop=True), inc=s_mm)
                mm_st = s_mm.total
                p.wait("vector", s_mm, mm_st)
                p.op("vector", lambda E, nn=n, hn=hnxt: E.tensor_copy(
                    HT[0:DS, hn, nn * 512:(nn + 1) * 512],
                    PS[6][0:DS, 0:N512]), inc=s_vv)
                p.wait("tensor", s_vv, s_vv.total)
            p.wait("scalar", s_mm, mm_y - 1)
            p.op("scalar", lambda E: E.activation(
                SYC[:, 0:4, :],
                PS[4].ap().rearrange("p (m n) -> p m n", n=128), AF.Silu))
            p.wait("scalar", s_mm, mm_y)
            p.op("scalar", lambda E: E.activation(
                SYC[:, 4:8, :],
                PS[5].ap().rearrange("p (m n) -> p m n", n=128), AF.Silu),
                inc=s_ev)
            sy_ok = s_ev.total
            if tap_sy is not None and half == 0:
                p.wait("sync", s_ev, sy_ok)
                p.op("sync", lambda E, cc=c: E.dma_start(
                    out=tap_sy[:, cc, :, :], in_=SYC), inc=s_st, inc_n=16)
            p.wait("tensor", s_ev, sy_ok)
            if c == 0:
                p.wait("tensor", dma_w, wv)
            for n in range(2):
                for m in range(NK):
                    last = (m == NK - 1) and (half == 1)
                    p.op("tensor", lambda E, mm=m, nn=n, st_=last: E.matmul(
                        PS[7][:, 0:N512],
                        SYC[:, mm, :], WOUT_H[:, mm, nn * 512:(nn + 1) * 512],
                        start=(mm == 0), stop=st_),
                        inc=(s_mm if last else None))
                if half == 0:
                    p.op("tensor", lambda E, nn=n: E.matmul(
                        PS[7][:, 0:N512], ONESR[0:1, 0:128],
                        BOUTB[0:1, nn * 512:(nn + 1) * 512],
                        start=False, stop=True), inc=s_mm)
                mm_o = s_mm.total
                p.wait("vector", s_mm, mm_o)
                p.op("vector", lambda E, cc=c, nn=n: E.tensor_tensor(
                    X[:, cc, nn * 512:(nn + 1) * 512],
                    X[:, cc, nn * 512:(nn + 1) * 512],
                    PS[7][:, 0:N512], ALU.add), inc=s_fb)
                p.wait("tensor", s_fb, s_fb.total)
        return

    p.wait("vector", s_gp, GP_L0)
    mamba_half(0)
    mamba_half(1)
    MAMBA_DONE_V = s_fb.total

    tap_store("x_after_mamba", X.ap(), (128, NT, D), F32,
              wait=[(s_fb, MAMBA_DONE_V)])

    RW1B = arn_b(0, NK, D)
    LGT = arn_f(16384, T)[0:E0, :]
    EW2RES = arn_b(0, NM, D)
    WBC = arn_b(32768, T)
    EW1R = arn_b(36864, 2, NK, 128)
    WTR = arn_b(40960, T)[0:E0, :]
    EB2RB = arn_b(45056, D)[0:E0, :]

    R1T = BIGT.ap().rearrange("p (m t) -> p m t", m=NK)
    EHW = BIGT.ap().rearrange("p (m t) -> p m t", m=NM)

    def moe_l0(stop=""):
        p.wait("gpsimd", s_fb, s_fb.total)
        dma("gpsimd", RW1B, RW1.rearrange("(k p) m -> p k m", p=128), dma_w)
        w_rw1 = dma_w.total
        dma("gpsimd", EB2RB, EB2R, dma_c)
        c_eb2 = dma_c.total

        p.wait("tensor", dma_w, w_rw1)
        p.wait("tensor", s_fb, s_fb.total)
        p.wait("tensor", s_ev, s_ev.total)
        p.wait("tensor", s_vv, s_vv.total)
        for m8 in range(NK):
            for n in range(4):
                bank = (m8 * 4 + n) % 2
                for k in range(NK):
                    p.op("tensor", lambda E, kk=k, mm=m8, nn=n, b=bank: E.matmul(
                        PS[b][:, 0:N512], RW1B[:, kk, mm * 128:(mm + 1) * 128],
                        XNT[:, kk, nn * 512:(nn + 1) * 512],
                        start=(kk == 0), stop=(kk == NK - 1)),
                        inc=(s_mm if k == NK - 1 else None))
                mmv = s_mm.total
                p.wait("scalar", s_mm, mmv)
                p.op("scalar", lambda E, mm=m8, nn=n, b=bank: E.activation(
                    R1T[:, mm, nn * 512:(nn + 1) * 512], PS[b][:, 0:N512],
                    AF.Silu, bias=RB1S[:, mm:mm + 1]), inc=s_ev)
                p.wait("tensor", s_ev, s_ev.total)
        if stop == "rw1":
            return
        for n in range(4):
            for k in range(NK):
                p.op("tensor", lambda E, kk=k, nn=n: E.matmul(
                    PS[2][0:E0, 0:N512], RW2B[:, kk, :],
                    R1T[:, kk, nn * 512:(nn + 1) * 512],
                    start=(kk == 0), stop=(kk == NK - 1)),
                    inc=(s_mm if k == NK - 1 else None))
            mmv = s_mm.total
            p.wait("scalar", s_mm, mmv)
            p.op("scalar", lambda E, nn=n: E.activation(
                LGT[0:E0, nn * 512:(nn + 1) * 512], PS[2][0:E0, 0:N512],
                AF.Identity, bias=RB2S.ap()), inc=s_ev)
            p.wait("tensor", s_ev, s_ev.total)
        for t in range(NT):
            bank = t % 2
            p.wait("tensor", s_vv, s_vv.total)
            p.op("tensor", lambda E, tt=t, b=bank: E.transpose(
                PS[b][:, 0:E0], LGT[0:E0, tt * 128:(tt + 1) * 128],
                IDF[0:E0, 0:E0]), inc=s_mm)
            mmv = s_mm.total
            p.wait("vector", s_mm, mmv)
            p.op("vector", lambda E, tt=t, b=bank: E.tensor_copy(
                LGTM[:, tt, :], PS[b][:, 0:E0]), inc=s_vv)
        LG_OK = s_vv.total
        if "lgtm" in taps:
            tap_store("lgtm", LGTM.ap(), (128, NT, E0), F32, wait=[(s_vv, LG_OK)])
        if stop == "logits":
            return
        p.op("vector", lambda E: E.reduce_max(
            TOP_M1.ap(), LGTM.ap(), axis=AX.X))
        p.emit("vector", lambda E: E.drain())
        p.op("vector", lambda E: E.tensor_scalar(
            TOP_M1N.ap(), TOP_M1.ap(), -1.0, 0.0, ALU.mult, ALU.add),
            inc=s_vv)
        for t in range(NT):
            p.op("vector", lambda E, tt=t: E.tensor_scalar(
                TOP_MSK[:, tt, :], LGTM[:, tt, :], TOP_M1[:, tt:tt + 1], None,
                ALU.is_ge))
        for t in range(NT):
            p.op("vector", lambda E, tt=t: E.scalar_tensor_tensor(
                TOP_MSK[:, tt, :], TOP_MSK[:, tt, :], -1e30, LGTM[:, tt, :],
                ALU.mult, ALU.add))
        for t in range(NT):
            p.op("vector", lambda E, tt=t: E.reduce_max(
                TOP_M2[:, tt:tt + 1], TOP_MSK[:, tt, :], axis=AX.X),
                inc=(s_vv if t == NT - 1 else None))
        M2_OK = s_vv.total
        p.wait("scalar", s_vv, M2_OK)
        for t in range(NT):
            p.op("scalar", lambda E, tt=t: E.activation(
                TOP_E1[:, tt, :], LGTM[:, tt, :], AF.Exp,
                bias=TOP_M1N[:, tt:tt + 1]), inc=(s_ev if t == NT - 1 else None))
        E1_OK = s_ev.total
        for t in range(NT):
            p.op("vector", lambda E, tt=t: E.tensor_scalar(
                TOP_MSK[:, tt, :], LGTM[:, tt, :], TOP_M2[:, tt:tt + 1], None,
                ALU.is_ge))
        p.wait("vector", s_ev, E1_OK)
        for t in range(NT):
            p.op("vector", lambda E, tt=t: E.tensor_tensor(
                TOP_E1[:, tt, :], TOP_E1[:, tt, :], TOP_MSK[:, tt, :],
                ALU.mult))
        for t in range(NT):
            p.op("vector", lambda E, tt=t: E.reduce_sum(
                TOP_WS[:, tt:tt + 1], TOP_E1[:, tt, :], axis=AX.X))
        p.emit("vector", lambda E: E.drain())
        p.op("vector", lambda E: E.reciprocal(TOP_M1.ap(), TOP_WS.ap()))
        p.emit("vector", lambda E: E.drain())
        for t in range(NT):
            p.op("vector", lambda E, tt=t: E.tensor_scalar(
                WTM[:, tt, :], TOP_E1[:, tt, :], TOP_M1[:, tt:tt + 1], None,
                ALU.mult), inc=(s_vv if t == NT - 1 else None))
        W_OK = s_vv.total
        p.wait("tensor", s_vv, W_OK)
        for t in range(NT):
            bank = t % 2
            p.wait("tensor", s_vv, s_vv.total)
            p.op("tensor", lambda E, tt=t, b=bank: E.transpose(
                PS[b].ap().bitcast(BF16)[0:E0, 0:128], WTM[:, tt, :],
                IDB.ap()), inc=s_mm)
            mmv = s_mm.total
            p.wait("vector", s_mm, mmv)
            p.op("vector", lambda E, tt=t, b=bank: E.tensor_copy(
                WTR[0:E0, tt * 128:(tt + 1) * 128],
                PS[b].ap().bitcast(BF16)[0:E0, 0:128]), inc=s_vv)
        WT_OK = s_vv.total
        if "wtm" in taps:
            tap_store("wtm", WTM.ap(), (128, NT, E0), BF16, wait=[(s_vv, W_OK)])
        if stop == "top2":
            return
        ew2_free_mm = s_mm.total
        n_exp = 1 if stop == "e1" else E0
        for e in range(n_exp):
            p.wait("gpsimd", s_mm, ew2_free_mm)
            if e == 0:
                p.wait("gpsimd", s_ev, s_ev.total)
            dma("gpsimd", EW2RES,
                EW2[e, :, :].rearrange("(k p) m -> p k m", p=128), dma_w)
            w_ew2 = dma_w.total
            if stop == "ew2dma":
                p.wait("sync", dma_w, w_ew2)
                return
            p.wait("vector", s_vv, WT_OK)
            p.wait("vector", s_fb, s_fb.total)
            p.wait("scalar", s_vv, s_vv.total)
            p.op("scalar", lambda E, ee=e: E.dma_start(
                out=WROW.ap(), in_=WTR[ee:ee + 1, :]), inc=s_ev, inc_n=16)
            wr_ok = s_ev.total
            if stop == "wrow":
                p.wait("sync", s_ev, wr_ok)
                return
            p.wait("tensor", s_ev, wr_ok)
            for n in range(4):
                b = 2 + (n % 2)
                if n >= 2:
                    p.wait("tensor", s_vv, s_vv.total - 1)
                p.op("tensor", lambda E, nn=n, bb=b: E.matmul(
                    PS[bb][:, 0:N512], ONESR[0:1, 0:128],
                    WROW[0:1, nn * 512:(nn + 1) * 512],
                    start=True, stop=True), inc=s_mm)
                mmv = s_mm.total
                p.wait("vector", s_mm, mmv)
                p.op("vector", lambda E, nn=n, bb=b: E.tensor_copy(
                    WBC[:, nn * 512:(nn + 1) * 512], PS[bb][:, 0:N512]),
                    inc=s_vv)
            wbc_ok = s_vv.total
            if stop == "wbc":
                return
            for ht in range(2):
                for m in range(NM):
                    ring = m % 2
                    p.wait("gpsimd", s_mm, s_mm.total)
                    dma("gpsimd", EW1R[:, ring, :, :],
                        EW1[e, :, m * 128:(m + 1) * 128].rearrange(
                            "(k p) h -> p k h", p=128), dma_w)
                    w_blk = dma_w.total
                    p.wait("tensor", dma_w, w_blk)
                    p.wait("tensor", s_vv, wbc_ok)
                    for n in range(2):
                        bank = (2 * m + n) % 2
                        if s_ex.total >= 1:
                            p.wait("tensor", s_ex, s_ex.total - 1)
                        for k in range(NK):
                            p.op("tensor", lambda E, kk=k, rr=ring, nn=n, b=bank,
                                 h0=ht: E.matmul(
                                     PS[b][:, 0:N512], EW1R[:, rr, kk, :],
                                     XNT[:, kk,
                                         h0 * 1024 + nn * 512:
                                         h0 * 1024 + (nn + 1) * 512],
                                     start=(kk == 0), stop=(kk == NK - 1)),
                                 inc=(s_mm if k == NK - 1 else None))
                        mmv = s_mm.total
                        p.wait("scalar", s_mm, mmv)
                        p.op("scalar", lambda E, mm=m, nn=n, b=bank, ee=e:
                             E.activation(
                                 EHW[:, mm, nn * 512:(nn + 1) * 512],
                                 PS[b][:, 0:N512], AF.Silu,
                                 bias=EB1S[:, ee, mm:mm + 1]), inc=s_ex)
                        p.wait("vector", s_ex, s_ex.total)
                        p.op("vector", lambda E, mm=m, nn=n, h0=ht: E.tensor_tensor(
                            EHW[:, mm, nn * 512:(nn + 1) * 512],
                            EHW[:, mm, nn * 512:(nn + 1) * 512],
                            WBC[:, h0 * 1024 + nn * 512:h0 * 1024 + (nn + 1) * 512],
                            ALU.mult), inc=s_vv)
                if stop == "ew1h0":
                    return
                p.wait("tensor", dma_w, w_ew2)
                p.wait("tensor", dma_c, c_eb2)
                p.wait("tensor", s_vv, s_vv.total)
                for tl in range(8):
                    t = ht * 8 + tl
                    b0 = 2 + 2 * (tl % 2)
                    for n in range(2):
                        for k in range(NM):
                            p.op("tensor", lambda E, kk=k, nn=n, tt=tl, b=b0:
                                 E.matmul(
                                     PS[b + nn][:, 0:N512],
                                     EHW[:, kk, tt * 128:(tt + 1) * 128],
                                     EW2RES[:, kk, nn * 512:(nn + 1) * 512],
                                     start=(kk == 0), stop=False))
                        if e == 0:
                            p.op("tensor", lambda E, nn=n, tt=t, b=b0: E.matmul(
                                PS[b + nn][:, 0:N512],
                                WTR[0:E0, tt * 128:(tt + 1) * 128],
                                EB2RB[0:E0, nn * 512:(nn + 1) * 512],
                                start=False, stop=True),
                                inc=(s_mm if n == 1 else None))
                        else:
                            p.items["tensor"].pop()
                            p.op("tensor", lambda E, nn=n, tt=tl, b=b0: E.matmul(
                                PS[b + nn][:, 0:N512],
                                EHW[:, NM - 1, tt * 128:(tt + 1) * 128],
                                EW2RES[:, NM - 1, nn * 512:(nn + 1) * 512],
                                start=False, stop=True),
                                inc=(s_mm if n == 1 else None))
                    mmv = s_mm.total
                    p.wait("vector", s_mm, mmv)
                    for n in range(2):
                        p.op("vector", lambda E, tt=t, nn=n, b=b0: E.tensor_tensor(
                            X[:, tt, nn * 512:(nn + 1) * 512],
                            X[:, tt, nn * 512:(nn + 1) * 512],
                            PS[b + nn][:, 0:N512], ALU.add),
                            inc=(s_fb if n == 1 else None))
                    if tl % 2 == 1:
                        p.wait("tensor", s_fb, s_fb.total)
            ew2_free_mm = s_mm.total
        return

    if stop != "mamba":
        moe_l0(stop)
    MOE_DONE = s_fb.total
    tap_store("x_after_l0", X.ap(), (128, NT, D), F32,
              wait=[(s_fb, MOE_DONE)])

    WDQB = arn_b(0, NK, DC)
    WDKVB = arn_b(512, NK, DC)
    WKRB = arn_b(1024, NK, DHR)
    WUQB = arn_b(1536, NHEAD * DK)[0:DC + 1, :]
    WUKB = arn_b(3584, NHEAD * DK)[0:DC + 1, :]
    WUVB = arn_b(5632, NHEAD * DK)[0:DC + 1, :]
    WQRB = arn_b(7680, NHEAD * DHR)[0:DC + 1, :]
    CQT = arn_b(8192, T)[0:DC + 1, :]
    CKVT = arn_b(12288, T)[0:DC + 1, :]
    KRX = arn_b(16384, T)[0:DHR + 1, :]
    QRX = arn_b(20480, 1024)[0:DHR + 1, :]
    QCB = arn_b(22528, 1024)
    KCB = arn_b(24576, T)
    VCT = arn_b(28672, NT, 128)
    PT = arn_b(32768, 2, 1024)
    QSQ = arn_b(32768, T)
    QSQR = arn_b(36864, T)[0:DHR, :]
    KN2 = arn_b(32768, T)[0:1, :]
    SUMS = arn_f(38912, 1024)[0:1, :]
    RCPSB = arn_b(43008, 1024)[0:1, :]
    RCPB = arn_f(45056, 1024)
    WOB = arn_b(8192, NK, D)
    BORB2 = arn_b(22528, D)[0:1, :]
    KEYR = arn_b(0, 2, NK, H1)
    ESELN = arn_b(4096, D)[0:E1, :]
    SELTB = arn_b(4096, T)[0:E1, :]
    VALB = arn_b(12288, E1, D)
    SELBH = arn_b(8192, 1024)
    ESELTB = arn_b(45056, NK, E1)

    def l1_layer():
        p.wait("vector", s_fb, s_fb.total)
        p.wait("scalar", s_fb, s_fb.total)
        p.wait("tensor", s_fb, s_fb.total)
        xnt_l1, _ = layernorm(add_xn_to_x=False)
        if stop == "ln1":
            return

        p.wait("gpsimd", s_mm, s_mm.total)
        dma("gpsimd", WDQB, WDQ.rearrange("(k p) c -> p k c", p=128), dma_w)
        dma("gpsimd", WDKVB, WDKV.rearrange("(k p) c -> p k c", p=128), dma_w)
        dma("gpsimd", WKRB, WKR.rearrange("(k p) c -> p k c", p=128), dma_w)
        dma("gpsimd", WUQB, WUQ, dma_w)
        dma("gpsimd", WUKB, WUK, dma_w)
        dma("gpsimd", WUVB, WUV, dma_w)
        dma("gpsimd", WQRB, WQR, dma_w)
        w_attn = dma_w.total
        if stop == "attw":
            p.wait("sync", dma_w, w_attn)
            return

        p.wait("tensor", dma_w, w_attn)
        p.wait("tensor", s_tr, xnt_l1)
        p.op("vector", lambda E: E.memset(CQT[DC:DC + 1, :], 1.0))
        p.op("vector", lambda E: E.memset(CKVT[DC:DC + 1, :], 1.0))
        p.op("vector", lambda E: E.memset(KRX[DHR:DHR + 1, :], 1.0), inc=s_vv)
        ones_ok = s_vv.total

        def proj32(dst, lhs_w, bias_ap, nm):
            for n in range(4):
                bank = n % 2
                if s_ex.total >= 1:
                    p.wait("tensor", s_ex, s_ex.total - 1)
                for k in range(NK):
                    p.op("tensor", lambda E, kk=k, nn=n, b=bank, w=lhs_w: E.matmul(
                        PS[b][0:nm, 0:N512], w[:, kk, :],
                        XNT[:, kk, nn * 512:(nn + 1) * 512],
                        start=(kk == 0), stop=(kk == NK - 1)),
                        inc=(s_mm if k == NK - 1 else None))
                mmv = s_mm.total
                p.wait("scalar", s_mm, mmv)
                p.op("scalar", lambda E, nn=n, b=bank, d=dst, ba=bias_ap, m=nm:
                     E.activation(d[0:m, nn * 512:(nn + 1) * 512],
                                  PS[b][0:m, 0:N512], AF.Identity, bias=ba),
                     inc=s_ex)

        proj32(CQT, WDQB, BDQS.ap(), DC)
        proj32(CKVT, WDKVB, BDKVS.ap(), DC)
        proj32(KRX, WKRB, BKRS.ap(), DHR)
        p.wait("tensor", s_ex, s_ex.total)
        p.wait("tensor", s_vv, ones_ok)

        n_heads = 1 if stop == "h1" else NHEAD
        for h in range(n_heads):
            for n in range(4):
                bank = n % 2
                if s_ex.total >= 1:
                    p.wait("tensor", s_ex, s_ex.total - 1)
                p.op("tensor", lambda E, nn=n, b=bank, hh=h: E.matmul(
                    PS[b][:, 0:N512], WUKB[0:DC + 1, hh * 128:(hh + 1) * 128],
                    CKVT[0:DC + 1, nn * 512:(nn + 1) * 512],
                    start=True, stop=True), inc=s_mm)
                mmv = s_mm.total
                p.wait("scalar", s_mm, mmv)
                p.op("scalar", lambda E, nn=n, b=bank: E.copy(
                    KCB[:, nn * 512:(nn + 1) * 512], PS[b][:, 0:N512]),
                    inc=s_ex)
            p.wait("tensor", s_vv, s_vv.total)
            for tg in range(4):
                bank = 6 + (tg % 2)
                if tg >= 2:
                    p.wait("tensor", s_ex, s_ex.total - 1)
                for ti in range(4):
                    tp = tg * 4 + ti
                    p.op("tensor", lambda E, t_=tp, b=bank, o=ti * 128, hh=h:
                         E.matmul(PS[b][:, o:o + 128],
                                  CKVT[0:DC + 1, t_ * 128:(t_ + 1) * 128],
                                  WUVB[0:DC + 1, hh * 128:(hh + 1) * 128],
                                  start=True, stop=True),
                         inc=(s_mm if ti == 3 else None))
                mmv = s_mm.total
                p.wait("scalar", s_mm, mmv)
                p.op("scalar", lambda E, tg_=tg, b=bank: E.copy(
                    VCT[:, tg_ * 4:(tg_ + 1) * 4, :],
                    PS[b].ap().rearrange("p (a q) -> p a q", q=128)),
                    inc=s_ex)
            p.wait("vector", s_ex, s_ex.total)
            p.op("vector", lambda E: E.tensor_tensor(
                QSQ, KCB, KCB, ALU.mult))
            p.op("vector", lambda E: E.tensor_tensor(
                QSQR[0:DHR, :], KRX[0:DHR, :], KRX[0:DHR, :], ALU.mult),
                inc=s_vv)
            p.wait("tensor", s_vv, s_vv.total)
            p.wait("tensor", s_ex, s_ex.total)
            for n in range(4):
                kb = 4 + (n % 2)
                if n >= 2:
                    p.wait("tensor", s_vv, s_vv.total - 1)
                p.op("tensor", lambda E, nn=n, b=kb: E.matmul(
                    PS[b][0:1, 0:N512], ONESC.ap(),
                    QSQ[:, nn * 512:(nn + 1) * 512], start=True, stop=False))
                p.op("tensor", lambda E, nn=n, b=kb: E.matmul(
                    PS[b][0:1, 0:N512], ONESC[0:DHR, 0:1],
                    QSQR[0:DHR, nn * 512:(nn + 1) * 512],
                    start=False, stop=True), inc=s_mm)
                mmv = s_mm.total
                p.wait("vector", s_mm, mmv)
                p.op("vector", lambda E, nn=n, b=kb: E.tensor_copy(
                    KN2[0:1, nn * 512:(nn + 1) * 512], PS[b][0:1, 0:N512]),
                    inc=s_vv)
            p.emit("vector", lambda E: E.drain())
            p.op("vector", lambda E: E.reduce_max(
                KMAX[0:1, 0:1], KN2[0:1, :], axis=AX.X))
            p.emit("vector", lambda E: E.drain())
            p.op("vector", lambda E: E.tensor_scalar(
                KMAX[0:1, 1:2], KMAX[0:1, 0:1], 0.5, 0.0, ALU.mult, ALU.add),
                inc=s_vv)
            p.emit("vector", lambda E: E.drain())
            kmax_ok = s_vv.total

            for half in range(2):
                toff = half * 1024
                for n in range(2):
                    bank = n % 2
                    p.wait("tensor", s_ex, s_ex.total - 1)
                    p.op("tensor", lambda E, nn=n, b=bank, hh=h, to=toff: E.matmul(
                        PS[b][:, 0:N512],
                        WUQB[0:DC + 1, hh * 128:(hh + 1) * 128],
                        CQT[0:DC + 1, to + nn * 512:to + (nn + 1) * 512],
                        start=True, stop=True), inc=s_mm)
                    mmv = s_mm.total
                    p.wait("scalar", s_mm, mmv)
                    p.op("scalar", lambda E, nn=n, b=bank: E.copy(
                        QCB[:, nn * 512:(nn + 1) * 512], PS[b][:, 0:N512]),
                        inc=s_ex)
                for n in range(2):
                    bank = n % 2
                    p.wait("tensor", s_ex, s_ex.total - 1)
                    p.op("tensor", lambda E, nn=n, b=bank, hh=h, to=toff: E.matmul(
                        PS[b][0:DHR, 0:N512],
                        WQRB[0:DC + 1, hh * 32:(hh + 1) * 32],
                        CQT[0:DC + 1, to + nn * 512:to + (nn + 1) * 512],
                        start=True, stop=True), inc=s_mm)
                    mmv = s_mm.total
                    p.wait("scalar", s_mm, mmv)
                    p.op("scalar", lambda E, nn=n, b=bank: E.copy(
                        QRX[0:DHR, nn * 512:(nn + 1) * 512],
                        PS[b][0:DHR, 0:N512]), inc=s_ex)
                p.wait("vector", s_ex, s_ex.total)
                p.op("vector", lambda E: E.tensor_tensor(
                    QSQ[:, 0:1024], QCB, QCB, ALU.mult))
                p.op("vector", lambda E: E.tensor_tensor(
                    QSQR[0:DHR, 0:1024], QRX[0:DHR, :], QRX[0:DHR, :],
                    ALU.mult), inc=s_vv)
                p.wait("tensor", s_vv, s_vv.total)
                for n in range(2):
                    qb = 4 + (n % 2)
                    p.op("tensor", lambda E, nn=n, b=qb: E.matmul(
                        PS[b][0:1, 0:N512], ONESC.ap(),
                        QSQ[:, nn * 512:(nn + 1) * 512], start=True, stop=False))
                    p.op("tensor", lambda E, nn=n, b=qb: E.matmul(
                        PS[b][0:1, 0:N512], ONESC[0:DHR, 0:1],
                        QSQR[0:DHR, nn * 512:(nn + 1) * 512],
                        start=False, stop=True), inc=s_mm)
                    mmv = s_mm.total
                    p.wait("scalar", s_mm, mmv)
                    p.op("scalar", lambda E, nn=n, b=qb: E.copy(
                        RCPSB[0:1, nn * 512:(nn + 1) * 512],
                        PS[b][0:1, 0:N512]), inc=s_ex)
                p.wait("vector", s_ex, s_ex.total)
                p.wait("vector", s_vv, kmax_ok)
                p.op("vector", lambda E: E.tensor_scalar(
                    QRX[DHR:DHR + 1, :], RCPSB[0:1, :], -0.5,
                    KMAX[0:1, 1:2], ALU.mult, ALU.subtract), inc=s_vv)
                qrx_ok = s_vv.total

                p.wait("tensor", s_vv, qrx_ok)
                for tp in range(NT):
                    for n in range(2):
                        g = tp * 2 + n
                        bank = g % 2
                        if s_ex.total >= 1:
                            p.wait("tensor", s_ex, s_ex.total - 1)
                        p.op("tensor", lambda E, t_=tp, nn=n, b=bank: E.matmul(
                            PS[b][:, 0:N512], KCB[:, t_ * 128:(t_ + 1) * 128],
                            QCB[:, nn * 512:(nn + 1) * 512],
                            start=True, stop=False))
                        p.op("tensor", lambda E, t_=tp, nn=n, b=bank: E.matmul(
                            PS[b][:, 0:N512],
                            KRX[0:DHR + 1, t_ * 128:(t_ + 1) * 128],
                            QRX[0:DHR + 1, nn * 512:(nn + 1) * 512],
                            start=False, stop=True), inc=s_mm)
                        mmv = s_mm.total
                        p.wait("scalar", s_mm, mmv)
                        p.op("scalar", lambda E, t_=tp, nn=n, b=bank: E.activation(
                            PT[:, t_ % 2, nn * 512:(nn + 1) * 512],
                            PS[b][:, 0:N512], AF.Exp, scale=SCALE), inc=s_ex)
                    p.wait("tensor", s_ex, s_ex.total)
                    for n in range(2):
                        p.op("tensor", lambda E, t_=tp, nn=n: E.matmul(
                            PS[2 + nn][:, 0:N512], VCT[:, t_, :],
                            PT[:, t_ % 2, nn * 512:(nn + 1) * 512],
                            start=(t_ == 0), stop=(t_ == NT - 1)))
                        p.op("tensor", lambda E, t_=tp, nn=n: E.matmul(
                            PS[4 + nn][0:1, 0:N512], ONESC.ap(),
                            PT[:, t_ % 2, nn * 512:(nn + 1) * 512],
                            start=(t_ == 0), stop=(t_ == NT - 1)),
                            inc=(s_mm if (tp == NT - 1 and n == 1) else None))
                mmv = s_mm.total
                p.wait("vector", s_mm, mmv)
                p.op("vector", lambda E: E.tensor_copy(SUMS[0:1, 0:512],
                                                       PS[4][0:1, 0:N512]))
                p.op("vector", lambda E: E.tensor_copy(SUMS[0:1, 512:1024],
                                                       PS[5][0:1, 0:N512]))
                p.emit("vector", lambda E: E.drain())
                def _recip(E):
                    with nc.allow_low_precision(
                            reason="softmax 1/sum broadcast operand in bf16; "
                                   "P and V operands are bf16 anyway"):
                        return E.reciprocal(RCPSB[0:1, :], SUMS[0:1, :])
                p.op("vector", _recip, inc=s_vv)
                rs_ok = s_vv.total
                p.wait("tensor", s_vv, rs_ok)
                for n in range(2):
                    p.op("tensor", lambda E, nn=n: E.matmul(
                        PS[6 + nn][:, 0:N512], ONESR[0:1, 0:128],
                        RCPSB[0:1, nn * 512:(nn + 1) * 512],
                        start=True, stop=True),
                        inc=(s_mm if n == 1 else None))
                mmv = s_mm.total
                p.wait("vector", s_mm, mmv)
                for n in range(2):
                    p.op("vector", lambda E, nn=n: E.tensor_copy(
                        RCPB[:, nn * 512:(nn + 1) * 512],
                        PS[6 + nn][:, 0:N512]), inc=s_vv)
                p.emit("vector", lambda E: E.drain())
                for n in range(2):
                    p.op("vector", lambda E, nn=n, hh=h, to=toff: E.tensor_tensor(
                        OT[:, hh, to + nn * 512:to + (nn + 1) * 512],
                        PS[2 + nn][:, 0:N512],
                        RCPB[:, nn * 512:(nn + 1) * 512], ALU.mult),
                        inc=s_fb)
                p.wait("tensor", s_fb, s_fb.total)
        if stop in ("h1", "attn"):
            return
        p.wait("gpsimd", s_mm, s_mm.total)
        dma("gpsimd", WOB, WO.rearrange("(k p) d -> p k d", p=128), dma_w)
        dma("gpsimd", BORB2, BOR, dma_w)
        w_wo = dma_w.total
        p.wait("tensor", dma_w, w_wo)
        for t in range(NT):
            b0 = 2 + 2 * (t % 2)
            for n in range(2):
                for k in range(NK):
                    p.op("tensor", lambda E, kk=k, nn=n, tt=t, b=b0: E.matmul(
                        PS[b + nn][:, 0:N512], OT[:, kk, tt * 128:(tt + 1) * 128],
                        WOB[:, kk, nn * 512:(nn + 1) * 512],
                        start=(kk == 0), stop=False))
                p.op("tensor", lambda E, nn=n, b=b0: E.matmul(
                    PS[b + nn][:, 0:N512], ONESR[0:1, 0:128],
                    BORB2[0:1, nn * 512:(nn + 1) * 512],
                    start=False, stop=True),
                    inc=(s_mm if n == 1 else None))
            mmv = s_mm.total
            p.wait("vector", s_mm, mmv)
            for n in range(2):
                p.op("vector", lambda E, tt=t, nn=n, b=b0: E.tensor_tensor(
                    X[:, tt, nn * 512:(nn + 1) * 512],
                    X[:, tt, nn * 512:(nn + 1) * 512],
                    PS[b + nn][:, 0:N512], ALU.add),
                    inc=(s_fb if n == 1 else None))
            if t % 2 == 1:
                p.wait("tensor", s_fb, s_fb.total)
        WO_DONE = s_fb.total
        if stop == "wo":
            return
        p.wait("gpsimd", s_mm, s_mm.total)
        dma("gpsimd", ESELN, ESEL, dma_w)
        dma("gpsimd", VALB, VALUES.rearrange("e p v -> p e v"), dma_w)
        w_ffn = dma_w.total
        if stop == "ffnw":
            p.wait("sync", dma_w, w_ffn)
            return
        p.wait("tensor", dma_w, w_ffn - 16)
        p.wait("tensor", s_fb, WO_DONE)
        for k in range(NK):
            bank = k % 2
            p.wait("tensor", s_vv, s_vv.total)
            p.op("tensor", lambda E, kk=k, b=bank: E.transpose(
                PS[b].ap().bitcast(BF16)[:, 0:E1],
                ESELN[0:E1, kk * 128:(kk + 1) * 128], IDB[0:E1, 0:E1]),
                inc=s_mm)
            mmv = s_mm.total
            p.wait("vector", s_mm, mmv)
            p.op("vector", lambda E, kk=k, b=bank: E.tensor_copy(
                ESELTB[:, kk, :], PS[b].ap().bitcast(BF16)[:, 0:E1]),
                inc=s_vv)
        es_ok = s_vv.total
        if stop == "eselt":
            return
        p.wait("tensor", s_vv, es_ok)
        for n in range(4):
            bank = n % 2
            if s_ex.total >= 1:
                p.wait("tensor", s_ex, s_ex.total - 1)
            for k in range(NK):
                p.op("tensor", lambda E, kk=k, nn=n, b=bank: E.matmul(
                    PS[b][0:E1, 0:N512], ESELTB[:, kk, :],
                    XNT[:, kk, nn * 512:(nn + 1) * 512],
                    start=(kk == 0), stop=(kk == NK - 1)),
                    inc=(s_mm if k == NK - 1 else None))
            mmv = s_mm.total
            p.wait("scalar", s_mm, mmv)
            p.op("scalar", lambda E, nn=n, b=bank: E.activation(
                SELTB[0:E1, nn * 512:(nn + 1) * 512], PS[b][0:E1, 0:N512],
                AF.Sigmoid), inc=s_ex)
        SW = BIGT.ap().rearrange("p (e t) -> p e t", e=E1)
        if stop == "sel":
            return
        for ht in range(2):
            toff = ht * 1024
            for e in range(E1):
                ring = e % 2
                p.wait("gpsimd", s_mm, s_mm.total)
                dma("gpsimd", KEYR[:, ring, :, :],
                    KEYS[e, :, :].rearrange("(k p) hh -> p k hh", p=128), dma_w)
                w_blk = dma_w.total
                p.wait("scalar", s_ex, s_ex.total)
                p.op("scalar", lambda E, ee=e: E.dma_start(
                    out=WROW[0:1, 0:T], in_=SELTB[ee:ee + 1, :]),
                    inc=s_ev, inc_n=16)
                sr_ok = s_ev.total
                p.wait("tensor", s_ev, sr_ok)
                p.wait("tensor", s_vv, s_vv.total)
                for n in range(2):
                    p.op("tensor", lambda E, nn=n, to=toff: E.matmul(
                        PS[6 + nn][:, 0:N512], ONESR[0:1, 0:128],
                        WROW[0:1, to + nn * 512:to + (nn + 1) * 512],
                        start=True, stop=True),
                        inc=(s_mm if n == 1 else None))
                mmv = s_mm.total
                p.wait("vector", s_mm, mmv)
                for n in range(2):
                    p.op("vector", lambda E, nn=n: E.tensor_copy(
                        SELBH[:, nn * 512:(nn + 1) * 512],
                        PS[6 + nn][:, 0:N512]), inc=(s_vv if n == 1 else None))
                selb_ok = s_vv.total
                p.wait("tensor", dma_w, w_blk)
                for n in range(2):
                    g = e * 2 + n
                    bank = g % 2
                    if s_ex.total >= 1:
                        p.wait("tensor", s_ex, s_ex.total - 1)
                    for k in range(NK):
                        p.op("tensor", lambda E, kk=k, rr=ring, nn=n, b=bank,
                             to=toff: E.matmul(
                                 PS[b][:, 0:N512], KEYR[:, rr, kk, :],
                                 XNT[:, kk, to + nn * 512:to + (nn + 1) * 512],
                                 start=(kk == 0), stop=(kk == NK - 1)),
                             inc=(s_mm if k == NK - 1 else None))
                    mmv = s_mm.total
                    p.wait("scalar", s_mm, mmv)
                    p.op("scalar", lambda E, ee=e, nn=n, b=bank: E.activation(
                        SW[:, ee, nn * 512:(nn + 1) * 512], PS[b][:, 0:N512],
                        AF.Relu), inc=s_ex)
                    p.wait("vector", s_ex, s_ex.total)
                    p.wait("vector", s_vv, selb_ok)
                    p.op("vector", lambda E, ee=e, nn=n: E.tensor_tensor(
                        SW[:, ee, nn * 512:(nn + 1) * 512],
                        SW[:, ee, nn * 512:(nn + 1) * 512],
                        SELBH[:, nn * 512:(nn + 1) * 512], ALU.mult),
                        inc=s_vv)
            if stop == "keysh0":
                return
            p.wait("tensor", s_vv, s_vv.total)
            p.wait("tensor", dma_w, w_ffn)
            for tl in range(8):
                t = ht * 8 + tl
                b0 = 2 + 2 * (tl % 2)
                for n in range(2):
                    for e in range(E1):
                        p.op("tensor", lambda E, ee=e, nn=n, tt=tl, b=b0:
                             E.matmul(
                                 PS[b + nn][:, 0:N512],
                                 SW[:, ee, tt * 128:(tt + 1) * 128],
                                 VALB[:, ee, nn * 512:(nn + 1) * 512],
                                 start=(ee == 0), stop=(ee == E1 - 1)),
                             inc=(s_mm if (e == E1 - 1 and n == 1) else None))
                mmv = s_mm.total
                p.wait("vector", s_mm, mmv)
                for n in range(2):
                    p.op("vector", lambda E, tt=t, nn=n, b=b0: E.tensor_tensor(
                        X[:, tt, nn * 512:(nn + 1) * 512],
                        X[:, tt, nn * 512:(nn + 1) * 512],
                        PS[b + nn][:, 0:N512], ALU.add),
                        inc=(s_fb if n == 1 else None))
                fbv = s_fb.total
                if tl % 2 == 1:
                    p.wait("tensor", s_fb, fbv)
                if stop == "valh0" and tl == 7:
                    return
                p.wait("sync", s_fb, fbv)
                p.op("sync", lambda E, tt=t: E.dma_start(
                    out=OUT.rearrange("(n p) d -> p n d", p=128)[:, tt, :],
                    in_=X[:, tt, :]), inc=s_st, inc_n=16)
        return

    OT = BIGT.ap().rearrange("p (hh t) -> p hh t", hh=NHEAD)
    L0_STOPS = ("mamba", "rw1", "logits", "top2", "wbc", "ew1h0", "e1",
                "ew2dma", "wrow")
    if stop not in L0_STOPS:
        l1_layer()
    L1_DONE = s_st.total

    if stop != "":
        p.wait("sync", s_fb, s_fb.total)
        p.wait("sync", s_gp, s_gp.total)
        p.wait("sync", s_vv, s_vv.total)
        p.wait("sync", s_ex, s_ex.total)
        p.op("sync", lambda E: E.dma_start(
            out=OUT.rearrange("(n p) d -> p n d", p=128), in_=X.ap()),
            inc=s_st, inc_n=16)
    p.wait("sync", s_st, s_st.total)

    import contextlib
    with contextlib.ExitStack() as stack:
        for s in p.sems.values():
            s.h = stack.enter_context(nc.semaphore(s.name))
        block = stack.enter_context(nc.Block())
        for name in Plan.ENGINES:
            getattr(block, name)(
                (lambda nm: lambda E: [fn(E) for fn in p.items[nm]])(name))
    return nc, tap_outs


def _prep_inputs(inputs, core):
    g = inputs
    d = {}
    d["x"] = np.ascontiguousarray(g["L0_attn_x"][core])
    d["idm"] = np.eye(128, dtype=np.float32)
    d["jm"] = np.eye(128, dtype=np.float32)[::-1].copy()
    d["win"] = g["L0_attn_W_in"]
    d["binT"] = np.ascontiguousarray(g["L0_attn_b_in"].reshape(1, DI))
    d["convw"] = np.ascontiguousarray(
        g["L0_attn_conv_w"][:, 0, :].reshape(NM, 128, 4).transpose(1, 0, 2))
    d["convb"] = np.ascontiguousarray(g["L0_attn_conv_b"].reshape(NM, 128).T)
    d["amat"] = g["L0_attn_A"]
    d["bp"] = g["L0_attn_Bp"].reshape(DS, 1)
    d["cp"] = g["L0_attn_Cp"].reshape(DS, 1)
    d["dp"] = g["L0_attn_Dp"].reshape(1, 1)
    d["wout"] = g["L0_attn_W_out"]
    d["boutr"] = g["L0_attn_b_out"].reshape(1, D)
    d["rw1"] = g["L0_ffn_rw1"]
    d["rb1c"] = np.ascontiguousarray(g["L0_ffn_rb1"].reshape(NK, 128).T)
    d["rw2"] = g["L0_ffn_rw2"]
    d["rb2c"] = g["L0_ffn_rb2"].reshape(E0, 1)
    d["ew1"] = g["L0_ffn_ew1"]
    d["eb1c"] = np.ascontiguousarray(
        g["L0_ffn_eb1"].reshape(E0, NM, 128).transpose(2, 0, 1))
    d["ew2"] = g["L0_ffn_ew2"]
    d["eb2r"] = g["L0_ffn_eb2"]
    d["wdq"] = g["L1_attn_W_DQ"]
    d["bdqc"] = g["L1_attn_b_DQ"].reshape(DC, 1)
    d["wuq"] = np.vstack([g["L1_attn_W_UQ"], g["L1_attn_b_UQ"][None]])
    d["wdkv"] = g["L1_attn_W_DKV"]
    d["bdkvc"] = g["L1_attn_b_DKV"].reshape(DC, 1)
    d["wuk"] = np.vstack([g["L1_attn_W_UK"], g["L1_attn_b_UK"][None]])
    d["wuv"] = np.vstack([g["L1_attn_W_UV"], g["L1_attn_b_UV"][None]])
    d["wqr"] = np.vstack([g["L1_attn_W_QR"], g["L1_attn_b_QR"][None]])
    d["wkr"] = g["L1_attn_W_KR"]
    d["bkrc"] = g["L1_attn_b_KR"].reshape(DHR, 1)
    d["wo"] = g["L1_attn_W_O"]
    d["bor"] = g["L1_attn_b_O"].reshape(1, D)
    d["esel"] = g["L1_ffn_expert_sel"]
    d["keys"] = g["L1_ffn_keys"]
    d["values"] = g["L1_ffn_values"]
    return {k: np.ascontiguousarray(np.asarray(v).astype(np.float32, copy=False))
            for k, v in d.items()}


def kernel(**inputs):
    nc, _ = build_nc()
    in_maps = [_prep_inputs(inputs, c) for c in range(8)]
    res = run_bass_kernel_spmd(nc, in_maps, core_ids=list(range(8)))
    return np.stack([r["out"] for r in res.results], axis=0)


# version 04
# speedup vs baseline: 1.3022x; 1.3022x over previous
import os
import numpy as np
import concourse.bass as bass
import concourse.mybir as mybir
from concourse.bass_utils import run_bass_kernel_spmd

F32 = mybir.dt.float32
BF16 = mybir.dt.bfloat16
AF = mybir.ActivationFunctionType
ALU = mybir.AluOpType
AX = mybir.AxisListType

T = 2048
NT = 16
D = 1024
NK = 8
DI = 2048
NM = 16
DS = 16
CH = 128
NCH = 16
E0, H0 = 8, 2048
E1, H1 = 16, 128
NHEAD, DK, DC, DHR = 8, 128, 32, 32
SCALE = 1.0 / float(np.sqrt(DK + DHR))
LN_EPS = 1e-5
N512 = 512

ARENA_BYTES = 49152


class Sem:
    def __init__(self, name):
        self.name = name
        self.total = 0
        self.h = None

    def inc(self, n=1):
        self.total += n
        return self.total


class Plan:
    ENGINES = ("tensor", "vector", "scalar", "gpsimd", "sync")

    def __init__(self):
        self.items = {e: [] for e in self.ENGINES}
        self.sems = {}

    def sem(self, name):
        s = self.sems.get(name)
        if s is None:
            s = self.sems[name] = Sem(name)
        return s

    def emit(self, eng, fn):
        self.items[eng].append(fn)

    def wait(self, eng, sem, val=None):
        v = sem.total if val is None else val
        if v > 0:
            self.emit(eng, lambda E, s=sem, vv=v: E.wait_ge(s.h, vv))

    def op(self, eng, fn, inc=None, inc_n=1):
        if inc is not None:
            inc.inc(inc_n)
            self.emit(eng, lambda E, f=fn, s=inc, n=inc_n: f(E).then_inc(s.h, n))
        else:
            self.emit(eng, fn)


def build_nc(debug_taps=(), stop=None):
    stop = stop if stop is not None else os.environ.get("MK_STOP", "")
    nc = bass.Bass("TRN2", target_bir_lowering=False, debug=False)
    p = Plan()
    taps = set(debug_taps)

    def din(name, shape):
        return nc.dram_tensor(name, list(shape), F32, kind="ExternalInput").ap()

    X_hbm = din("x", (T, D))
    ID_hbm = din("idm", (128, 128))
    J_hbm = din("jm", (128, 128))
    WIN = din("win", (D, DI))
    BINT = din("binT", (1, DI))
    CONVW = din("convw", (128, NM, 4))
    CONVB = din("convb", (128, NM))
    A_in = din("amat", (DS, DS))
    BP = din("bp", (DS, 1))
    CP = din("cp", (DS, 1))
    DP = din("dp", (1, 1))
    WOUT = din("wout", (DI, D))
    BOUT = din("boutr", (1, D))
    RW1 = din("rw1", (D, D))
    RB1C = din("rb1c", (128, NK))
    RW2 = din("rw2", (D, E0))
    RB2C = din("rb2c", (E0, 1))
    EW1 = din("ew1", (E0, D, H0))
    EB1C = din("eb1c", (128, E0, NM))
    EW2 = din("ew2", (E0, H0, D))
    EB2R = din("eb2r", (E0, D))
    WDQ = din("wdq", (D, DC))
    BDQC = din("bdqc", (DC, 1))
    WUQ = din("wuq", (DC + 1, NHEAD * DK))
    WDKV = din("wdkv", (D, DC))
    BDKVC = din("bdkvc", (DC, 1))
    WUK = din("wuk", (DC + 1, NHEAD * DK))
    WUV = din("wuv", (DC + 1, NHEAD * DK))
    WQR = din("wqr", (DC + 1, NHEAD * DHR))
    WKR = din("wkr", (D, DHR))
    BKRC = din("bkrc", (DHR, 1))
    WO = din("wo", (NHEAD * DK, D))
    BOR = din("bor", (1, D))
    ESEL = din("esel", (E1, D))
    KEYS = din("keys", (E1, D, H1))
    VALUES = din("values", (E1, H1, D))

    OUT = nc.dram_tensor("out", [T, D], F32, kind="ExternalOutput").ap()

    tap_outs = {}

    def tap(name, shape, dtype=F32):
        if name in taps:
            tap_outs[name] = nc.dram_tensor(
                "tap_" + name, list(shape), dtype, kind="ExternalOutput").ap()
            return tap_outs[name]
        return None

    sb = nc.alloc_sbuf_tensor

    X = sb("X", [128, NT, D], F32)
    XNT = sb("XNT", [128, NK, T], BF16)
    BIGT = sb("BIG", [128, 16384], BF16)
    ARENA = sb("ARENA", [128, ARENA_BYTES // 2], BF16)

    ARB = ARENA.ap()
    ARF = ARENA.ap().bitcast(F32)

    def arn_b(off_bytes, *shape):
        n = int(np.prod(shape))
        assert off_bytes % 2 == 0 and off_bytes + 2 * n <= ARENA_BYTES, (off_bytes, shape)
        v = ARB[:, off_bytes // 2: off_bytes // 2 + n]
        if len(shape) > 1:
            dims = " ".join(f"d{i}" for i in range(len(shape)))
            v = v.rearrange(f"p ({dims}) -> p {dims}",
                            **{f"d{i}": s for i, s in enumerate(shape)})
        return v

    def arn_f(off_bytes, *shape):
        n = int(np.prod(shape))
        assert off_bytes % 4 == 0 and off_bytes + 4 * n <= ARENA_BYTES, (off_bytes, shape)
        v = ARF[:, off_bytes // 4: off_bytes // 4 + n]
        if len(shape) > 1:
            dims = " ".join(f"d{i}" for i in range(len(shape)))
            v = v.rearrange(f"p ({dims}) -> p {dims}",
                            **{f"d{i}": s for i, s in enumerate(shape)})
        return v

    IDB = sb("IDB", [128, 128], BF16)
    IDF = sb("IDF", [128, 128], F32)
    JB = sb("JB", [128, 128], BF16)
    ONESC = sb("ONESC", [128, 1], BF16)
    ONESR = sb("ONESR", [1, N512], BF16)

    LN_SUM = sb("LN_SUM", [128, NT], F32)
    LN_SQ = sb("LN_SQ", [128, NT], F32)
    LN_MEAN = sb("LN_MEAN", [128, NT], F32)
    LN_RSTD = sb("LN_RSTD", [128, NT], F32)
    SCR8 = sb("SCR8", [128, D], F32)
    XNB = sb("XNB", [128, D], BF16)

    PQH = sb("PQH", [DS, 16 * DS], BF16)
    AMS = sb("AMS", [DS, DS], F32)
    ATS = sb("ATS", [DS, DS], F32)
    ATB = sb("ATB", [DS, DS], BF16)
    BPC = sb("BPC", [DS, 1], BF16)
    CPC = sb("CPC", [DS, 1], BF16)
    DPS = sb("DPS", [1, 1], F32)
    BT_CH = sb("BT_CH", [DS, 128], BF16)
    CT_CH = sb("CT_CH", [DS, 128], BF16)
    BTT = sb("BTT", [128, DS], BF16)
    WB_M = sb("WB_M", [128, DS], BF16)
    WCT = sb("WCT", [DS, 128], BF16)
    ATLT = sb("ATLT", [DS, DS], BF16)
    KROW = sb("KROW", [1, 128], F32)
    KROWB = sb("KROWB", [1, 128], BF16)
    TTOEP = sb("TTOEP", [128, 128], BF16)
    HT = sb("HT", [DS, 2, D], BF16)

    BINB = sb("BINB", [1, DI], BF16)
    CONVWS = sb("CONVWS", [128, NM, 4], F32)
    CONVBS = sb("CONVBS", [128, NM], F32)
    RB1S = sb("RB1S", [128, NK], F32)
    RW2B = sb("RW2B", [128, NK, E0], BF16)
    RB2S = sb("RB2S", [E0, 1], F32)
    LGTM = sb("LGTM", [128, NT, E0], F32)
    TOP_M1 = sb("TOP_M1", [128, NT], F32)
    TOP_M1N = sb("TOP_M1N", [128, NT], F32)
    TOP_M2 = sb("TOP_M2", [128, NT], F32)
    TOP_E1 = sb("TOP_E1", [128, NT, E0], F32)
    TOP_SCR = sb("TOP_SCR", [128, E0], F32)
    TOP_WS = sb("TOP_WS", [128, NT], F32)
    WTM = sb("WTM", [128, NT, E0], BF16)
    EB1S = sb("EB1S", [128, E0, NM], F32)

    TOP_MSK = sb("TOP_MSK", [128, NT, E0], F32)
    WROW = sb("WROW", [1, T], BF16)

    BDQS = sb("BDQS", [DC, 1], F32)
    BDKVS = sb("BDKVS", [DC, 1], F32)
    BKRS = sb("BKRS", [DHR, 1], F32)
    KMAX = sb("KMAX", [1, 2], F32)

    WIN_H = arn_b(0, NK, D)
    WOUT_H = arn_b(16384, NK, D)
    UC = arn_b(32768, 2, NK, 132)
    VCF = arn_b(36992, NK, 128)
    VC = arn_b(39040, 2, D)
    SYC = arn_b(43136, NK, 128)
    BOUTB = arn_b(45184, D)[0:1, :]

    PS = [nc.alloc_psum_tensor(f"PS{i}", [128, N512], F32) for i in range(8)]

    dma_x = p.sem("dma_x")
    dma_c = p.sem("dma_c")
    dma_w = p.sem("dma_w")
    s_mm = p.sem("s_mm")
    s_ev = p.sem("s_ev")
    s_vv = p.sem("s_vv")
    s_pr = p.sem("s_pr")
    s_tr = p.sem("s_tr")
    s_st = p.sem("s_st")
    s_gp = p.sem("s_gp")
    s_fb = p.sem("s_fb")
    s_ex = p.sem("s_ex")

    def dma(eng, out_ap, in_ap, sem):
        p.op(eng, lambda E, o=out_ap, i=in_ap: E.dma_start(out=o, in_=i),
             inc=sem, inc_n=16)

    dma("sync", X.ap(), X_hbm.rearrange("(n p) d -> p n d", p=128), dma_x)

    dma("sync", IDF.ap(), ID_hbm, dma_c)
    dma("gpsimd", IDB.ap(), ID_hbm, dma_c)
    dma("gpsimd", JB.ap(), J_hbm, dma_c)
    dma("sync", CONVWS.ap(), CONVW, dma_c)
    dma("sync", CONVBS.ap(), CONVB, dma_c)
    dma("sync", AMS.ap(), A_in, dma_c)
    dma("gpsimd", BPC.ap(), BP, dma_c)
    dma("gpsimd", CPC.ap(), CP, dma_c)
    dma("sync", DPS.ap(), DP, dma_c)
    dma("gpsimd", BINB.ap(), BINT, dma_c)
    dma("gpsimd", BOUTB, BOUT, dma_c)
    dma("sync", RB1S.ap(), RB1C, dma_c)
    dma("gpsimd", RW2B.ap(), RW2.rearrange("(k p) e -> p k e", p=128), dma_c)
    dma("sync", RB2S.ap(), RB2C, dma_c)
    dma("sync", EB1S.ap(), EB1C, dma_c)
    dma("sync", BDQS.ap(), BDQC, dma_c)
    dma("sync", BDKVS.ap(), BDKVC, dma_c)
    dma("sync", BKRS.ap(), BKRC, dma_c)
    CONST_N = dma_c.total

    p.emit("vector", lambda E: E.memset(ONESC.ap(), 1.0))
    p.emit("vector", lambda E: E.memset(ONESR.ap(), 1.0))
    p.emit("vector", lambda E: E.memset(TTOEP.ap(), 0.0))
    p.op("vector", lambda E: E.memset(HT.ap(), 0.0), inc=s_vv)
    V_MEMSET = s_vv.total

    def ssm_prep():
        p.wait("scalar", dma_c, CONST_N)
        p.op("scalar", lambda E: E.activation(ATS.ap(), AMS.ap(), AF.Tanh),
             inc=s_pr)
        p.wait("vector", s_pr, s_pr.total)
        p.op("vector", lambda E: E.tensor_copy(ATB.ap(), ATS.ap()))
        p.op("vector", lambda E: E.tensor_copy(PQH[0:DS, 0:DS], ATS.ap()),
             inc=s_pr)
        p.wait("tensor", s_pr, s_pr.total)
        p.wait("tensor", dma_c, CONST_N)
        p.op("tensor", lambda E: E.transpose(PS[0].ap().bitcast(BF16)[0:DS, 0:DS],
                                             ATB.ap(), IDB[0:DS, 0:DS]), inc=s_mm)
        p.wait("vector", s_mm, s_mm.total)
        p.op("vector", lambda E: E.tensor_copy(
            PQH[0:DS, DS:2 * DS], PS[0].ap().bitcast(BF16)[0:DS, 0:DS]), inc=s_pr)

        def slotP(j):
            return PQH[0:DS, (2 * j) * DS:(2 * j + 1) * DS]

        def slotQ(j):
            return PQH[0:DS, (2 * j + 1) * DS:(2 * j + 2) * DS]

        for j in range(7):
            p.wait("tensor", s_pr, s_pr.total)
            p.op("tensor", lambda E, a=slotQ(j), b=slotP(j): E.matmul(
                PS[1][0:DS, 0:DS], a, b, start=True, stop=True))
            p.op("tensor", lambda E, a=slotP(j), b=slotQ(j): E.matmul(
                PS[1][0:DS, DS:2 * DS], a, b, start=True,
                stop=True), inc=s_mm)
            p.wait("vector", s_mm, s_mm.total)
            p.op("vector", lambda E, d=slotP(j + 1): E.tensor_copy(
                d, PS[1][0:DS, 0:DS]))
            p.op("vector", lambda E, d=slotQ(j + 1): E.tensor_copy(
                d, PS[1][0:DS, DS:2 * DS]), inc=s_pr)
        P128, Q128 = slotP(7), slotQ(7)
        p.wait("vector", s_pr, s_pr.total)
        p.op("vector", lambda E: E.tensor_copy(ATLT.ap(), Q128))
        p.op("vector", lambda E: E.tensor_copy(BT_CH[0:DS, 0:1], BPC.ap()))
        p.op("vector", lambda E: E.tensor_copy(CT_CH[0:DS, 0:1], CPC.ap()),
             inc=s_pr)
        for j in range(7):
            m = 1 << j
            p.wait("tensor", s_pr, s_pr.total)
            p.op("tensor", lambda E, q=slotQ(j), mm=m: E.matmul(
                PS[1][0:DS, 0:mm], q, BT_CH[0:DS, 0:mm],
                start=True, stop=True))
            p.op("tensor", lambda E, pp=slotP(j), mm=m: E.matmul(
                PS[1][0:DS, 128:128 + mm], pp, CT_CH[0:DS, 0:mm],
                start=True, stop=True), inc=s_mm)
            p.wait("vector", s_mm, s_mm.total)
            p.op("vector", lambda E, mm=m: E.tensor_copy(
                BT_CH[0:DS, mm:2 * mm], PS[1][0:DS, 0:mm]))
            p.op("vector", lambda E, mm=m: E.tensor_copy(
                CT_CH[0:DS, mm:2 * mm], PS[1][0:DS, 128:128 + mm]),
                inc=s_pr)

        p.wait("tensor", s_pr, s_pr.total)
        p.op("tensor", lambda E: E.matmul(PS[1][0:1, 0:128], BPC.ap(),
                                          CT_CH[0:DS, 0:128], start=True,
                                          stop=True), inc=s_mm)
        p.wait("vector", s_mm, s_mm.total)
        p.op("vector", lambda E: E.tensor_copy(KROW.ap(), PS[1][0:1, 0:128]),
             inc=s_vv)
        p.wait("scalar", s_vv, s_vv.total)
        p.op("scalar", lambda E: E.add(KROW[0:1, 0:1], KROW[0:1, 0:1],
                                       DPS.ap()), inc=s_ev)
        p.wait("vector", s_ev, s_ev.total)
        p.op("vector", lambda E: E.tensor_copy(KROWB.ap(), KROW.ap()),
             inc=s_pr)
        p.wait("scalar", s_pr, s_pr.total)
        p.wait("scalar", s_vv, V_MEMSET)
        for s in range(128):
            p.op("scalar",
                 lambda E, ss=s: E.dma_start(out=TTOEP[ss:ss + 1, ss:128],
                                             in_=KROWB[0:1, 0:128 - ss]),
                 inc=s_pr, inc_n=16)
        t_ready = s_pr.total

        p.emit("vector", lambda E: E.drain())
        p.op("vector", lambda E: E.tensor_copy(WCT[0:DS, 0:127],
                                               CT_CH[0:DS, 1:128]), inc=s_pr)
        p.wait("tensor", s_pr, s_pr.total)
        p.op("tensor", lambda E: E.matmul(PS[1][0:DS, 0:1],
                                          P128, CPC.ap(), start=True,
                                          stop=True), inc=s_mm)
        p.wait("vector", s_mm, s_mm.total)
        p.op("vector", lambda E: E.tensor_copy(
            WCT[0:DS, 127:128], PS[1][0:DS, 0:1]), inc=s_pr)
        p.wait("tensor", s_pr, s_pr.total)
        p.op("tensor", lambda E: E.transpose(PS[2].ap().bitcast(BF16)[:, 0:DS],
                                             BT_CH[0:DS, 0:128],
                                             IDB[0:DS, 0:DS]), inc=s_mm)
        p.wait("vector", s_mm, s_mm.total)
        p.op("vector", lambda E: E.tensor_copy(
            BTT.ap(), PS[2].ap().bitcast(BF16)[:, 0:DS]), inc=s_pr)
        p.wait("tensor", s_pr, s_pr.total)
        p.op("tensor", lambda E: E.matmul(PS[2][:, 0:DS],
                                          JB.ap(), BTT.ap(), start=True,
                                          stop=True), inc=s_mm)
        p.wait("vector", s_mm, s_mm.total)
        p.op("vector", lambda E: E.tensor_copy(
            WB_M.ap(), PS[2][:, 0:DS]), inc=s_pr)
        return t_ready

    T_READY = ssm_prep()
    PREP_DONE = s_pr.total

    def tap_store(name, sbuf_ap, shape, dtype=F32, wait=()):
        t_ap = tap(name, shape, dtype)
        if t_ap is None:
            return
        for sem, val in wait:
            p.wait("sync", sem, val)
        p.op("sync", lambda E, o=t_ap, i=sbuf_ap: E.dma_start(out=o, in_=i),
             inc=s_st, inc_n=16)

    tap_store("ttoep", TTOEP.ap(), (128, 128), BF16, wait=[(s_pr, T_READY)])
    tap_store("wb", WB_M.ap(), (128, DS), BF16, wait=[(s_pr, PREP_DONE)])
    tap_store("wct", WCT.ap(), (DS, 128), BF16, wait=[(s_pr, PREP_DONE)])
    tap_store("atlt", ATLT.ap(), (DS, DS), BF16, wait=[(s_pr, PREP_DONE)])
    tap_store("krow", KROW.ap(), (1, 128), F32, wait=[(s_pr, PREP_DONE)])

    def layernorm(add_xn_to_x):
        p.wait("scalar", dma_x, 16)
        p.wait("vector", dma_x, 16)
        p.op("vector", lambda E: E.memset(LN_SUM.ap(), 0.0))
        p.op("vector", lambda E: E.memset(LN_MEAN.ap(), 0.0), inc=s_vv)
        p.op("vector", lambda E: E.memset(LN_SQ.ap(), 0.0), inc=s_vv)
        p.wait("scalar", s_vv, s_vv.total)
        for t in range(NT):
            p.op("scalar", lambda E, tt=t: E.activation(
                SCR8.ap(), X[:, tt, :], AF.Square,
                accum_out=LN_SQ[:, tt:tt + 1]), inc=s_ev)
            p.op("vector", lambda E, tt=t: E.tensor_scalar(
                XNB.ap().bitcast(F32)[:, 0:512], X[:, tt, 0:512], 1.0, 0.0,
                ALU.mult, ALU.add, accum_out=LN_SUM[:, tt:tt + 1]), inc=s_vv)
            p.op("vector", lambda E, tt=t: E.tensor_scalar(
                XNB.ap().bitcast(F32)[:, 0:512], X[:, tt, 512:1024], 1.0, 0.0,
                ALU.mult, ALU.add, accum_out=LN_MEAN[:, tt:tt + 1]), inc=s_vv)
        p.wait("vector", s_ev, s_ev.total)
        p.op("vector", lambda E: E.tensor_tensor(
            LN_SUM.ap(), LN_SUM.ap(), LN_MEAN.ap(), ALU.add))
        p.emit("vector", lambda E: E.drain())
        p.op("vector", lambda E: E.tensor_scalar(
            LN_MEAN.ap(), LN_SUM.ap(), 1.0 / D, 0.0, ALU.mult, ALU.add))
        p.op("vector", lambda E: E.tensor_scalar(
            LN_SQ.ap(), LN_SQ.ap(), 1.0 / D, 0.0, ALU.mult, ALU.add))
        p.emit("vector", lambda E: E.drain())
        p.op("vector", lambda E: E.tensor_tensor(
            LN_SUM.ap(), LN_MEAN.ap(), LN_MEAN.ap(), ALU.mult))
        p.emit("vector", lambda E: E.drain())
        p.op("vector", lambda E: E.tensor_tensor(
            LN_SQ.ap(), LN_SQ.ap(), LN_SUM.ap(), ALU.subtract))
        p.emit("vector", lambda E: E.drain())
        p.op("vector", lambda E: E.tensor_scalar(
            LN_SQ.ap(), LN_SQ.ap(), LN_EPS, 0.0, ALU.add, ALU.add))
        p.emit("vector", lambda E: E.drain())
        p.op("vector", lambda E: E.reciprocal(LN_SUM.ap(), LN_SQ.ap()),
             inc=s_vv)
        p.wait("scalar", s_vv, s_vv.total)
        p.op("scalar", lambda E: E.activation(LN_RSTD.ap(), LN_SUM.ap(),
                                              AF.Sqrt), inc=s_ev)
        p.wait("vector", s_ev, s_ev.total)
        for t in range(NT):
            if t >= 1:
                p.wait("vector", s_gp, s_gp.total)
                p.wait("vector", s_tr, s_tr.total)
            p.op("vector", lambda E, tt=t: E.tensor_scalar(
                SCR8.ap(), X[:, tt, :], LN_MEAN[:, tt:tt + 1],
                LN_RSTD[:, tt:tt + 1], ALU.subtract, ALU.mult))
            p.op("vector", lambda E: E.tensor_copy(XNB.ap(), SCR8.ap()),
                 inc=s_vv)
            if add_xn_to_x:
                p.wait("gpsimd", s_vv, s_vv.total)
                p.op("gpsimd", lambda E, tt=t: E.tensor_tensor(
                    X[:, tt, :], X[:, tt, :], SCR8.ap(), ALU.add), inc=s_gp)
            else:
                s_gp.inc(0)
            p.wait("tensor", s_vv, s_vv.total)
            for k in range(NK):
                bank = 6 + (k // 4)
                off = (k % 4) * 128
                p.op("tensor", lambda E, kk=k, b=bank, o=off: E.transpose(
                    PS[b].ap().bitcast(BF16)[:, o:o + 128],
                    XNB[:, kk * 128:(kk + 1) * 128], IDB.ap()),
                    inc=(s_mm if k in (3, 7) else None))
            mmv = s_mm.total
            p.wait("vector", s_mm, mmv - 1)
            for k in range(4):
                p.op("vector", lambda E, tt=t, kk=k: E.tensor_copy(
                    XNT[:, kk, tt * 128:(tt + 1) * 128],
                    PS[6].ap().bitcast(BF16)[:, kk * 128:(kk + 1) * 128]))
            p.wait("vector", s_mm, mmv)
            for k in range(4):
                p.op("vector", lambda E, tt=t, kk=k: E.tensor_copy(
                    XNT[:, 4 + kk, tt * 128:(tt + 1) * 128],
                    PS[7].ap().bitcast(BF16)[:, kk * 128:(kk + 1) * 128]),
                    inc=(s_tr if k == 3 else None))
        return s_tr.total, s_gp.total

    XNT_L0, GP_L0 = layernorm(add_xn_to_x=True)
    tap_store("ln_mean", LN_MEAN.ap(), (128, NT), F32, wait=[(s_tr, s_tr.total)])
    tap_store("ln_rstd", LN_RSTD.ap(), (128, NT), F32, wait=[(s_tr, s_tr.total)])
    tap_store("ln_sum", LN_SUM.ap(), (128, NT), F32, wait=[(s_tr, s_tr.total)])
    tap_store("ln_sq", LN_SQ.ap(), (128, NT), F32, wait=[(s_tr, s_tr.total)])
    tap_store("x0", X[:, 0, :], (128, D), F32, wait=[(dma_x, 16)])
    tap_store("xnt", XNT.ap(), (128, NK, T), BF16, wait=[(s_tr, XNT_L0)])


    def stream_mamba_weights(half):
        dma("gpsimd", WIN_H,
            WIN[:, half * 1024:(half + 1) * 1024].rearrange(
                "(k p) m -> p k m", p=128), dma_w)
        dma("gpsimd", WOUT_H,
            WOUT[half * 1024:(half + 1) * 1024, :].rearrange(
                "(k p) m -> p k m", p=128), dma_w)
        return dma_w.total

    tap_xu = tap("xu", (128, NCH, NK, 132), BF16)
    tap_vc = tap("vc", (128, NCH, D), BF16)
    tap_sy = tap("sy", (128, NCH, NK, 128), BF16)

    def mamba_half(half):
        p.wait("gpsimd", s_fb, s_fb.total)
        wv = stream_mamba_weights(half)
        p.wait("vector", s_fb, s_fb.total)
        p.op("vector", lambda E: E.memset(HT[0:DS, 0, :], 0.0))
        p.op("vector", lambda E: E.memset(UC[:, 0, :, 0:4], 0.0))
        p.op("vector", lambda E: E.memset(UC[:, 1, :, 0:4], 0.0), inc=s_vv)
        zs = s_vv.total

        p.wait("tensor", dma_w, wv - 16)
        p.wait("tensor", s_tr, XNT_L0)
        p.wait("tensor", s_pr, PREP_DONE)
        p.wait("tensor", s_vv, zs)
        p.wait("scalar", s_vv, zs)

        cv_done = {}
        for c in range(NCH):
            cur, prv = c % 2, 1 - c % 2
            for mg in range(2):
                bank = mg
                for mi in range(4):
                    m = mg * 4 + mi
                    for k in range(NK):
                        p.op("tensor", lambda E, kk=k, mm=m, b=bank, o=mi * 128, cc=c:
                             E.matmul(PS[b][:, o:o + 128],
                                      WIN_H[:, kk, mm * 128:(mm + 1) * 128],
                                      XNT[:, kk, cc * 128:(cc + 1) * 128],
                                      start=(kk == 0), stop=False))
                    gm0 = (half * NK + m) * 128
                    p.op("tensor", lambda E, b=bank, o=mi * 128, g0=gm0:
                         E.matmul(PS[b][:, o:o + 128],
                                  BINB[0:1, g0:g0 + 128],
                                  ONESR[0:1, 0:128], start=False, stop=True),
                         inc=(s_mm if mi == 3 else None))
            mm_inp = s_mm.total
            if c >= 2:
                p.wait("scalar", s_vv, cv_done[c - 2])
            p.wait("scalar", s_mm, mm_inp - 1)
            p.op("scalar", lambda E, cu=cur: E.copy(
                UC[:, cu, 0:4, 4:132],
                PS[0].ap().rearrange("p (m n) -> p m n", n=128)))
            p.wait("scalar", s_mm, mm_inp)
            p.op("scalar", lambda E, cu=cur: E.copy(
                UC[:, cu, 4:8, 4:132],
                PS[1].ap().rearrange("p (m n) -> p m n", n=128)), inc=s_ev)
            ev_u = s_ev.total
            p.wait("vector", s_ev, ev_u)
            if c > 0:
                p.op("vector", lambda E, cu=cur, pv=prv: E.tensor_copy(
                    UC[:, cu, :, 0:4], UC[:, pv, :, 128:132]))
            for m in range(NK):
                g = half * NK + m
                p.op("vector", lambda E, mm=m, gg=g, cu=cur: E.tensor_scalar(
                    VCF[:, mm, :], UC[:, cu, mm, 1:129], CONVWS[:, gg, 0:1],
                    CONVBS[:, gg:gg + 1], ALU.mult, ALU.add))
            for k in range(1, 4):
                for m in range(NK):
                    g = half * NK + m
                    p.op("vector",
                         lambda E, mm=m, gg=g, kk=k, cu=cur: E.scalar_tensor_tensor(
                             VCF[:, mm, :], UC[:, cu, mm, 1 + kk:129 + kk],
                             CONVWS[:, gg, kk:kk + 1], VCF[:, mm, :],
                             ALU.mult, ALU.add),
                         inc=(s_vv if (m == NK - 1 and k == 3) else None))
            cv_done[c] = s_vv.total
            if tap_xu is not None and half == 0:
                p.wait("sync", s_vv, cv_done[c])
                p.op("sync", lambda E, cc=c, cu=cur: E.dma_start(
                    out=tap_xu[:, cc, :, :], in_=UC[:, cu, :, :]),
                    inc=s_st, inc_n=16)
            p.wait("tensor", s_vv, cv_done[c])
            for m in range(NK):
                bank = 2 + (m // 4)
                off = (m % 4) * 128
                p.op("tensor", lambda E, mm=m, b=bank, o=off: E.transpose(
                    PS[b].ap().bitcast(BF16)[:, o:o + 128],
                    VCF[:, mm, :], IDB.ap()),
                    inc=(s_mm if m in (3, 7) else None))
            mm_tr = s_mm.total
            p.wait("vector", s_mm, mm_tr - 1)
            p.op("vector", lambda E, cu=cur: E.tensor_copy(
                VC[:, cu, 0:512], PS[2].ap().bitcast(BF16)[:, 0:512]))
            p.wait("vector", s_mm, mm_tr)
            p.op("vector", lambda E, cu=cur: E.tensor_copy(
                VC[:, cu, 512:1024], PS[3].ap().bitcast(BF16)[:, 0:512]),
                inc=s_vv)
            vc_ok = s_vv.total
            if tap_vc is not None and half == 0:
                p.wait("sync", s_vv, vc_ok)
                p.op("sync", lambda E, cc=c, cu=cur: E.dma_start(
                    out=tap_vc[:, cc, :], in_=VC[:, cu, :]), inc=s_st, inc_n=16)
            p.wait("tensor", s_vv, vc_ok)
            hcur, hnxt = c % 2, 1 - c % 2
            for m in range(NK):
                bank = 4 + (m // 4)
                off = (m % 4) * 128
                p.op("tensor", lambda E, mm=m, b=bank, o=off, cu=cur: E.matmul(
                    PS[b][:, o:o + 128],
                    VC[:, cu, mm * 128:(mm + 1) * 128], TTOEP.ap(),
                    start=True, stop=False))
                p.op("tensor", lambda E, mm=m, b=bank, o=off, hc=hcur: E.matmul(
                    PS[b][:, o:o + 128],
                    HT[0:DS, hc, mm * 128:(mm + 1) * 128], WCT.ap(),
                    start=False, stop=True),
                    inc=(s_mm if m in (3, 7) else None))
            mm_y = s_mm.total
            for n in range(2):
                p.op("tensor", lambda E, nn=n, hc=hcur: E.matmul(
                    PS[6][0:DS, 0:N512], ATLT.ap(),
                    HT[0:DS, hc, nn * 512:(nn + 1) * 512],
                    start=True, stop=False))
                p.op("tensor", lambda E, nn=n, cu=cur: E.matmul(
                    PS[6][0:DS, 0:N512], WB_M.ap(),
                    VC[:, cu, nn * 512:(nn + 1) * 512],
                    start=False, stop=True), inc=s_mm)
                mm_st = s_mm.total
                p.wait("vector", s_mm, mm_st)
                p.op("vector", lambda E, nn=n, hn=hnxt: E.tensor_copy(
                    HT[0:DS, hn, nn * 512:(nn + 1) * 512],
                    PS[6][0:DS, 0:N512]), inc=s_vv)
                p.wait("tensor", s_vv, s_vv.total)
            p.wait("scalar", s_mm, mm_y - 1)
            p.op("scalar", lambda E: E.activation(
                SYC[:, 0:4, :],
                PS[4].ap().rearrange("p (m n) -> p m n", n=128), AF.Silu))
            p.wait("scalar", s_mm, mm_y)
            p.op("scalar", lambda E: E.activation(
                SYC[:, 4:8, :],
                PS[5].ap().rearrange("p (m n) -> p m n", n=128), AF.Silu),
                inc=s_ev)
            sy_ok = s_ev.total
            if tap_sy is not None and half == 0:
                p.wait("sync", s_ev, sy_ok)
                p.op("sync", lambda E, cc=c: E.dma_start(
                    out=tap_sy[:, cc, :, :], in_=SYC), inc=s_st, inc_n=16)
            p.wait("tensor", s_ev, sy_ok)
            if c == 0:
                p.wait("tensor", dma_w, wv)
            for n in range(2):
                for m in range(NK):
                    last = (m == NK - 1) and (half == 1)
                    p.op("tensor", lambda E, mm=m, nn=n, st_=last: E.matmul(
                        PS[7][:, 0:N512],
                        SYC[:, mm, :], WOUT_H[:, mm, nn * 512:(nn + 1) * 512],
                        start=(mm == 0), stop=st_),
                        inc=(s_mm if last else None))
                if half == 0:
                    p.op("tensor", lambda E, nn=n: E.matmul(
                        PS[7][:, 0:N512], ONESR[0:1, 0:128],
                        BOUTB[0:1, nn * 512:(nn + 1) * 512],
                        start=False, stop=True), inc=s_mm)
                mm_o = s_mm.total
                p.wait("vector", s_mm, mm_o)
                p.op("vector", lambda E, cc=c, nn=n: E.tensor_tensor(
                    X[:, cc, nn * 512:(nn + 1) * 512],
                    X[:, cc, nn * 512:(nn + 1) * 512],
                    PS[7][:, 0:N512], ALU.add), inc=s_fb)
                p.wait("tensor", s_fb, s_fb.total)
        return

    p.wait("vector", s_gp, GP_L0)
    mamba_half(0)
    mamba_half(1)
    MAMBA_DONE_V = s_fb.total

    tap_store("x_after_mamba", X.ap(), (128, NT, D), F32,
              wait=[(s_fb, MAMBA_DONE_V)])

    RW1B = arn_b(0, NK, D)
    LGT = arn_f(16384, T)[0:E0, :]
    EW2RES = arn_b(0, NM, D)
    WBC = arn_b(32768, T)
    EW1R = arn_b(36864, 2, NK, 256)
    WTR = arn_b(45056, T)[0:E0, :]
    EB2RB = arn_b(16384, D)[0:E0, :]

    R1T = BIGT.ap().rearrange("p (m t) -> p m t", m=NK)
    EHW = BIGT.ap().rearrange("p (m t) -> p m t", m=NM)

    def moe_l0(stop=""):
        p.wait("gpsimd", s_fb, s_fb.total)
        dma("gpsimd", RW1B, RW1.rearrange("(k p) m -> p k m", p=128), dma_w)
        w_rw1 = dma_w.total
        p.wait("tensor", dma_w, w_rw1)
        p.wait("tensor", s_fb, s_fb.total)
        p.wait("tensor", s_ev, s_ev.total)
        p.wait("tensor", s_vv, s_vv.total)
        for m8 in range(NK):
            for n in range(4):
                bank = (m8 * 4 + n) % 2
                for k in range(NK):
                    p.op("tensor", lambda E, kk=k, mm=m8, nn=n, b=bank: E.matmul(
                        PS[b][:, 0:N512], RW1B[:, kk, mm * 128:(mm + 1) * 128],
                        XNT[:, kk, nn * 512:(nn + 1) * 512],
                        start=(kk == 0), stop=(kk == NK - 1)),
                        inc=(s_mm if k == NK - 1 else None))
                mmv = s_mm.total
                p.wait("scalar", s_mm, mmv)
                p.op("scalar", lambda E, mm=m8, nn=n, b=bank: E.activation(
                    R1T[:, mm, nn * 512:(nn + 1) * 512], PS[b][:, 0:N512],
                    AF.Silu, bias=RB1S[:, mm:mm + 1]), inc=s_ev)
                p.wait("tensor", s_ev, s_ev.total)
        if stop == "rw1":
            return
        for n in range(4):
            for k in range(NK):
                p.op("tensor", lambda E, kk=k, nn=n: E.matmul(
                    PS[2][0:E0, 0:N512], RW2B[:, kk, :],
                    R1T[:, kk, nn * 512:(nn + 1) * 512],
                    start=(kk == 0), stop=(kk == NK - 1)),
                    inc=(s_mm if k == NK - 1 else None))
            mmv = s_mm.total
            p.wait("scalar", s_mm, mmv)
            p.op("scalar", lambda E, nn=n: E.activation(
                LGT[0:E0, nn * 512:(nn + 1) * 512], PS[2][0:E0, 0:N512],
                AF.Identity, bias=RB2S.ap()), inc=s_ev)
            p.wait("tensor", s_ev, s_ev.total)
        for t in range(NT):
            bank = t % 2
            p.wait("tensor", s_vv, s_vv.total)
            p.op("tensor", lambda E, tt=t, b=bank: E.transpose(
                PS[b][:, 0:E0], LGT[0:E0, tt * 128:(tt + 1) * 128],
                IDF[0:E0, 0:E0]), inc=s_mm)
            mmv = s_mm.total
            p.wait("vector", s_mm, mmv)
            p.op("vector", lambda E, tt=t, b=bank: E.tensor_copy(
                LGTM[:, tt, :], PS[b][:, 0:E0]), inc=s_vv)
        LG_OK = s_vv.total
        if "lgtm" in taps:
            tap_store("lgtm", LGTM.ap(), (128, NT, E0), F32, wait=[(s_vv, LG_OK)])
        if stop == "logits":
            return
        p.op("vector", lambda E: E.reduce_max(
            TOP_M1.ap(), LGTM.ap(), axis=AX.X))
        p.emit("vector", lambda E: E.drain())
        p.op("vector", lambda E: E.tensor_scalar(
            TOP_M1N.ap(), TOP_M1.ap(), -1.0, 0.0, ALU.mult, ALU.add),
            inc=s_vv)
        for t in range(NT):
            p.op("vector", lambda E, tt=t: E.tensor_scalar(
                TOP_MSK[:, tt, :], LGTM[:, tt, :], TOP_M1[:, tt:tt + 1], None,
                ALU.is_ge))
        for t in range(NT):
            p.op("vector", lambda E, tt=t: E.scalar_tensor_tensor(
                TOP_MSK[:, tt, :], TOP_MSK[:, tt, :], -1e30, LGTM[:, tt, :],
                ALU.mult, ALU.add))
        for t in range(NT):
            p.op("vector", lambda E, tt=t: E.reduce_max(
                TOP_M2[:, tt:tt + 1], TOP_MSK[:, tt, :], axis=AX.X),
                inc=(s_vv if t == NT - 1 else None))
        M2_OK = s_vv.total
        p.wait("scalar", s_vv, M2_OK)
        for t in range(NT):
            p.op("scalar", lambda E, tt=t: E.activation(
                TOP_E1[:, tt, :], LGTM[:, tt, :], AF.Exp,
                bias=TOP_M1N[:, tt:tt + 1]), inc=(s_ev if t == NT - 1 else None))
        E1_OK = s_ev.total
        for t in range(NT):
            p.op("vector", lambda E, tt=t: E.tensor_scalar(
                TOP_MSK[:, tt, :], LGTM[:, tt, :], TOP_M2[:, tt:tt + 1], None,
                ALU.is_ge))
        p.wait("vector", s_ev, E1_OK)
        for t in range(NT):
            p.op("vector", lambda E, tt=t: E.tensor_tensor(
                TOP_E1[:, tt, :], TOP_E1[:, tt, :], TOP_MSK[:, tt, :],
                ALU.mult))
        for t in range(NT):
            p.op("vector", lambda E, tt=t: E.reduce_sum(
                TOP_WS[:, tt:tt + 1], TOP_E1[:, tt, :], axis=AX.X))
        p.emit("vector", lambda E: E.drain())
        p.op("vector", lambda E: E.reciprocal(TOP_M1.ap(), TOP_WS.ap()))
        p.emit("vector", lambda E: E.drain())
        for t in range(NT):
            p.op("vector", lambda E, tt=t: E.tensor_scalar(
                WTM[:, tt, :], TOP_E1[:, tt, :], TOP_M1[:, tt:tt + 1], None,
                ALU.mult), inc=(s_vv if t == NT - 1 else None))
        W_OK = s_vv.total
        p.wait("tensor", s_vv, W_OK)
        for t in range(NT):
            bank = t % 2
            p.wait("tensor", s_vv, s_vv.total)
            p.op("tensor", lambda E, tt=t, b=bank: E.transpose(
                PS[b].ap().bitcast(BF16)[0:E0, 0:128], WTM[:, tt, :],
                IDB.ap()), inc=s_mm)
            mmv = s_mm.total
            p.wait("vector", s_mm, mmv)
            p.op("vector", lambda E, tt=t, b=bank: E.tensor_copy(
                WTR[0:E0, tt * 128:(tt + 1) * 128],
                PS[b].ap().bitcast(BF16)[0:E0, 0:128]), inc=s_vv)
        WT_OK = s_vv.total
        if "wtm" in taps:
            tap_store("wtm", WTM.ap(), (128, NT, E0), BF16, wait=[(s_vv, W_OK)])
        if stop == "top2":
            return
        p.wait("gpsimd", s_mm, s_mm.total)
        dma("gpsimd", EB2RB, EB2R, dma_c)
        c_eb2 = dma_c.total
        p.wait("tensor", dma_c, c_eb2)
        p.wait("tensor", s_vv, WT_OK)
        for t in range(NT):
            b0 = 2 + 2 * (t % 2)
            for n in range(2):
                p.op("tensor", lambda E, tt=t, nn=n, b=b0: E.matmul(
                    PS[b + nn][:, 0:N512],
                    WTR[0:E0, tt * 128:(tt + 1) * 128],
                    EB2RB[0:E0, nn * 512:(nn + 1) * 512],
                    start=True, stop=True),
                    inc=(s_mm if n == 1 else None))
            mmv = s_mm.total
            p.wait("vector", s_mm, mmv)
            for n in range(2):
                p.op("vector", lambda E, tt=t, nn=n, b=b0: E.tensor_tensor(
                    X[:, tt, nn * 512:(nn + 1) * 512],
                    X[:, tt, nn * 512:(nn + 1) * 512],
                    PS[b + nn][:, 0:N512], ALU.add),
                    inc=(s_fb if n == 1 else None))
            if t % 2 == 1:
                p.wait("tensor", s_fb, s_fb.total)
        ew2_free_mm = s_mm.total
        ew1_hist = []
        n_exp = 1 if stop == "e1" else E0
        for e in range(n_exp):
            p.wait("gpsimd", s_mm, ew2_free_mm)
            if e == 0:
                p.wait("gpsimd", s_ev, s_ev.total)
            dma("gpsimd", EW2RES,
                EW2[e, :, :].rearrange("(k p) m -> p k m", p=128), dma_w)
            w_ew2 = dma_w.total
            if stop == "ew2dma":
                p.wait("sync", dma_w, w_ew2)
                return
            p.wait("vector", s_vv, WT_OK)
            p.wait("vector", s_fb, s_fb.total)
            p.wait("scalar", s_vv, s_vv.total)
            p.op("scalar", lambda E, ee=e: E.dma_start(
                out=WROW.ap(), in_=WTR[ee:ee + 1, :]), inc=s_ev, inc_n=16)
            wr_ok = s_ev.total
            if stop == "wrow":
                p.wait("sync", s_ev, wr_ok)
                return
            p.wait("tensor", s_ev, wr_ok)
            for n in range(4):
                b = 2 + (n % 2)
                if n >= 2:
                    p.wait("tensor", s_vv, s_vv.total - 1)
                p.op("tensor", lambda E, nn=n, bb=b: E.matmul(
                    PS[bb][:, 0:N512], ONESR[0:1, 0:128],
                    WROW[0:1, nn * 512:(nn + 1) * 512],
                    start=True, stop=True), inc=s_mm)
                mmv = s_mm.total
                p.wait("vector", s_mm, mmv)
                p.op("vector", lambda E, nn=n, bb=b: E.tensor_copy(
                    WBC[:, nn * 512:(nn + 1) * 512], PS[bb][:, 0:N512]),
                    inc=s_vv)
            wbc_ok = s_vv.total
            if stop == "wbc":
                return
            for ht in range(2):
                for j in range(8):
                    gj = len(ew1_hist)
                    ring = gj % 2
                    if gj >= 2:
                        p.wait("gpsimd", s_mm, ew1_hist[gj - 2])
                    dma("gpsimd", EW1R[:, ring, :, :],
                        EW1[e, :, j * 256:(j + 1) * 256].rearrange(
                            "(k p) h -> p k h", p=128), dma_w)
                    w_blk = dma_w.total
                    p.wait("tensor", dma_w, w_blk)
                    if j == 0:
                        p.wait("tensor", s_vv, wbc_ok)
                    for mi in range(2):
                        m = 2 * j + mi
                        for n in range(2):
                            bank = (2 * m + n) % 2
                            if s_ex.total >= 1:
                                p.wait("tensor", s_ex, s_ex.total - 1)
                            for k in range(NK):
                                p.op("tensor", lambda E, kk=k, rr=ring, nn=n,
                                     b=bank, h0=ht, mo=mi * 128: E.matmul(
                                         PS[b][:, 0:N512],
                                         EW1R[:, rr, kk, mo:mo + 128],
                                         XNT[:, kk,
                                             h0 * 1024 + nn * 512:
                                             h0 * 1024 + (nn + 1) * 512],
                                         start=(kk == 0), stop=(kk == NK - 1)),
                                     inc=(s_mm if k == NK - 1 else None))
                            mmv = s_mm.total
                            p.wait("scalar", s_mm, mmv)
                            p.op("scalar", lambda E, mm=m, nn=n, b=bank, ee=e:
                                 E.activation(
                                     EHW[:, mm, nn * 512:(nn + 1) * 512],
                                     PS[b][:, 0:N512], AF.Silu,
                                     bias=EB1S[:, ee, mm:mm + 1]), inc=s_ex)
                            p.wait("vector", s_ex, s_ex.total)
                            p.op("vector", lambda E, mm=m, nn=n, h0=ht:
                                 E.tensor_tensor(
                                     EHW[:, mm, nn * 512:(nn + 1) * 512],
                                     EHW[:, mm, nn * 512:(nn + 1) * 512],
                                     WBC[:, h0 * 1024 + nn * 512:
                                         h0 * 1024 + (nn + 1) * 512],
                                     ALU.mult), inc=s_vv)
                    ew1_hist.append(s_mm.total)
                if stop == "ew1h0":
                    return
                p.wait("tensor", dma_w, w_ew2)
                p.wait("tensor", s_vv, s_vv.total)
                for tl in range(8):
                    t = ht * 8 + tl
                    b0 = 2 + 2 * (tl % 2)
                    for n in range(2):
                        for k in range(NM):
                            p.op("tensor", lambda E, kk=k, nn=n, tt=tl, b=b0:
                                 E.matmul(
                                     PS[b + nn][:, 0:N512],
                                     EHW[:, kk, tt * 128:(tt + 1) * 128],
                                     EW2RES[:, kk, nn * 512:(nn + 1) * 512],
                                     start=(kk == 0), stop=False))
                        p.items["tensor"].pop()
                        p.op("tensor", lambda E, nn=n, tt=tl, b=b0: E.matmul(
                            PS[b + nn][:, 0:N512],
                            EHW[:, NM - 1, tt * 128:(tt + 1) * 128],
                            EW2RES[:, NM - 1, nn * 512:(nn + 1) * 512],
                            start=False, stop=True),
                            inc=(s_mm if n == 1 else None))
                    mmv = s_mm.total
                    p.wait("vector", s_mm, mmv)
                    for n in range(2):
                        p.op("vector", lambda E, tt=t, nn=n, b=b0: E.tensor_tensor(
                            X[:, tt, nn * 512:(nn + 1) * 512],
                            X[:, tt, nn * 512:(nn + 1) * 512],
                            PS[b + nn][:, 0:N512], ALU.add),
                            inc=(s_fb if n == 1 else None))
                    if tl % 2 == 1:
                        p.wait("tensor", s_fb, s_fb.total)
            ew2_free_mm = s_mm.total
        return

    if stop != "mamba":
        moe_l0(stop)
    MOE_DONE = s_fb.total
    tap_store("x_after_l0", X.ap(), (128, NT, D), F32,
              wait=[(s_fb, MOE_DONE)])

    WDQB = arn_b(0, NK, DC)
    WDKVB = arn_b(512, NK, DC)
    WKRB = arn_b(1024, NK, DHR)
    WUQB = arn_b(1536, NHEAD * DK)[0:DC + 1, :]
    WUKB = arn_b(3584, NHEAD * DK)[0:DC + 1, :]
    WUVB = arn_b(5632, NHEAD * DK)[0:DC + 1, :]
    WQRB = arn_b(7680, NHEAD * DHR)[0:DC + 1, :]
    CQT = arn_b(8192, T)[0:DC + 1, :]
    CKVT = arn_b(12288, T)[0:DC + 1, :]
    KRX = arn_b(16384, T)[0:DHR + 1, :]
    QRX = arn_b(20480, 1024)[0:DHR + 1, :]
    QCB = arn_b(22528, 1024)
    KCB = arn_b(24576, T)
    VCT = arn_b(28672, NT, 128)
    PT = arn_b(32768, 2, 1024)
    QSQ = arn_b(32768, T)
    QSQR = arn_b(36864, T)[0:DHR, :]
    KN2 = arn_b(32768, T)[0:1, :]
    SUMS = arn_f(38912, 1024)[0:1, :]
    RCPSB = arn_b(43008, 1024)[0:1, :]
    RCPB = arn_f(45056, 1024)
    WOB = arn_b(8192, NK, D)
    BORB2 = arn_b(22528, D)[0:1, :]
    KEYR = arn_b(0, 2, NK, H1)
    ESELN = arn_b(4096, D)[0:E1, :]
    SELTB = arn_b(4096, T)[0:E1, :]
    VALB = arn_b(12288, E1, D)
    SELBH = arn_b(8192, 1024)
    ESELTB = arn_b(45056, NK, E1)

    def l1_layer():
        p.wait("vector", s_fb, s_fb.total)
        p.wait("scalar", s_fb, s_fb.total)
        p.wait("tensor", s_fb, s_fb.total)
        xnt_l1, _ = layernorm(add_xn_to_x=False)
        if stop == "ln1":
            return

        p.wait("gpsimd", s_mm, s_mm.total)
        dma("gpsimd", WDQB, WDQ.rearrange("(k p) c -> p k c", p=128), dma_w)
        dma("gpsimd", WDKVB, WDKV.rearrange("(k p) c -> p k c", p=128), dma_w)
        dma("gpsimd", WKRB, WKR.rearrange("(k p) c -> p k c", p=128), dma_w)
        dma("gpsimd", WUQB, WUQ, dma_w)
        dma("gpsimd", WUKB, WUK, dma_w)
        dma("gpsimd", WUVB, WUV, dma_w)
        dma("gpsimd", WQRB, WQR, dma_w)
        w_attn = dma_w.total
        if stop == "attw":
            p.wait("sync", dma_w, w_attn)
            return

        p.wait("tensor", dma_w, w_attn)
        p.wait("tensor", s_tr, xnt_l1)
        p.op("vector", lambda E: E.memset(CQT[DC:DC + 1, :], 1.0))
        p.op("vector", lambda E: E.memset(CKVT[DC:DC + 1, :], 1.0))
        p.op("vector", lambda E: E.memset(KRX[DHR:DHR + 1, :], 1.0), inc=s_vv)
        ones_ok = s_vv.total

        def proj32(dst, lhs_w, bias_ap, nm):
            for n in range(4):
                bank = n % 2
                if s_ex.total >= 1:
                    p.wait("tensor", s_ex, s_ex.total - 1)
                for k in range(NK):
                    p.op("tensor", lambda E, kk=k, nn=n, b=bank, w=lhs_w: E.matmul(
                        PS[b][0:nm, 0:N512], w[:, kk, :],
                        XNT[:, kk, nn * 512:(nn + 1) * 512],
                        start=(kk == 0), stop=(kk == NK - 1)),
                        inc=(s_mm if k == NK - 1 else None))
                mmv = s_mm.total
                p.wait("scalar", s_mm, mmv)
                p.op("scalar", lambda E, nn=n, b=bank, d=dst, ba=bias_ap, m=nm:
                     E.activation(d[0:m, nn * 512:(nn + 1) * 512],
                                  PS[b][0:m, 0:N512], AF.Identity, bias=ba),
                     inc=s_ex)

        proj32(CQT, WDQB, BDQS.ap(), DC)
        proj32(CKVT, WDKVB, BDKVS.ap(), DC)
        proj32(KRX, WKRB, BKRS.ap(), DHR)
        p.wait("tensor", s_ex, s_ex.total)
        p.wait("tensor", s_vv, ones_ok)

        n_heads = 1 if stop == "h1" else NHEAD
        for h in range(n_heads):
            for n in range(4):
                bank = n % 2
                if s_ex.total >= 1:
                    p.wait("tensor", s_ex, s_ex.total - 1)
                p.op("tensor", lambda E, nn=n, b=bank, hh=h: E.matmul(
                    PS[b][:, 0:N512], WUKB[0:DC + 1, hh * 128:(hh + 1) * 128],
                    CKVT[0:DC + 1, nn * 512:(nn + 1) * 512],
                    start=True, stop=True), inc=s_mm)
                mmv = s_mm.total
                p.wait("scalar", s_mm, mmv)
                p.op("scalar", lambda E, nn=n, b=bank: E.copy(
                    KCB[:, nn * 512:(nn + 1) * 512], PS[b][:, 0:N512]),
                    inc=s_ex)
            p.wait("tensor", s_vv, s_vv.total)
            for tg in range(4):
                bank = 6 + (tg % 2)
                if tg >= 2:
                    p.wait("tensor", s_ex, s_ex.total - 1)
                for ti in range(4):
                    tp = tg * 4 + ti
                    p.op("tensor", lambda E, t_=tp, b=bank, o=ti * 128, hh=h:
                         E.matmul(PS[b][:, o:o + 128],
                                  CKVT[0:DC + 1, t_ * 128:(t_ + 1) * 128],
                                  WUVB[0:DC + 1, hh * 128:(hh + 1) * 128],
                                  start=True, stop=True),
                         inc=(s_mm if ti == 3 else None))
                mmv = s_mm.total
                p.wait("scalar", s_mm, mmv)
                p.op("scalar", lambda E, tg_=tg, b=bank: E.copy(
                    VCT[:, tg_ * 4:(tg_ + 1) * 4, :],
                    PS[b].ap().rearrange("p (a q) -> p a q", q=128)),
                    inc=s_ex)
            p.wait("vector", s_ex, s_ex.total)
            p.op("vector", lambda E: E.tensor_tensor(
                QSQ, KCB, KCB, ALU.mult))
            p.op("vector", lambda E: E.tensor_tensor(
                QSQR[0:DHR, :], KRX[0:DHR, :], KRX[0:DHR, :], ALU.mult),
                inc=s_vv)
            p.wait("tensor", s_vv, s_vv.total)
            p.wait("tensor", s_ex, s_ex.total)
            for n in range(4):
                kb = 4 + (n % 2)
                if n >= 2:
                    p.wait("tensor", s_vv, s_vv.total - 1)
                p.op("tensor", lambda E, nn=n, b=kb: E.matmul(
                    PS[b][0:1, 0:N512], ONESC.ap(),
                    QSQ[:, nn * 512:(nn + 1) * 512], start=True, stop=False))
                p.op("tensor", lambda E, nn=n, b=kb: E.matmul(
                    PS[b][0:1, 0:N512], ONESC[0:DHR, 0:1],
                    QSQR[0:DHR, nn * 512:(nn + 1) * 512],
                    start=False, stop=True), inc=s_mm)
                mmv = s_mm.total
                p.wait("vector", s_mm, mmv)
                p.op("vector", lambda E, nn=n, b=kb: E.tensor_copy(
                    KN2[0:1, nn * 512:(nn + 1) * 512], PS[b][0:1, 0:N512]),
                    inc=s_vv)
            p.emit("vector", lambda E: E.drain())
            p.op("vector", lambda E: E.reduce_max(
                KMAX[0:1, 0:1], KN2[0:1, :], axis=AX.X))
            p.emit("vector", lambda E: E.drain())
            p.op("vector", lambda E: E.tensor_scalar(
                KMAX[0:1, 1:2], KMAX[0:1, 0:1], 0.5, 0.0, ALU.mult, ALU.add),
                inc=s_vv)
            p.emit("vector", lambda E: E.drain())
            kmax_ok = s_vv.total

            for half in range(2):
                toff = half * 1024
                for n in range(2):
                    bank = n % 2
                    p.wait("tensor", s_ex, s_ex.total - 1)
                    p.op("tensor", lambda E, nn=n, b=bank, hh=h, to=toff: E.matmul(
                        PS[b][:, 0:N512],
                        WUQB[0:DC + 1, hh * 128:(hh + 1) * 128],
                        CQT[0:DC + 1, to + nn * 512:to + (nn + 1) * 512],
                        start=True, stop=True), inc=s_mm)
                    mmv = s_mm.total
                    p.wait("scalar", s_mm, mmv)
                    p.op("scalar", lambda E, nn=n, b=bank: E.copy(
                        QCB[:, nn * 512:(nn + 1) * 512], PS[b][:, 0:N512]),
                        inc=s_ex)
                for n in range(2):
                    bank = n % 2
                    p.wait("tensor", s_ex, s_ex.total - 1)
                    p.op("tensor", lambda E, nn=n, b=bank, hh=h, to=toff: E.matmul(
                        PS[b][0:DHR, 0:N512],
                        WQRB[0:DC + 1, hh * 32:(hh + 1) * 32],
                        CQT[0:DC + 1, to + nn * 512:to + (nn + 1) * 512],
                        start=True, stop=True), inc=s_mm)
                    mmv = s_mm.total
                    p.wait("scalar", s_mm, mmv)
                    p.op("scalar", lambda E, nn=n, b=bank: E.copy(
                        QRX[0:DHR, nn * 512:(nn + 1) * 512],
                        PS[b][0:DHR, 0:N512]), inc=s_ex)
                p.wait("vector", s_ex, s_ex.total)
                p.op("vector", lambda E: E.tensor_tensor(
                    QSQ[:, 0:1024], QCB, QCB, ALU.mult))
                p.op("vector", lambda E: E.tensor_tensor(
                    QSQR[0:DHR, 0:1024], QRX[0:DHR, :], QRX[0:DHR, :],
                    ALU.mult), inc=s_vv)
                p.wait("tensor", s_vv, s_vv.total)
                for n in range(2):
                    qb = 4 + (n % 2)
                    p.op("tensor", lambda E, nn=n, b=qb: E.matmul(
                        PS[b][0:1, 0:N512], ONESC.ap(),
                        QSQ[:, nn * 512:(nn + 1) * 512], start=True, stop=False))
                    p.op("tensor", lambda E, nn=n, b=qb: E.matmul(
                        PS[b][0:1, 0:N512], ONESC[0:DHR, 0:1],
                        QSQR[0:DHR, nn * 512:(nn + 1) * 512],
                        start=False, stop=True), inc=s_mm)
                    mmv = s_mm.total
                    p.wait("scalar", s_mm, mmv)
                    p.op("scalar", lambda E, nn=n, b=qb: E.copy(
                        RCPSB[0:1, nn * 512:(nn + 1) * 512],
                        PS[b][0:1, 0:N512]), inc=s_ex)
                p.wait("vector", s_ex, s_ex.total)
                p.wait("vector", s_vv, kmax_ok)
                p.op("vector", lambda E: E.tensor_scalar(
                    QRX[DHR:DHR + 1, :], RCPSB[0:1, :], -0.5,
                    KMAX[0:1, 1:2], ALU.mult, ALU.subtract), inc=s_vv)
                qrx_ok = s_vv.total

                p.wait("tensor", s_vv, qrx_ok)
                for tp in range(NT):
                    for n in range(2):
                        g = tp * 2 + n
                        bank = g % 2
                        if s_ex.total >= 1:
                            p.wait("tensor", s_ex, s_ex.total - 1)
                        p.op("tensor", lambda E, t_=tp, nn=n, b=bank: E.matmul(
                            PS[b][:, 0:N512], KCB[:, t_ * 128:(t_ + 1) * 128],
                            QCB[:, nn * 512:(nn + 1) * 512],
                            start=True, stop=False))
                        p.op("tensor", lambda E, t_=tp, nn=n, b=bank: E.matmul(
                            PS[b][:, 0:N512],
                            KRX[0:DHR + 1, t_ * 128:(t_ + 1) * 128],
                            QRX[0:DHR + 1, nn * 512:(nn + 1) * 512],
                            start=False, stop=True), inc=s_mm)
                        mmv = s_mm.total
                        p.wait("scalar", s_mm, mmv)
                        p.op("scalar", lambda E, t_=tp, nn=n, b=bank: E.activation(
                            PT[:, t_ % 2, nn * 512:(nn + 1) * 512],
                            PS[b][:, 0:N512], AF.Exp, scale=SCALE), inc=s_ex)
                    p.wait("tensor", s_ex, s_ex.total)
                    for n in range(2):
                        p.op("tensor", lambda E, t_=tp, nn=n: E.matmul(
                            PS[2 + nn][:, 0:N512], VCT[:, t_, :],
                            PT[:, t_ % 2, nn * 512:(nn + 1) * 512],
                            start=(t_ == 0), stop=(t_ == NT - 1)))
                        p.op("tensor", lambda E, t_=tp, nn=n: E.matmul(
                            PS[4 + nn][0:1, 0:N512], ONESC.ap(),
                            PT[:, t_ % 2, nn * 512:(nn + 1) * 512],
                            start=(t_ == 0), stop=(t_ == NT - 1)),
                            inc=(s_mm if (tp == NT - 1 and n == 1) else None))
                mmv = s_mm.total
                p.wait("vector", s_mm, mmv)
                p.op("vector", lambda E: E.tensor_copy(SUMS[0:1, 0:512],
                                                       PS[4][0:1, 0:N512]))
                p.op("vector", lambda E: E.tensor_copy(SUMS[0:1, 512:1024],
                                                       PS[5][0:1, 0:N512]))
                p.emit("vector", lambda E: E.drain())
                def _recip(E):
                    with nc.allow_low_precision(
                            reason="softmax 1/sum broadcast operand in bf16; "
                                   "P and V operands are bf16 anyway"):
                        return E.reciprocal(RCPSB[0:1, :], SUMS[0:1, :])
                p.op("vector", _recip, inc=s_vv)
                rs_ok = s_vv.total
                p.wait("tensor", s_vv, rs_ok)
                for n in range(2):
                    p.op("tensor", lambda E, nn=n: E.matmul(
                        PS[6 + nn][:, 0:N512], ONESR[0:1, 0:128],
                        RCPSB[0:1, nn * 512:(nn + 1) * 512],
                        start=True, stop=True),
                        inc=(s_mm if n == 1 else None))
                mmv = s_mm.total
                p.wait("vector", s_mm, mmv)
                for n in range(2):
                    p.op("vector", lambda E, nn=n: E.tensor_copy(
                        RCPB[:, nn * 512:(nn + 1) * 512],
                        PS[6 + nn][:, 0:N512]), inc=s_vv)
                p.emit("vector", lambda E: E.drain())
                for n in range(2):
                    p.op("vector", lambda E, nn=n, hh=h, to=toff: E.tensor_tensor(
                        OT[:, hh, to + nn * 512:to + (nn + 1) * 512],
                        PS[2 + nn][:, 0:N512],
                        RCPB[:, nn * 512:(nn + 1) * 512], ALU.mult),
                        inc=s_fb)
                p.wait("tensor", s_fb, s_fb.total)
        if stop in ("h1", "attn"):
            return
        p.wait("gpsimd", s_mm, s_mm.total)
        dma("gpsimd", WOB, WO.rearrange("(k p) d -> p k d", p=128), dma_w)
        dma("gpsimd", BORB2, BOR, dma_w)
        w_wo = dma_w.total
        p.wait("tensor", dma_w, w_wo)
        for t in range(NT):
            b0 = 2 + 2 * (t % 2)
            for n in range(2):
                for k in range(NK):
                    p.op("tensor", lambda E, kk=k, nn=n, tt=t, b=b0: E.matmul(
                        PS[b + nn][:, 0:N512], OT[:, kk, tt * 128:(tt + 1) * 128],
                        WOB[:, kk, nn * 512:(nn + 1) * 512],
                        start=(kk == 0), stop=False))
                p.op("tensor", lambda E, nn=n, b=b0: E.matmul(
                    PS[b + nn][:, 0:N512], ONESR[0:1, 0:128],
                    BORB2[0:1, nn * 512:(nn + 1) * 512],
                    start=False, stop=True),
                    inc=(s_mm if n == 1 else None))
            mmv = s_mm.total
            p.wait("vector", s_mm, mmv)
            for n in range(2):
                p.op("vector", lambda E, tt=t, nn=n, b=b0: E.tensor_tensor(
                    X[:, tt, nn * 512:(nn + 1) * 512],
                    X[:, tt, nn * 512:(nn + 1) * 512],
                    PS[b + nn][:, 0:N512], ALU.add),
                    inc=(s_fb if n == 1 else None))
            if t % 2 == 1:
                p.wait("tensor", s_fb, s_fb.total)
        WO_DONE = s_fb.total
        if stop == "wo":
            return
        p.wait("gpsimd", s_mm, s_mm.total)
        dma("gpsimd", ESELN, ESEL, dma_w)
        dma("gpsimd", VALB, VALUES.rearrange("e p v -> p e v"), dma_w)
        w_ffn = dma_w.total
        if stop == "ffnw":
            p.wait("sync", dma_w, w_ffn)
            return
        p.wait("tensor", dma_w, w_ffn - 16)
        p.wait("tensor", s_fb, WO_DONE)
        for k in range(NK):
            bank = k % 2
            p.wait("tensor", s_vv, s_vv.total)
            p.op("tensor", lambda E, kk=k, b=bank: E.transpose(
                PS[b].ap().bitcast(BF16)[:, 0:E1],
                ESELN[0:E1, kk * 128:(kk + 1) * 128], IDB[0:E1, 0:E1]),
                inc=s_mm)
            mmv = s_mm.total
            p.wait("vector", s_mm, mmv)
            p.op("vector", lambda E, kk=k, b=bank: E.tensor_copy(
                ESELTB[:, kk, :], PS[b].ap().bitcast(BF16)[:, 0:E1]),
                inc=s_vv)
        es_ok = s_vv.total
        if stop == "eselt":
            return
        p.wait("tensor", s_vv, es_ok)
        for n in range(4):
            bank = n % 2
            if s_ex.total >= 1:
                p.wait("tensor", s_ex, s_ex.total - 1)
            for k in range(NK):
                p.op("tensor", lambda E, kk=k, nn=n, b=bank: E.matmul(
                    PS[b][0:E1, 0:N512], ESELTB[:, kk, :],
                    XNT[:, kk, nn * 512:(nn + 1) * 512],
                    start=(kk == 0), stop=(kk == NK - 1)),
                    inc=(s_mm if k == NK - 1 else None))
            mmv = s_mm.total
            p.wait("scalar", s_mm, mmv)
            p.op("scalar", lambda E, nn=n, b=bank: E.activation(
                SELTB[0:E1, nn * 512:(nn + 1) * 512], PS[b][0:E1, 0:N512],
                AF.Sigmoid), inc=s_ex)
        SW = BIGT.ap().rearrange("p (e t) -> p e t", e=E1)
        key_hist = []
        if stop == "sel":
            return
        for ht in range(2):
            toff = ht * 1024
            for e in range(E1):
                gk = len(key_hist)
                ring = gk % 2
                if gk >= 2:
                    p.wait("gpsimd", s_mm, key_hist[gk - 2])
                dma("gpsimd", KEYR[:, ring, :, :],
                    KEYS[e, :, :].rearrange("(k p) hh -> p k hh", p=128), dma_w)
                w_blk = dma_w.total
                p.wait("scalar", s_ex, s_ex.total)
                p.op("scalar", lambda E, ee=e: E.dma_start(
                    out=WROW[0:1, 0:T], in_=SELTB[ee:ee + 1, :]),
                    inc=s_ev, inc_n=16)
                sr_ok = s_ev.total
                p.wait("tensor", s_ev, sr_ok)
                p.wait("tensor", s_vv, s_vv.total)
                for n in range(2):
                    p.op("tensor", lambda E, nn=n, to=toff: E.matmul(
                        PS[6 + nn][:, 0:N512], ONESR[0:1, 0:128],
                        WROW[0:1, to + nn * 512:to + (nn + 1) * 512],
                        start=True, stop=True),
                        inc=(s_mm if n == 1 else None))
                mmv = s_mm.total
                p.wait("vector", s_mm, mmv)
                for n in range(2):
                    p.op("vector", lambda E, nn=n: E.tensor_copy(
                        SELBH[:, nn * 512:(nn + 1) * 512],
                        PS[6 + nn][:, 0:N512]), inc=(s_vv if n == 1 else None))
                selb_ok = s_vv.total
                p.wait("tensor", dma_w, w_blk)
                for n in range(2):
                    g = e * 2 + n
                    bank = g % 2
                    if s_ex.total >= 1:
                        p.wait("tensor", s_ex, s_ex.total - 1)
                    for k in range(NK):
                        p.op("tensor", lambda E, kk=k, rr=ring, nn=n, b=bank,
                             to=toff: E.matmul(
                                 PS[b][:, 0:N512], KEYR[:, rr, kk, :],
                                 XNT[:, kk, to + nn * 512:to + (nn + 1) * 512],
                                 start=(kk == 0), stop=(kk == NK - 1)),
                             inc=(s_mm if k == NK - 1 else None))
                    mmv = s_mm.total
                    p.wait("scalar", s_mm, mmv)
                    p.op("scalar", lambda E, ee=e, nn=n, b=bank: E.activation(
                        SW[:, ee, nn * 512:(nn + 1) * 512], PS[b][:, 0:N512],
                        AF.Relu), inc=s_ex)
                    p.wait("vector", s_ex, s_ex.total)
                    p.wait("vector", s_vv, selb_ok)
                    p.op("vector", lambda E, ee=e, nn=n: E.tensor_tensor(
                        SW[:, ee, nn * 512:(nn + 1) * 512],
                        SW[:, ee, nn * 512:(nn + 1) * 512],
                        SELBH[:, nn * 512:(nn + 1) * 512], ALU.mult),
                        inc=s_vv)
                key_hist.append(s_mm.total)
            if stop == "keysh0":
                return
            p.wait("tensor", s_vv, s_vv.total)
            p.wait("tensor", dma_w, w_ffn)
            for tl in range(8):
                t = ht * 8 + tl
                b0 = 2 + 2 * (tl % 2)
                for n in range(2):
                    for e in range(E1):
                        p.op("tensor", lambda E, ee=e, nn=n, tt=tl, b=b0:
                             E.matmul(
                                 PS[b + nn][:, 0:N512],
                                 SW[:, ee, tt * 128:(tt + 1) * 128],
                                 VALB[:, ee, nn * 512:(nn + 1) * 512],
                                 start=(ee == 0), stop=(ee == E1 - 1)),
                             inc=(s_mm if (e == E1 - 1 and n == 1) else None))
                mmv = s_mm.total
                p.wait("vector", s_mm, mmv)
                for n in range(2):
                    p.op("vector", lambda E, tt=t, nn=n, b=b0: E.tensor_tensor(
                        X[:, tt, nn * 512:(nn + 1) * 512],
                        X[:, tt, nn * 512:(nn + 1) * 512],
                        PS[b + nn][:, 0:N512], ALU.add),
                        inc=(s_fb if n == 1 else None))
                fbv = s_fb.total
                if tl % 2 == 1:
                    p.wait("tensor", s_fb, fbv)
                if stop == "valh0" and tl == 7:
                    return
                p.wait("sync", s_fb, fbv)
                p.op("sync", lambda E, tt=t: E.dma_start(
                    out=OUT.rearrange("(n p) d -> p n d", p=128)[:, tt, :],
                    in_=X[:, tt, :]), inc=s_st, inc_n=16)
        return

    OT = BIGT.ap().rearrange("p (hh t) -> p hh t", hh=NHEAD)
    L0_STOPS = ("mamba", "rw1", "logits", "top2", "wbc", "ew1h0", "e1",
                "ew2dma", "wrow")
    if stop not in L0_STOPS:
        l1_layer()
    L1_DONE = s_st.total

    if stop != "":
        p.wait("sync", s_fb, s_fb.total)
        p.wait("sync", s_gp, s_gp.total)
        p.wait("sync", s_vv, s_vv.total)
        p.wait("sync", s_ex, s_ex.total)
        p.op("sync", lambda E: E.dma_start(
            out=OUT.rearrange("(n p) d -> p n d", p=128), in_=X.ap()),
            inc=s_st, inc_n=16)
    p.wait("sync", s_st, s_st.total)

    import contextlib
    with contextlib.ExitStack() as stack:
        for s in p.sems.values():
            s.h = stack.enter_context(nc.semaphore(s.name))
        block = stack.enter_context(nc.Block())
        for name in Plan.ENGINES:
            getattr(block, name)(
                (lambda nm: lambda E: [fn(E) for fn in p.items[nm]])(name))
    return nc, tap_outs


def _prep_inputs(inputs, core):
    g = inputs
    d = {}
    d["x"] = np.ascontiguousarray(g["L0_attn_x"][core])
    d["idm"] = np.eye(128, dtype=np.float32)
    d["jm"] = np.eye(128, dtype=np.float32)[::-1].copy()
    d["win"] = g["L0_attn_W_in"]
    d["binT"] = np.ascontiguousarray(g["L0_attn_b_in"].reshape(1, DI))
    d["convw"] = np.ascontiguousarray(
        g["L0_attn_conv_w"][:, 0, :].reshape(NM, 128, 4).transpose(1, 0, 2))
    d["convb"] = np.ascontiguousarray(g["L0_attn_conv_b"].reshape(NM, 128).T)
    d["amat"] = g["L0_attn_A"]
    d["bp"] = g["L0_attn_Bp"].reshape(DS, 1)
    d["cp"] = g["L0_attn_Cp"].reshape(DS, 1)
    d["dp"] = g["L0_attn_Dp"].reshape(1, 1)
    d["wout"] = g["L0_attn_W_out"]
    d["boutr"] = g["L0_attn_b_out"].reshape(1, D)
    d["rw1"] = g["L0_ffn_rw1"]
    d["rb1c"] = np.ascontiguousarray(g["L0_ffn_rb1"].reshape(NK, 128).T)
    d["rw2"] = g["L0_ffn_rw2"]
    d["rb2c"] = g["L0_ffn_rb2"].reshape(E0, 1)
    d["ew1"] = g["L0_ffn_ew1"]
    d["eb1c"] = np.ascontiguousarray(
        g["L0_ffn_eb1"].reshape(E0, NM, 128).transpose(2, 0, 1))
    d["ew2"] = g["L0_ffn_ew2"]
    d["eb2r"] = g["L0_ffn_eb2"]
    d["wdq"] = g["L1_attn_W_DQ"]
    d["bdqc"] = g["L1_attn_b_DQ"].reshape(DC, 1)
    d["wuq"] = np.vstack([g["L1_attn_W_UQ"], g["L1_attn_b_UQ"][None]])
    d["wdkv"] = g["L1_attn_W_DKV"]
    d["bdkvc"] = g["L1_attn_b_DKV"].reshape(DC, 1)
    d["wuk"] = np.vstack([g["L1_attn_W_UK"], g["L1_attn_b_UK"][None]])
    d["wuv"] = np.vstack([g["L1_attn_W_UV"], g["L1_attn_b_UV"][None]])
    d["wqr"] = np.vstack([g["L1_attn_W_QR"], g["L1_attn_b_QR"][None]])
    d["wkr"] = g["L1_attn_W_KR"]
    d["bkrc"] = g["L1_attn_b_KR"].reshape(DHR, 1)
    d["wo"] = g["L1_attn_W_O"]
    d["bor"] = g["L1_attn_b_O"].reshape(1, D)
    d["esel"] = g["L1_ffn_expert_sel"]
    d["keys"] = g["L1_ffn_keys"]
    d["values"] = g["L1_ffn_values"]
    return {k: np.ascontiguousarray(np.asarray(v).astype(np.float32, copy=False))
            for k, v in d.items()}


def kernel(**inputs):
    nc, _ = build_nc()
    in_maps = [_prep_inputs(inputs, c) for c in range(8)]
    res = run_bass_kernel_spmd(nc, in_maps, core_ids=list(range(8)))
    return np.stack([r["out"] for r in res.results], axis=0)


# version 06
# speedup vs baseline: 1.3464x; 1.0339x over previous
import os
import numpy as np
import concourse.bass as bass
import concourse.mybir as mybir
from concourse.bass_utils import run_bass_kernel_spmd

F32 = mybir.dt.float32
BF16 = mybir.dt.bfloat16
AF = mybir.ActivationFunctionType
ALU = mybir.AluOpType
AX = mybir.AxisListType

T = 2048
NT = 16
D = 1024
NK = 8
DI = 2048
NM = 16
DS = 16
CH = 128
NCH = 16
E0, H0 = 8, 2048
E1, H1 = 16, 128
NHEAD, DK, DC, DHR = 8, 128, 32, 32
SCALE = 1.0 / float(np.sqrt(DK + DHR))
LN_EPS = 1e-5
N512 = 512

ARENA_BYTES = 49152


class Sem:
    def __init__(self, name):
        self.name = name
        self.total = 0
        self.h = None

    def inc(self, n=1):
        self.total += n
        return self.total


class Plan:
    ENGINES = ("tensor", "vector", "scalar", "gpsimd", "sync")

    def __init__(self):
        self.items = {e: [] for e in self.ENGINES}
        self.sems = {}

    def sem(self, name):
        s = self.sems.get(name)
        if s is None:
            s = self.sems[name] = Sem(name)
        return s

    def emit(self, eng, fn):
        self.items[eng].append(fn)

    def wait(self, eng, sem, val=None):
        v = sem.total if val is None else val
        if v > 0:
            self.emit(eng, lambda E, s=sem, vv=v: E.wait_ge(s.h, vv))

    def op(self, eng, fn, inc=None, inc_n=1):
        if inc is not None:
            inc.inc(inc_n)
            self.emit(eng, lambda E, f=fn, s=inc, n=inc_n: f(E).then_inc(s.h, n))
        else:
            self.emit(eng, fn)


def build_nc(debug_taps=(), stop=None):
    stop = stop if stop is not None else os.environ.get("MK_STOP", "")
    nc = bass.Bass("TRN2", target_bir_lowering=False, debug=False)
    p = Plan()
    taps = set(debug_taps)

    def din(name, shape):
        return nc.dram_tensor(name, list(shape), F32, kind="ExternalInput").ap()

    X_hbm = din("x", (T, D))
    ID_hbm = din("idm", (128, 128))
    J_hbm = din("jm", (128, 128))
    WIN = din("win", (D, DI))
    BINT = din("binT", (1, DI))
    CONVW = din("convw", (128, NM, 4))
    CONVB = din("convb", (128, NM))
    A_in = din("amat", (DS, DS))
    BP = din("bp", (DS, 1))
    CP = din("cp", (DS, 1))
    DP = din("dp", (1, 1))
    WOUT = din("wout", (DI, D))
    BOUT = din("boutr", (1, D))
    RW1 = din("rw1", (D, D))
    RB1C = din("rb1c", (128, NK))
    RW2 = din("rw2", (D, E0))
    RB2C = din("rb2c", (E0, 1))
    EW1 = din("ew1", (E0, D, H0))
    EB1C = din("eb1c", (128, E0, NM))
    EW2 = din("ew2", (E0, H0, D))
    EB2R = din("eb2r", (E0, D))
    WDQ = din("wdq", (D, DC))
    BDQC = din("bdqc", (DC, 1))
    WUQ = din("wuq", (DC + 1, NHEAD * DK))
    WDKV = din("wdkv", (D, DC))
    BDKVC = din("bdkvc", (DC, 1))
    WUK = din("wuk", (DC + 1, NHEAD * DK))
    WUV = din("wuv", (DC + 1, NHEAD * DK))
    WQR = din("wqr", (DC + 1, NHEAD * DHR))
    WKR = din("wkr", (D, DHR))
    BKRC = din("bkrc", (DHR, 1))
    WO = din("wo", (NHEAD * DK, D))
    BOR = din("bor", (1, D))
    ESEL = din("esel", (E1, D))
    KEYS = din("keys", (E1, D, H1))
    VALUES = din("values", (E1, H1, D))

    OUT = nc.dram_tensor("out", [T, D], F32, kind="ExternalOutput").ap()

    tap_outs = {}

    def tap(name, shape, dtype=F32):
        if name in taps:
            tap_outs[name] = nc.dram_tensor(
                "tap_" + name, list(shape), dtype, kind="ExternalOutput").ap()
            return tap_outs[name]
        return None

    sb = nc.alloc_sbuf_tensor

    X = sb("X", [128, NT, D], F32)
    XNT = sb("XNT", [128, NK, T], BF16)
    BIGT = sb("BIG", [128, 16384], BF16)
    ARENA = sb("ARENA", [128, ARENA_BYTES // 2], BF16)

    ARB = ARENA.ap()
    ARF = ARENA.ap().bitcast(F32)

    def arn_b(off_bytes, *shape):
        n = int(np.prod(shape))
        assert off_bytes % 2 == 0 and off_bytes + 2 * n <= ARENA_BYTES, (off_bytes, shape)
        v = ARB[:, off_bytes // 2: off_bytes // 2 + n]
        if len(shape) > 1:
            dims = " ".join(f"d{i}" for i in range(len(shape)))
            v = v.rearrange(f"p ({dims}) -> p {dims}",
                            **{f"d{i}": s for i, s in enumerate(shape)})
        return v

    def arn_f(off_bytes, *shape):
        n = int(np.prod(shape))
        assert off_bytes % 4 == 0 and off_bytes + 4 * n <= ARENA_BYTES, (off_bytes, shape)
        v = ARF[:, off_bytes // 4: off_bytes // 4 + n]
        if len(shape) > 1:
            dims = " ".join(f"d{i}" for i in range(len(shape)))
            v = v.rearrange(f"p ({dims}) -> p {dims}",
                            **{f"d{i}": s for i, s in enumerate(shape)})
        return v

    IDB = sb("IDB", [128, 128], BF16)
    IDF = sb("IDF", [128, 128], F32)
    JB = sb("JB", [128, 128], BF16)
    ONESC = sb("ONESC", [128, 1], BF16)
    ONESR = sb("ONESR", [1, N512], BF16)

    LN_SUM = sb("LN_SUM", [128, NT], F32)
    LN_SQ = sb("LN_SQ", [128, NT], F32)
    LN_MEAN = sb("LN_MEAN", [128, NT], F32)
    LN_RSTD = sb("LN_RSTD", [128, NT], F32)
    SCR8 = sb("SCR8", [128, D], F32)
    XNB = sb("XNB", [128, D], BF16)

    PQH = sb("PQH", [DS, 16 * DS], BF16)
    AMS = sb("AMS", [DS, DS], F32)
    ATS = sb("ATS", [DS, DS], F32)
    ATB = sb("ATB", [DS, DS], BF16)
    BPC = sb("BPC", [DS, 1], BF16)
    CPC = sb("CPC", [DS, 1], BF16)
    DPS = sb("DPS", [1, 1], F32)
    BT_CH = sb("BT_CH", [DS, 128], BF16)
    CT_CH = sb("CT_CH", [DS, 128], BF16)
    BTT = sb("BTT", [128, DS], BF16)
    WB_M = sb("WB_M", [128, DS], BF16)
    WCT = sb("WCT", [DS, 128], BF16)
    ATLT = sb("ATLT", [DS, DS], BF16)
    KROW = sb("KROW", [1, 128], F32)
    KROWB = sb("KROWB", [1, 128], BF16)
    TTOEP = sb("TTOEP", [128, 128], BF16)
    HT = sb("HT", [DS, 2, D], BF16)

    BINB = sb("BINB", [1, DI], BF16)
    CONVWS = sb("CONVWS", [128, NM, 4], F32)
    CONVBS = sb("CONVBS", [128, NM], F32)
    RB1S = sb("RB1S", [128, NK], F32)
    RW2B = sb("RW2B", [128, NK, E0], BF16)
    RB2S = sb("RB2S", [E0, 1], F32)
    LGTM = sb("LGTM", [128, NT, E0], F32)
    TOP_M1 = sb("TOP_M1", [128, NT], F32)
    TOP_M1N = sb("TOP_M1N", [128, NT], F32)
    TOP_M2 = sb("TOP_M2", [128, NT], F32)
    TOP_E1 = sb("TOP_E1", [128, NT, E0], F32)
    TOP_SCR = sb("TOP_SCR", [128, E0], F32)
    TOP_WS = sb("TOP_WS", [128, NT], F32)
    WTM = sb("WTM", [128, NT, E0], BF16)
    EB1S = sb("EB1S", [128, E0, NM], F32)

    TOP_MSK = sb("TOP_MSK", [128, NT, E0], F32)
    WROW = sb("WROW", [1, T], BF16)

    BDQS = sb("BDQS", [DC, 1], F32)
    BDKVS = sb("BDKVS", [DC, 1], F32)
    BKRS = sb("BKRS", [DHR, 1], F32)
    KMAX = sb("KMAX", [1, 2], F32)

    WIN_H = arn_b(0, NK, D)
    WOUT_H = arn_b(16384, NK, D)
    UC = arn_b(32768, 2, NK, 132)
    VCF = arn_b(36992, NK, 128)
    VC = arn_b(39040, 2, D)
    SYC = arn_b(43136, NK, 128)
    BOUTB = arn_b(45184, D)[0:1, :]

    PS = [nc.alloc_psum_tensor(f"PS{i}", [128, N512], F32) for i in range(8)]

    dma_x = p.sem("dma_x")
    dma_c = p.sem("dma_c")
    dma_w = p.sem("dma_w")
    s_mm = p.sem("s_mm")
    s_ev = p.sem("s_ev")
    s_vv = p.sem("s_vv")
    s_pr = p.sem("s_pr")
    s_tr = p.sem("s_tr")
    s_st = p.sem("s_st")
    s_gp = p.sem("s_gp")
    s_fb = p.sem("s_fb")
    s_ex = p.sem("s_ex")

    def dma(eng, out_ap, in_ap, sem):
        p.op(eng, lambda E, o=out_ap, i=in_ap: E.dma_start(out=o, in_=i),
             inc=sem, inc_n=16)

    dma("sync", X.ap(), X_hbm.rearrange("(n p) d -> p n d", p=128), dma_x)

    dma("sync", IDF.ap(), ID_hbm, dma_c)
    dma("gpsimd", IDB.ap(), ID_hbm, dma_c)
    dma("gpsimd", JB.ap(), J_hbm, dma_c)
    dma("sync", CONVWS.ap(), CONVW, dma_c)
    dma("sync", CONVBS.ap(), CONVB, dma_c)
    dma("sync", AMS.ap(), A_in, dma_c)
    dma("gpsimd", BPC.ap(), BP, dma_c)
    dma("gpsimd", CPC.ap(), CP, dma_c)
    dma("sync", DPS.ap(), DP, dma_c)
    dma("gpsimd", BINB.ap(), BINT, dma_c)
    dma("gpsimd", BOUTB, BOUT, dma_c)
    dma("sync", RB1S.ap(), RB1C, dma_c)
    dma("gpsimd", RW2B.ap(), RW2.rearrange("(k p) e -> p k e", p=128), dma_c)
    dma("sync", RB2S.ap(), RB2C, dma_c)
    dma("sync", EB1S.ap(), EB1C, dma_c)
    dma("sync", BDQS.ap(), BDQC, dma_c)
    dma("sync", BDKVS.ap(), BDKVC, dma_c)
    dma("sync", BKRS.ap(), BKRC, dma_c)
    CONST_N = dma_c.total

    p.emit("vector", lambda E: E.memset(ONESC.ap(), 1.0))
    p.emit("vector", lambda E: E.memset(ONESR.ap(), 1.0))
    p.emit("vector", lambda E: E.memset(TTOEP.ap(), 0.0))
    p.op("vector", lambda E: E.memset(HT.ap(), 0.0), inc=s_vv)
    V_MEMSET = s_vv.total

    def ssm_prep():
        p.wait("scalar", dma_c, CONST_N)
        p.op("scalar", lambda E: E.activation(ATS.ap(), AMS.ap(), AF.Tanh),
             inc=s_pr)
        p.wait("vector", s_pr, s_pr.total)
        p.op("vector", lambda E: E.tensor_copy(ATB.ap(), ATS.ap()))
        p.op("vector", lambda E: E.tensor_copy(PQH[0:DS, 0:DS], ATS.ap()),
             inc=s_pr)
        p.wait("tensor", s_pr, s_pr.total)
        p.wait("tensor", dma_c, CONST_N)
        p.op("tensor", lambda E: E.transpose(PS[0].ap().bitcast(BF16)[0:DS, 0:DS],
                                             ATB.ap(), IDB[0:DS, 0:DS]), inc=s_mm)
        p.wait("vector", s_mm, s_mm.total)
        p.op("vector", lambda E: E.tensor_copy(
            PQH[0:DS, DS:2 * DS], PS[0].ap().bitcast(BF16)[0:DS, 0:DS]), inc=s_pr)

        def slotP(j):
            return PQH[0:DS, (2 * j) * DS:(2 * j + 1) * DS]

        def slotQ(j):
            return PQH[0:DS, (2 * j + 1) * DS:(2 * j + 2) * DS]

        for j in range(7):
            p.wait("tensor", s_pr, s_pr.total)
            p.op("tensor", lambda E, a=slotQ(j), b=slotP(j): E.matmul(
                PS[1][0:DS, 0:DS], a, b, start=True, stop=True))
            p.op("tensor", lambda E, a=slotP(j), b=slotQ(j): E.matmul(
                PS[1][0:DS, DS:2 * DS], a, b, start=True,
                stop=True), inc=s_mm)
            p.wait("vector", s_mm, s_mm.total)
            p.op("vector", lambda E, d=slotP(j + 1): E.tensor_copy(
                d, PS[1][0:DS, 0:DS]))
            p.op("vector", lambda E, d=slotQ(j + 1): E.tensor_copy(
                d, PS[1][0:DS, DS:2 * DS]), inc=s_pr)
        P128, Q128 = slotP(7), slotQ(7)
        p.wait("vector", s_pr, s_pr.total)
        p.op("vector", lambda E: E.tensor_copy(ATLT.ap(), Q128))
        p.op("vector", lambda E: E.tensor_copy(BT_CH[0:DS, 0:1], BPC.ap()))
        p.op("vector", lambda E: E.tensor_copy(CT_CH[0:DS, 0:1], CPC.ap()),
             inc=s_pr)
        for j in range(7):
            m = 1 << j
            p.wait("tensor", s_pr, s_pr.total)
            p.op("tensor", lambda E, q=slotQ(j), mm=m: E.matmul(
                PS[1][0:DS, 0:mm], q, BT_CH[0:DS, 0:mm],
                start=True, stop=True))
            p.op("tensor", lambda E, pp=slotP(j), mm=m: E.matmul(
                PS[1][0:DS, 128:128 + mm], pp, CT_CH[0:DS, 0:mm],
                start=True, stop=True), inc=s_mm)
            p.wait("vector", s_mm, s_mm.total)
            p.op("vector", lambda E, mm=m: E.tensor_copy(
                BT_CH[0:DS, mm:2 * mm], PS[1][0:DS, 0:mm]))
            p.op("vector", lambda E, mm=m: E.tensor_copy(
                CT_CH[0:DS, mm:2 * mm], PS[1][0:DS, 128:128 + mm]),
                inc=s_pr)

        p.wait("tensor", s_pr, s_pr.total)
        p.op("tensor", lambda E: E.matmul(PS[1][0:1, 0:128], BPC.ap(),
                                          CT_CH[0:DS, 0:128], start=True,
                                          stop=True), inc=s_mm)
        p.wait("vector", s_mm, s_mm.total)
        p.op("vector", lambda E: E.tensor_copy(KROW.ap(), PS[1][0:1, 0:128]),
             inc=s_vv)
        p.wait("scalar", s_vv, s_vv.total)
        p.op("scalar", lambda E: E.add(KROW[0:1, 0:1], KROW[0:1, 0:1],
                                       DPS.ap()), inc=s_ev)
        p.wait("vector", s_ev, s_ev.total)
        p.op("vector", lambda E: E.tensor_copy(KROWB.ap(), KROW.ap()),
             inc=s_pr)
        p.wait("scalar", s_pr, s_pr.total)
        p.wait("scalar", s_vv, V_MEMSET)
        for s in range(128):
            p.op("scalar",
                 lambda E, ss=s: E.dma_start(out=TTOEP[ss:ss + 1, ss:128],
                                             in_=KROWB[0:1, 0:128 - ss]),
                 inc=s_pr, inc_n=16)
        t_ready = s_pr.total

        p.emit("vector", lambda E: E.drain())
        p.op("vector", lambda E: E.tensor_copy(WCT[0:DS, 0:127],
                                               CT_CH[0:DS, 1:128]), inc=s_pr)
        p.wait("tensor", s_pr, s_pr.total)
        p.op("tensor", lambda E: E.matmul(PS[1][0:DS, 0:1],
                                          P128, CPC.ap(), start=True,
                                          stop=True), inc=s_mm)
        p.wait("vector", s_mm, s_mm.total)
        p.op("vector", lambda E: E.tensor_copy(
            WCT[0:DS, 127:128], PS[1][0:DS, 0:1]), inc=s_pr)
        p.wait("tensor", s_pr, s_pr.total)
        p.op("tensor", lambda E: E.transpose(PS[2].ap().bitcast(BF16)[:, 0:DS],
                                             BT_CH[0:DS, 0:128],
                                             IDB[0:DS, 0:DS]), inc=s_mm)
        p.wait("vector", s_mm, s_mm.total)
        p.op("vector", lambda E: E.tensor_copy(
            BTT.ap(), PS[2].ap().bitcast(BF16)[:, 0:DS]), inc=s_pr)
        p.wait("tensor", s_pr, s_pr.total)
        p.op("tensor", lambda E: E.matmul(PS[2][:, 0:DS],
                                          JB.ap(), BTT.ap(), start=True,
                                          stop=True), inc=s_mm)
        p.wait("vector", s_mm, s_mm.total)
        p.op("vector", lambda E: E.tensor_copy(
            WB_M.ap(), PS[2][:, 0:DS]), inc=s_pr)
        return t_ready

    T_READY = ssm_prep()
    PREP_DONE = s_pr.total

    def tap_store(name, sbuf_ap, shape, dtype=F32, wait=()):
        t_ap = tap(name, shape, dtype)
        if t_ap is None:
            return
        for sem, val in wait:
            p.wait("sync", sem, val)
        p.op("sync", lambda E, o=t_ap, i=sbuf_ap: E.dma_start(out=o, in_=i),
             inc=s_st, inc_n=16)

    tap_store("ttoep", TTOEP.ap(), (128, 128), BF16, wait=[(s_pr, T_READY)])
    tap_store("wb", WB_M.ap(), (128, DS), BF16, wait=[(s_pr, PREP_DONE)])
    tap_store("wct", WCT.ap(), (DS, 128), BF16, wait=[(s_pr, PREP_DONE)])
    tap_store("atlt", ATLT.ap(), (DS, DS), BF16, wait=[(s_pr, PREP_DONE)])
    tap_store("krow", KROW.ap(), (1, 128), F32, wait=[(s_pr, PREP_DONE)])

    def layernorm(add_xn_to_x):
        p.wait("scalar", dma_x, 16)
        p.wait("vector", dma_x, 16)
        p.op("vector", lambda E: E.memset(LN_SUM.ap(), 0.0))
        p.op("vector", lambda E: E.memset(LN_MEAN.ap(), 0.0), inc=s_vv)
        p.op("vector", lambda E: E.memset(LN_SQ.ap(), 0.0), inc=s_vv)
        p.wait("scalar", s_vv, s_vv.total)
        for t in range(NT):
            p.op("scalar", lambda E, tt=t: E.activation(
                SCR8.ap(), X[:, tt, :], AF.Square,
                accum_out=LN_SQ[:, tt:tt + 1]), inc=s_ev)
            p.op("vector", lambda E, tt=t: E.tensor_scalar(
                XNB.ap().bitcast(F32)[:, 0:512], X[:, tt, 0:512], 1.0, 0.0,
                ALU.mult, ALU.add, accum_out=LN_SUM[:, tt:tt + 1]), inc=s_vv)
            p.op("vector", lambda E, tt=t: E.tensor_scalar(
                XNB.ap().bitcast(F32)[:, 0:512], X[:, tt, 512:1024], 1.0, 0.0,
                ALU.mult, ALU.add, accum_out=LN_MEAN[:, tt:tt + 1]), inc=s_vv)
        p.wait("vector", s_ev, s_ev.total)
        p.op("vector", lambda E: E.tensor_tensor(
            LN_SUM.ap(), LN_SUM.ap(), LN_MEAN.ap(), ALU.add))
        p.emit("vector", lambda E: E.drain())
        p.op("vector", lambda E: E.tensor_scalar(
            LN_MEAN.ap(), LN_SUM.ap(), 1.0 / D, 0.0, ALU.mult, ALU.add))
        p.op("vector", lambda E: E.tensor_scalar(
            LN_SQ.ap(), LN_SQ.ap(), 1.0 / D, 0.0, ALU.mult, ALU.add))
        p.emit("vector", lambda E: E.drain())
        p.op("vector", lambda E: E.tensor_tensor(
            LN_SUM.ap(), LN_MEAN.ap(), LN_MEAN.ap(), ALU.mult))
        p.emit("vector", lambda E: E.drain())
        p.op("vector", lambda E: E.tensor_tensor(
            LN_SQ.ap(), LN_SQ.ap(), LN_SUM.ap(), ALU.subtract))
        p.emit("vector", lambda E: E.drain())
        p.op("vector", lambda E: E.tensor_scalar(
            LN_SQ.ap(), LN_SQ.ap(), LN_EPS, 0.0, ALU.add, ALU.add))
        p.emit("vector", lambda E: E.drain())
        p.op("vector", lambda E: E.reciprocal(LN_SUM.ap(), LN_SQ.ap()),
             inc=s_vv)
        p.wait("scalar", s_vv, s_vv.total)
        p.op("scalar", lambda E: E.activation(LN_RSTD.ap(), LN_SUM.ap(),
                                              AF.Sqrt), inc=s_ev)
        p.wait("vector", s_ev, s_ev.total)
        for t in range(NT):
            if t >= 1:
                p.wait("vector", s_gp, s_gp.total)
                p.wait("vector", s_tr, s_tr.total)
            p.op("vector", lambda E, tt=t: E.tensor_scalar(
                SCR8.ap(), X[:, tt, :], LN_MEAN[:, tt:tt + 1],
                LN_RSTD[:, tt:tt + 1], ALU.subtract, ALU.mult))
            p.op("vector", lambda E: E.tensor_copy(XNB.ap(), SCR8.ap()),
                 inc=s_vv)
            if add_xn_to_x:
                p.wait("gpsimd", s_vv, s_vv.total)
                p.op("gpsimd", lambda E, tt=t: E.tensor_tensor(
                    X[:, tt, :], X[:, tt, :], SCR8.ap(), ALU.add), inc=s_gp)
            else:
                s_gp.inc(0)
            p.wait("tensor", s_vv, s_vv.total)
            for k in range(NK):
                bank = 6 + (k // 4)
                off = (k % 4) * 128
                p.op("tensor", lambda E, kk=k, b=bank, o=off: E.transpose(
                    PS[b].ap().bitcast(BF16)[:, o:o + 128],
                    XNB[:, kk * 128:(kk + 1) * 128], IDB.ap()),
                    inc=(s_mm if k in (3, 7) else None))
            mmv = s_mm.total
            p.wait("vector", s_mm, mmv - 1)
            for k in range(4):
                p.op("vector", lambda E, tt=t, kk=k: E.tensor_copy(
                    XNT[:, kk, tt * 128:(tt + 1) * 128],
                    PS[6].ap().bitcast(BF16)[:, kk * 128:(kk + 1) * 128]))
            p.wait("vector", s_mm, mmv)
            for k in range(4):
                p.op("vector", lambda E, tt=t, kk=k: E.tensor_copy(
                    XNT[:, 4 + kk, tt * 128:(tt + 1) * 128],
                    PS[7].ap().bitcast(BF16)[:, kk * 128:(kk + 1) * 128]),
                    inc=(s_tr if k == 3 else None))
        return s_tr.total, s_gp.total

    XNT_L0, GP_L0 = layernorm(add_xn_to_x=True)
    tap_store("ln_mean", LN_MEAN.ap(), (128, NT), F32, wait=[(s_tr, s_tr.total)])
    tap_store("ln_rstd", LN_RSTD.ap(), (128, NT), F32, wait=[(s_tr, s_tr.total)])
    tap_store("ln_sum", LN_SUM.ap(), (128, NT), F32, wait=[(s_tr, s_tr.total)])
    tap_store("ln_sq", LN_SQ.ap(), (128, NT), F32, wait=[(s_tr, s_tr.total)])
    tap_store("x0", X[:, 0, :], (128, D), F32, wait=[(dma_x, 16)])
    tap_store("xnt", XNT.ap(), (128, NK, T), BF16, wait=[(s_tr, XNT_L0)])


    def stream_mamba_weights(half):
        dma("gpsimd", WIN_H,
            WIN[:, half * 1024:(half + 1) * 1024].rearrange(
                "(k p) m -> p k m", p=128), dma_w)
        dma("gpsimd", WOUT_H,
            WOUT[half * 1024:(half + 1) * 1024, :].rearrange(
                "(k p) m -> p k m", p=128), dma_w)
        return dma_w.total

    tap_xu = tap("xu", (128, NCH, NK, 132), BF16)
    tap_vc = tap("vc", (128, NCH, D), BF16)
    tap_sy = tap("sy", (128, NCH, NK, 128), BF16)

    def mamba_half(half):
        p.wait("gpsimd", s_fb, s_fb.total)
        wv = stream_mamba_weights(half)
        p.wait("vector", s_fb, s_fb.total)
        p.op("vector", lambda E: E.memset(HT[0:DS, 0, :], 0.0))
        p.op("vector", lambda E: E.memset(UC[:, 0, :, 0:4], 0.0))
        p.op("vector", lambda E: E.memset(UC[:, 1, :, 0:4], 0.0), inc=s_vv)
        zs = s_vv.total

        p.wait("tensor", dma_w, wv - 16)
        p.wait("tensor", s_tr, XNT_L0)
        p.wait("tensor", s_pr, PREP_DONE)
        p.wait("tensor", s_vv, zs)
        p.wait("scalar", s_vv, zs)

        cv_done = {}
        for c in range(NCH):
            cur, prv = c % 2, 1 - c % 2
            for mg in range(2):
                bank = mg
                for mi in range(4):
                    m = mg * 4 + mi
                    for k in range(NK):
                        p.op("tensor", lambda E, kk=k, mm=m, b=bank, o=mi * 128, cc=c:
                             E.matmul(PS[b][:, o:o + 128],
                                      WIN_H[:, kk, mm * 128:(mm + 1) * 128],
                                      XNT[:, kk, cc * 128:(cc + 1) * 128],
                                      start=(kk == 0), stop=False))
                    gm0 = (half * NK + m) * 128
                    p.op("tensor", lambda E, b=bank, o=mi * 128, g0=gm0:
                         E.matmul(PS[b][:, o:o + 128],
                                  BINB[0:1, g0:g0 + 128],
                                  ONESR[0:1, 0:128], start=False, stop=True),
                         inc=(s_mm if mi == 3 else None))
            mm_inp = s_mm.total
            if c >= 2:
                p.wait("scalar", s_vv, cv_done[c - 2])
            p.wait("scalar", s_mm, mm_inp - 1)
            p.op("scalar", lambda E, cu=cur: E.copy(
                UC[:, cu, 0:4, 4:132],
                PS[0].ap().rearrange("p (m n) -> p m n", n=128)))
            p.wait("scalar", s_mm, mm_inp)
            p.op("scalar", lambda E, cu=cur: E.copy(
                UC[:, cu, 4:8, 4:132],
                PS[1].ap().rearrange("p (m n) -> p m n", n=128)), inc=s_ev)
            ev_u = s_ev.total
            p.wait("vector", s_ev, ev_u)
            if c > 0:
                p.op("vector", lambda E, cu=cur, pv=prv: E.tensor_copy(
                    UC[:, cu, :, 0:4], UC[:, pv, :, 128:132]))
            for m in range(NK):
                g = half * NK + m
                p.op("vector", lambda E, mm=m, gg=g, cu=cur: E.tensor_scalar(
                    VCF[:, mm, :], UC[:, cu, mm, 1:129], CONVWS[:, gg, 0:1],
                    CONVBS[:, gg:gg + 1], ALU.mult, ALU.add))
            for k in range(1, 4):
                for m in range(NK):
                    g = half * NK + m
                    p.op("vector",
                         lambda E, mm=m, gg=g, kk=k, cu=cur: E.scalar_tensor_tensor(
                             VCF[:, mm, :], UC[:, cu, mm, 1 + kk:129 + kk],
                             CONVWS[:, gg, kk:kk + 1], VCF[:, mm, :],
                             ALU.mult, ALU.add),
                         inc=(s_vv if (m == NK - 1 and k == 3) else None))
            cv_done[c] = s_vv.total
            if tap_xu is not None and half == 0:
                p.wait("sync", s_vv, cv_done[c])
                p.op("sync", lambda E, cc=c, cu=cur: E.dma_start(
                    out=tap_xu[:, cc, :, :], in_=UC[:, cu, :, :]),
                    inc=s_st, inc_n=16)
            p.wait("tensor", s_vv, cv_done[c])
            for m in range(NK):
                bank = 2 + (m // 4)
                off = (m % 4) * 128
                p.op("tensor", lambda E, mm=m, b=bank, o=off: E.transpose(
                    PS[b].ap().bitcast(BF16)[:, o:o + 128],
                    VCF[:, mm, :], IDB.ap()),
                    inc=(s_mm if m in (3, 7) else None))
            mm_tr = s_mm.total
            p.wait("vector", s_mm, mm_tr - 1)
            p.op("vector", lambda E, cu=cur: E.tensor_copy(
                VC[:, cu, 0:512], PS[2].ap().bitcast(BF16)[:, 0:512]))
            p.wait("vector", s_mm, mm_tr)
            p.op("vector", lambda E, cu=cur: E.tensor_copy(
                VC[:, cu, 512:1024], PS[3].ap().bitcast(BF16)[:, 0:512]),
                inc=s_vv)
            vc_ok = s_vv.total
            if tap_vc is not None and half == 0:
                p.wait("sync", s_vv, vc_ok)
                p.op("sync", lambda E, cc=c, cu=cur: E.dma_start(
                    out=tap_vc[:, cc, :], in_=VC[:, cu, :]), inc=s_st, inc_n=16)
            p.wait("tensor", s_vv, vc_ok)
            hcur, hnxt = c % 2, 1 - c % 2
            for m in range(NK):
                bank = 4 + (m // 4)
                off = (m % 4) * 128
                p.op("tensor", lambda E, mm=m, b=bank, o=off, cu=cur: E.matmul(
                    PS[b][:, o:o + 128],
                    VC[:, cu, mm * 128:(mm + 1) * 128], TTOEP.ap(),
                    start=True, stop=False))
                p.op("tensor", lambda E, mm=m, b=bank, o=off, hc=hcur: E.matmul(
                    PS[b][:, o:o + 128],
                    HT[0:DS, hc, mm * 128:(mm + 1) * 128], WCT.ap(),
                    start=False, stop=True),
                    inc=(s_mm if m in (3, 7) else None))
            mm_y = s_mm.total
            for n in range(2):
                p.op("tensor", lambda E, nn=n, hc=hcur: E.matmul(
                    PS[6][0:DS, 0:N512], ATLT.ap(),
                    HT[0:DS, hc, nn * 512:(nn + 1) * 512],
                    start=True, stop=False))
                p.op("tensor", lambda E, nn=n, cu=cur: E.matmul(
                    PS[6][0:DS, 0:N512], WB_M.ap(),
                    VC[:, cu, nn * 512:(nn + 1) * 512],
                    start=False, stop=True), inc=s_mm)
                mm_st = s_mm.total
                p.wait("vector", s_mm, mm_st)
                p.op("vector", lambda E, nn=n, hn=hnxt: E.tensor_copy(
                    HT[0:DS, hn, nn * 512:(nn + 1) * 512],
                    PS[6][0:DS, 0:N512]), inc=s_vv)
                p.wait("tensor", s_vv, s_vv.total)
            p.wait("scalar", s_mm, mm_y - 1)
            p.op("scalar", lambda E: E.activation(
                SYC[:, 0:4, :],
                PS[4].ap().rearrange("p (m n) -> p m n", n=128), AF.Silu))
            p.wait("scalar", s_mm, mm_y)
            p.op("scalar", lambda E: E.activation(
                SYC[:, 4:8, :],
                PS[5].ap().rearrange("p (m n) -> p m n", n=128), AF.Silu),
                inc=s_ev)
            sy_ok = s_ev.total
            if tap_sy is not None and half == 0:
                p.wait("sync", s_ev, sy_ok)
                p.op("sync", lambda E, cc=c: E.dma_start(
                    out=tap_sy[:, cc, :, :], in_=SYC), inc=s_st, inc_n=16)
            p.wait("tensor", s_ev, sy_ok)
            if c == 0:
                p.wait("tensor", dma_w, wv)
            for n in range(2):
                for m in range(NK):
                    last = (m == NK - 1) and (half == 1)
                    p.op("tensor", lambda E, mm=m, nn=n, st_=last: E.matmul(
                        PS[7][:, 0:N512],
                        SYC[:, mm, :], WOUT_H[:, mm, nn * 512:(nn + 1) * 512],
                        start=(mm == 0), stop=st_),
                        inc=(s_mm if last else None))
                if half == 0:
                    p.op("tensor", lambda E, nn=n: E.matmul(
                        PS[7][:, 0:N512], ONESR[0:1, 0:128],
                        BOUTB[0:1, nn * 512:(nn + 1) * 512],
                        start=False, stop=True), inc=s_mm)
                mm_o = s_mm.total
                p.wait("vector", s_mm, mm_o)
                p.op("vector", lambda E, cc=c, nn=n: E.tensor_tensor(
                    X[:, cc, nn * 512:(nn + 1) * 512],
                    X[:, cc, nn * 512:(nn + 1) * 512],
                    PS[7][:, 0:N512], ALU.add), inc=s_fb)
                p.wait("tensor", s_fb, s_fb.total)
        return

    p.wait("vector", s_gp, GP_L0)
    mamba_half(0)
    mamba_half(1)
    MAMBA_DONE_V = s_fb.total

    tap_store("x_after_mamba", X.ap(), (128, NT, D), F32,
              wait=[(s_fb, MAMBA_DONE_V)])

    RW1B = arn_b(0, NK, D)
    LGT = arn_f(16384, T)[0:E0, :]
    EW2RES = arn_b(0, NM, D)
    WBC = arn_b(32768, T)
    EW1R = arn_b(36864, 2, NK, 256)
    WTR = arn_b(45056, T)[0:E0, :]
    EB2RB = arn_b(16384, D)[0:E0, :]

    R1T = BIGT.ap().rearrange("p (m t) -> p m t", m=NK)
    EHW = BIGT.ap().rearrange("p (m t) -> p m t", m=NM)

    def moe_l0(stop=""):
        p.wait("gpsimd", s_fb, s_fb.total)
        dma("gpsimd", RW1B, RW1.rearrange("(k p) m -> p k m", p=128), dma_w)
        w_rw1 = dma_w.total
        p.wait("tensor", dma_w, w_rw1)
        p.wait("tensor", s_fb, s_fb.total)
        p.wait("tensor", s_ev, s_ev.total)
        p.wait("tensor", s_vv, s_vv.total)
        for m8 in range(NK):
            for n in range(4):
                bank = (m8 * 4 + n) % 2
                for k in range(NK):
                    p.op("tensor", lambda E, kk=k, mm=m8, nn=n, b=bank: E.matmul(
                        PS[b][:, 0:N512], RW1B[:, kk, mm * 128:(mm + 1) * 128],
                        XNT[:, kk, nn * 512:(nn + 1) * 512],
                        start=(kk == 0), stop=(kk == NK - 1)),
                        inc=(s_mm if k == NK - 1 else None))
                mmv = s_mm.total
                p.wait("scalar", s_mm, mmv)
                p.op("scalar", lambda E, mm=m8, nn=n, b=bank: E.activation(
                    R1T[:, mm, nn * 512:(nn + 1) * 512], PS[b][:, 0:N512],
                    AF.Silu, bias=RB1S[:, mm:mm + 1]), inc=s_ev)
                p.wait("tensor", s_ev, s_ev.total)
        if stop == "rw1":
            return
        for n in range(4):
            for k in range(NK):
                p.op("tensor", lambda E, kk=k, nn=n: E.matmul(
                    PS[2][0:E0, 0:N512], RW2B[:, kk, :],
                    R1T[:, kk, nn * 512:(nn + 1) * 512],
                    start=(kk == 0), stop=(kk == NK - 1)),
                    inc=(s_mm if k == NK - 1 else None))
            mmv = s_mm.total
            p.wait("scalar", s_mm, mmv)
            p.op("scalar", lambda E, nn=n: E.activation(
                LGT[0:E0, nn * 512:(nn + 1) * 512], PS[2][0:E0, 0:N512],
                AF.Identity, bias=RB2S.ap()), inc=s_ev)
            p.wait("tensor", s_ev, s_ev.total)
        for t in range(NT):
            bank = t % 2
            p.wait("tensor", s_vv, s_vv.total)
            p.op("tensor", lambda E, tt=t, b=bank: E.transpose(
                PS[b][:, 0:E0], LGT[0:E0, tt * 128:(tt + 1) * 128],
                IDF[0:E0, 0:E0]), inc=s_mm)
            mmv = s_mm.total
            p.wait("vector", s_mm, mmv)
            p.op("vector", lambda E, tt=t, b=bank: E.tensor_copy(
                LGTM[:, tt, :], PS[b][:, 0:E0]), inc=s_vv)
        LG_OK = s_vv.total
        if "lgtm" in taps:
            tap_store("lgtm", LGTM.ap(), (128, NT, E0), F32, wait=[(s_vv, LG_OK)])
        if stop == "logits":
            return
        p.op("vector", lambda E: E.reduce_max(
            TOP_M1.ap(), LGTM.ap(), axis=AX.X))
        p.emit("vector", lambda E: E.drain())
        p.op("vector", lambda E: E.tensor_scalar(
            TOP_M1N.ap(), TOP_M1.ap(), -1.0, 0.0, ALU.mult, ALU.add),
            inc=s_vv)
        for t in range(NT):
            p.op("vector", lambda E, tt=t: E.tensor_scalar(
                TOP_MSK[:, tt, :], LGTM[:, tt, :], TOP_M1[:, tt:tt + 1], None,
                ALU.is_ge))
        for t in range(NT):
            p.op("vector", lambda E, tt=t: E.scalar_tensor_tensor(
                TOP_MSK[:, tt, :], TOP_MSK[:, tt, :], -1e30, LGTM[:, tt, :],
                ALU.mult, ALU.add))
        for t in range(NT):
            p.op("vector", lambda E, tt=t: E.reduce_max(
                TOP_M2[:, tt:tt + 1], TOP_MSK[:, tt, :], axis=AX.X),
                inc=(s_vv if t == NT - 1 else None))
        M2_OK = s_vv.total
        p.wait("scalar", s_vv, M2_OK)
        for t in range(NT):
            p.op("scalar", lambda E, tt=t: E.activation(
                TOP_E1[:, tt, :], LGTM[:, tt, :], AF.Exp,
                bias=TOP_M1N[:, tt:tt + 1]), inc=(s_ev if t == NT - 1 else None))
        E1_OK = s_ev.total
        for t in range(NT):
            p.op("vector", lambda E, tt=t: E.tensor_scalar(
                TOP_MSK[:, tt, :], LGTM[:, tt, :], TOP_M2[:, tt:tt + 1], None,
                ALU.is_ge))
        p.wait("vector", s_ev, E1_OK)
        for t in range(NT):
            p.op("vector", lambda E, tt=t: E.tensor_tensor(
                TOP_E1[:, tt, :], TOP_E1[:, tt, :], TOP_MSK[:, tt, :],
                ALU.mult))
        for t in range(NT):
            p.op("vector", lambda E, tt=t: E.reduce_sum(
                TOP_WS[:, tt:tt + 1], TOP_E1[:, tt, :], axis=AX.X))
        p.emit("vector", lambda E: E.drain())
        p.op("vector", lambda E: E.reciprocal(TOP_M1.ap(), TOP_WS.ap()))
        p.emit("vector", lambda E: E.drain())
        for t in range(NT):
            p.op("vector", lambda E, tt=t: E.tensor_scalar(
                WTM[:, tt, :], TOP_E1[:, tt, :], TOP_M1[:, tt:tt + 1], None,
                ALU.mult), inc=(s_vv if t == NT - 1 else None))
        W_OK = s_vv.total
        p.wait("tensor", s_vv, W_OK)
        for t in range(NT):
            bank = t % 2
            p.wait("tensor", s_vv, s_vv.total)
            p.op("tensor", lambda E, tt=t, b=bank: E.transpose(
                PS[b].ap().bitcast(BF16)[0:E0, 0:128], WTM[:, tt, :],
                IDB.ap()), inc=s_mm)
            mmv = s_mm.total
            p.wait("vector", s_mm, mmv)
            p.op("vector", lambda E, tt=t, b=bank: E.tensor_copy(
                WTR[0:E0, tt * 128:(tt + 1) * 128],
                PS[b].ap().bitcast(BF16)[0:E0, 0:128]), inc=s_vv)
        WT_OK = s_vv.total
        if "wtm" in taps:
            tap_store("wtm", WTM.ap(), (128, NT, E0), BF16, wait=[(s_vv, W_OK)])
        if stop == "top2":
            return
        p.wait("gpsimd", s_mm, s_mm.total)
        dma("gpsimd", EB2RB, EB2R, dma_c)
        c_eb2 = dma_c.total
        p.wait("tensor", dma_c, c_eb2)
        p.wait("tensor", s_vv, WT_OK)
        for t in range(NT):
            b0 = 2 + 2 * (t % 2)
            for n in range(2):
                p.op("tensor", lambda E, tt=t, nn=n, b=b0: E.matmul(
                    PS[b + nn][:, 0:N512],
                    WTR[0:E0, tt * 128:(tt + 1) * 128],
                    EB2RB[0:E0, nn * 512:(nn + 1) * 512],
                    start=True, stop=True),
                    inc=(s_mm if n == 1 else None))
            mmv = s_mm.total
            p.wait("vector", s_mm, mmv)
            for n in range(2):
                p.op("vector", lambda E, tt=t, nn=n, b=b0: E.tensor_tensor(
                    X[:, tt, nn * 512:(nn + 1) * 512],
                    X[:, tt, nn * 512:(nn + 1) * 512],
                    PS[b + nn][:, 0:N512], ALU.add),
                    inc=(s_fb if n == 1 else None))
            if t % 2 == 1:
                p.wait("tensor", s_fb, s_fb.total)
        ew2_free_mm = s_mm.total
        ew1_hist = []
        n_exp = 1 if stop == "e1" else E0
        for e in range(n_exp):
            ew2_st = {"j": 0, "waited": False}

            def ew2_chunks(n_c):
                if not ew2_st["waited"]:
                    p.wait("gpsimd", s_mm, ew2_free_mm)
                    if e == 0:
                        p.wait("gpsimd", s_ev, s_ev.total)
                    ew2_st["waited"] = True
                for _ in range(n_c):
                    kk = ew2_st["j"]
                    dma("gpsimd", EW2RES[:, kk, :],
                        EW2[e, kk * 128:(kk + 1) * 128, :], dma_w)
                    ew2_st["j"] = kk + 1

            if stop == "ew2dma":
                ew2_chunks(NM)
                p.wait("sync", dma_w, dma_w.total)
                return
            p.wait("vector", s_vv, WT_OK)
            p.wait("vector", s_fb, s_fb.total)
            p.wait("scalar", s_vv, s_vv.total)
            p.op("scalar", lambda E, ee=e: E.dma_start(
                out=WROW.ap(), in_=WTR[ee:ee + 1, :]), inc=s_ev, inc_n=16)
            wr_ok = s_ev.total
            if stop == "wrow":
                p.wait("sync", s_ev, wr_ok)
                return
            p.wait("tensor", s_ev, wr_ok)
            for n in range(4):
                b = 2 + (n % 2)
                if n >= 2:
                    p.wait("tensor", s_vv, s_vv.total - 1)
                p.op("tensor", lambda E, nn=n, bb=b: E.matmul(
                    PS[bb][:, 0:N512], ONESR[0:1, 0:128],
                    WROW[0:1, nn * 512:(nn + 1) * 512],
                    start=True, stop=True), inc=s_mm)
                mmv = s_mm.total
                p.wait("vector", s_mm, mmv)
                p.op("vector", lambda E, nn=n, bb=b: E.tensor_copy(
                    WBC[:, nn * 512:(nn + 1) * 512], PS[bb][:, 0:N512]),
                    inc=s_vv)
            wbc_ok = s_vv.total
            if stop == "wbc":
                return
            for ht in range(2):
                for j in range(8):
                    gj = len(ew1_hist)
                    ring = gj % 2
                    if gj >= 2:
                        p.wait("gpsimd", s_mm, ew1_hist[gj - 2])
                    dma("gpsimd", EW1R[:, ring, :, :],
                        EW1[e, :, j * 256:(j + 1) * 256].rearrange(
                            "(k p) h -> p k h", p=128), dma_w)
                    w_blk = dma_w.total
                    if ht == 0 and j >= 2:
                        ew2_chunks({2: 3, 3: 3, 4: 3, 5: 3, 6: 2, 7: 2}[j])
                        if j == 7:
                            w_ew2 = dma_w.total
                    p.wait("tensor", dma_w, w_blk)
                    if j == 0:
                        p.wait("tensor", s_vv, wbc_ok)
                    for mi in range(2):
                        m = 2 * j + mi
                        for n in range(2):
                            bank = (2 * m + n) % 2
                            if s_ex.total >= 1:
                                p.wait("tensor", s_ex, s_ex.total - 1)
                            for k in range(NK):
                                p.op("tensor", lambda E, kk=k, rr=ring, nn=n,
                                     b=bank, h0=ht, mo=mi * 128: E.matmul(
                                         PS[b][:, 0:N512],
                                         EW1R[:, rr, kk, mo:mo + 128],
                                         XNT[:, kk,
                                             h0 * 1024 + nn * 512:
                                             h0 * 1024 + (nn + 1) * 512],
                                         start=(kk == 0), stop=(kk == NK - 1)),
                                     inc=(s_mm if k == NK - 1 else None))
                            mmv = s_mm.total
                            p.wait("scalar", s_mm, mmv)
                            p.op("scalar", lambda E, mm=m, nn=n, b=bank, ee=e:
                                 E.activation(
                                     EHW[:, mm, nn * 512:(nn + 1) * 512],
                                     PS[b][:, 0:N512], AF.Silu,
                                     bias=EB1S[:, ee, mm:mm + 1]), inc=s_ex)
                            p.wait("vector", s_ex, s_ex.total)
                            p.op("vector", lambda E, mm=m, nn=n, h0=ht:
                                 E.tensor_tensor(
                                     EHW[:, mm, nn * 512:(nn + 1) * 512],
                                     EHW[:, mm, nn * 512:(nn + 1) * 512],
                                     WBC[:, h0 * 1024 + nn * 512:
                                         h0 * 1024 + (nn + 1) * 512],
                                     ALU.mult), inc=s_vv)
                    ew1_hist.append(s_mm.total)
                if stop == "ew1h0":
                    return
                p.wait("tensor", dma_w, w_ew2)
                p.wait("tensor", s_vv, s_vv.total)
                for tl in range(8):
                    t = ht * 8 + tl
                    b0 = 2 + 2 * (tl % 2)
                    for n in range(2):
                        for k in range(NM):
                            p.op("tensor", lambda E, kk=k, nn=n, tt=tl, b=b0:
                                 E.matmul(
                                     PS[b + nn][:, 0:N512],
                                     EHW[:, kk, tt * 128:(tt + 1) * 128],
                                     EW2RES[:, kk, nn * 512:(nn + 1) * 512],
                                     start=(kk == 0), stop=False))
                        p.items["tensor"].pop()
                        p.op("tensor", lambda E, nn=n, tt=tl, b=b0: E.matmul(
                            PS[b + nn][:, 0:N512],
                            EHW[:, NM - 1, tt * 128:(tt + 1) * 128],
                            EW2RES[:, NM - 1, nn * 512:(nn + 1) * 512],
                            start=False, stop=True),
                            inc=(s_mm if n == 1 else None))
                    mmv = s_mm.total
                    p.wait("vector", s_mm, mmv)
                    for n in range(2):
                        p.op("vector", lambda E, tt=t, nn=n, b=b0: E.tensor_tensor(
                            X[:, tt, nn * 512:(nn + 1) * 512],
                            X[:, tt, nn * 512:(nn + 1) * 512],
                            PS[b + nn][:, 0:N512], ALU.add),
                            inc=(s_fb if n == 1 else None))
                    if tl % 2 == 1:
                        p.wait("tensor", s_fb, s_fb.total)
            ew2_free_mm = s_mm.total
        return

    if stop != "mamba":
        moe_l0(stop)
    MOE_DONE = s_fb.total
    tap_store("x_after_l0", X.ap(), (128, NT, D), F32,
              wait=[(s_fb, MOE_DONE)])

    WDQB = arn_b(0, NK, DC)
    WDKVB = arn_b(512, NK, DC)
    WKRB = arn_b(1024, NK, DHR)
    WUQB = arn_b(1536, NHEAD * DK)[0:DC + 1, :]
    WUKB = arn_b(3584, NHEAD * DK)[0:DC + 1, :]
    WUVB = arn_b(5632, NHEAD * DK)[0:DC + 1, :]
    WQRB = arn_b(7680, NHEAD * DHR)[0:DC + 1, :]
    CQT = arn_b(8192, T)[0:DC + 1, :]
    CKVT = arn_b(12288, T)[0:DC + 1, :]
    KRX = arn_b(16384, T)[0:DHR + 1, :]
    QRX = arn_b(20480, 1024)[0:DHR + 1, :]
    QCB = arn_b(22528, 1024)
    KCB = arn_b(24576, T)
    VCT = arn_b(28672, NT, 128)
    PT = arn_b(32768, 2, 1024)
    QSQ = arn_b(32768, T)
    QSQR = arn_b(36864, T)[0:DHR, :]
    KN2 = arn_b(32768, T)[0:1, :]
    SUMS = arn_f(38912, 1024)[0:1, :]
    RCPSB = arn_b(43008, 1024)[0:1, :]
    RCPB = arn_f(45056, 1024)
    WOB = arn_b(8192, NK, D)
    BORB2 = arn_b(22528, D)[0:1, :]
    KEYR = arn_b(0, 2, NK, H1)
    ESELN = arn_b(4096, D)[0:E1, :]
    SELTB = arn_b(4096, T)[0:E1, :]
    VALB = arn_b(12288, E1, D)
    SELBH = arn_b(8192, 1024)
    ESELTB = arn_b(45056, NK, E1)

    def l1_layer():
        p.wait("vector", s_fb, s_fb.total)
        p.wait("scalar", s_fb, s_fb.total)
        p.wait("tensor", s_fb, s_fb.total)
        xnt_l1, _ = layernorm(add_xn_to_x=False)
        if stop == "ln1":
            return

        p.wait("gpsimd", s_mm, s_mm.total)
        dma("gpsimd", WDQB, WDQ.rearrange("(k p) c -> p k c", p=128), dma_w)
        dma("gpsimd", WDKVB, WDKV.rearrange("(k p) c -> p k c", p=128), dma_w)
        dma("gpsimd", WKRB, WKR.rearrange("(k p) c -> p k c", p=128), dma_w)
        dma("gpsimd", WUQB, WUQ, dma_w)
        dma("gpsimd", WUKB, WUK, dma_w)
        dma("gpsimd", WUVB, WUV, dma_w)
        dma("gpsimd", WQRB, WQR, dma_w)
        w_attn = dma_w.total
        if stop == "attw":
            p.wait("sync", dma_w, w_attn)
            return

        p.wait("tensor", dma_w, w_attn)
        p.wait("tensor", s_tr, xnt_l1)
        p.op("vector", lambda E: E.memset(CQT[DC:DC + 1, :], 1.0))
        p.op("vector", lambda E: E.memset(CKVT[DC:DC + 1, :], 1.0))
        p.op("vector", lambda E: E.memset(KRX[DHR:DHR + 1, :], 1.0), inc=s_vv)
        ones_ok = s_vv.total

        def proj32(dst, lhs_w, bias_ap, nm):
            for n in range(4):
                bank = n % 2
                if s_ex.total >= 1:
                    p.wait("tensor", s_ex, s_ex.total - 1)
                for k in range(NK):
                    p.op("tensor", lambda E, kk=k, nn=n, b=bank, w=lhs_w: E.matmul(
                        PS[b][0:nm, 0:N512], w[:, kk, :],
                        XNT[:, kk, nn * 512:(nn + 1) * 512],
                        start=(kk == 0), stop=(kk == NK - 1)),
                        inc=(s_mm if k == NK - 1 else None))
                mmv = s_mm.total
                p.wait("scalar", s_mm, mmv)
                p.op("scalar", lambda E, nn=n, b=bank, d=dst, ba=bias_ap, m=nm:
                     E.activation(d[0:m, nn * 512:(nn + 1) * 512],
                                  PS[b][0:m, 0:N512], AF.Identity, bias=ba),
                     inc=s_ex)

        proj32(CQT, WDQB, BDQS.ap(), DC)
        proj32(CKVT, WDKVB, BDKVS.ap(), DC)
        proj32(KRX, WKRB, BKRS.ap(), DHR)
        p.wait("tensor", s_ex, s_ex.total)
        p.wait("tensor", s_vv, ones_ok)

        n_heads = 1 if stop == "h1" else NHEAD
        for h in range(n_heads):
            for n in range(4):
                bank = n % 2
                if s_ex.total >= 1:
                    p.wait("tensor", s_ex, s_ex.total - 1)
                p.op("tensor", lambda E, nn=n, b=bank, hh=h: E.matmul(
                    PS[b][:, 0:N512], WUKB[0:DC + 1, hh * 128:(hh + 1) * 128],
                    CKVT[0:DC + 1, nn * 512:(nn + 1) * 512],
                    start=True, stop=True), inc=s_mm)
                mmv = s_mm.total
                p.wait("scalar", s_mm, mmv)
                p.op("scalar", lambda E, nn=n, b=bank: E.copy(
                    KCB[:, nn * 512:(nn + 1) * 512], PS[b][:, 0:N512]),
                    inc=s_ex)
            p.wait("tensor", s_vv, s_vv.total)
            for tg in range(4):
                bank = 6 + (tg % 2)
                if tg >= 2:
                    p.wait("tensor", s_ex, s_ex.total - 1)
                for ti in range(4):
                    tp = tg * 4 + ti
                    p.op("tensor", lambda E, t_=tp, b=bank, o=ti * 128, hh=h:
                         E.matmul(PS[b][:, o:o + 128],
                                  CKVT[0:DC + 1, t_ * 128:(t_ + 1) * 128],
                                  WUVB[0:DC + 1, hh * 128:(hh + 1) * 128],
                                  start=True, stop=True),
                         inc=(s_mm if ti == 3 else None))
                mmv = s_mm.total
                p.wait("scalar", s_mm, mmv)
                p.op("scalar", lambda E, tg_=tg, b=bank: E.copy(
                    VCT[:, tg_ * 4:(tg_ + 1) * 4, :],
                    PS[b].ap().rearrange("p (a q) -> p a q", q=128)),
                    inc=s_ex)
            p.wait("vector", s_ex, s_ex.total)
            p.op("vector", lambda E: E.tensor_tensor(
                QSQ, KCB, KCB, ALU.mult))
            p.op("vector", lambda E: E.tensor_tensor(
                QSQR[0:DHR, :], KRX[0:DHR, :], KRX[0:DHR, :], ALU.mult),
                inc=s_vv)
            p.wait("tensor", s_vv, s_vv.total)
            p.wait("tensor", s_ex, s_ex.total)
            for n in range(4):
                kb = 4 + (n % 2)
                if n >= 2:
                    p.wait("tensor", s_vv, s_vv.total - 1)
                p.op("tensor", lambda E, nn=n, b=kb: E.matmul(
                    PS[b][0:1, 0:N512], ONESC.ap(),
                    QSQ[:, nn * 512:(nn + 1) * 512], start=True, stop=False))
                p.op("tensor", lambda E, nn=n, b=kb: E.matmul(
                    PS[b][0:1, 0:N512], ONESC[0:DHR, 0:1],
                    QSQR[0:DHR, nn * 512:(nn + 1) * 512],
                    start=False, stop=True), inc=s_mm)
                mmv = s_mm.total
                p.wait("vector", s_mm, mmv)
                p.op("vector", lambda E, nn=n, b=kb: E.tensor_copy(
                    KN2[0:1, nn * 512:(nn + 1) * 512], PS[b][0:1, 0:N512]),
                    inc=s_vv)
            p.emit("vector", lambda E: E.drain())
            p.op("vector", lambda E: E.reduce_max(
                KMAX[0:1, 0:1], KN2[0:1, :], axis=AX.X))
            p.emit("vector", lambda E: E.drain())
            p.op("vector", lambda E: E.tensor_scalar(
                KMAX[0:1, 1:2], KMAX[0:1, 0:1], 0.5, 0.0, ALU.mult, ALU.add),
                inc=s_vv)
            p.emit("vector", lambda E: E.drain())
            kmax_ok = s_vv.total

            for half in range(2):
                toff = half * 1024
                for n in range(2):
                    bank = n % 2
                    p.wait("tensor", s_ex, s_ex.total - 1)
                    p.op("tensor", lambda E, nn=n, b=bank, hh=h, to=toff: E.matmul(
                        PS[b][:, 0:N512],
                        WUQB[0:DC + 1, hh * 128:(hh + 1) * 128],
                        CQT[0:DC + 1, to + nn * 512:to + (nn + 1) * 512],
                        start=True, stop=True), inc=s_mm)
                    mmv = s_mm.total
                    p.wait("scalar", s_mm, mmv)
                    p.op("scalar", lambda E, nn=n, b=bank: E.copy(
                        QCB[:, nn * 512:(nn + 1) * 512], PS[b][:, 0:N512]),
                        inc=s_ex)
                for n in range(2):
                    bank = n % 2
                    p.wait("tensor", s_ex, s_ex.total - 1)
                    p.op("tensor", lambda E, nn=n, b=bank, hh=h, to=toff: E.matmul(
                        PS[b][0:DHR, 0:N512],
                        WQRB[0:DC + 1, hh * 32:(hh + 1) * 32],
                        CQT[0:DC + 1, to + nn * 512:to + (nn + 1) * 512],
                        start=True, stop=True), inc=s_mm)
                    mmv = s_mm.total
                    p.wait("scalar", s_mm, mmv)
                    p.op("scalar", lambda E, nn=n, b=bank: E.copy(
                        QRX[0:DHR, nn * 512:(nn + 1) * 512],
                        PS[b][0:DHR, 0:N512]), inc=s_ex)
                p.wait("vector", s_ex, s_ex.total)
                p.op("vector", lambda E: E.tensor_tensor(
                    QSQ[:, 0:1024], QCB, QCB, ALU.mult))
                p.op("vector", lambda E: E.tensor_tensor(
                    QSQR[0:DHR, 0:1024], QRX[0:DHR, :], QRX[0:DHR, :],
                    ALU.mult), inc=s_vv)
                p.wait("tensor", s_vv, s_vv.total)
                for n in range(2):
                    qb = 4 + (n % 2)
                    p.op("tensor", lambda E, nn=n, b=qb: E.matmul(
                        PS[b][0:1, 0:N512], ONESC.ap(),
                        QSQ[:, nn * 512:(nn + 1) * 512], start=True, stop=False))
                    p.op("tensor", lambda E, nn=n, b=qb: E.matmul(
                        PS[b][0:1, 0:N512], ONESC[0:DHR, 0:1],
                        QSQR[0:DHR, nn * 512:(nn + 1) * 512],
                        start=False, stop=True), inc=s_mm)
                    mmv = s_mm.total
                    p.wait("scalar", s_mm, mmv)
                    p.op("scalar", lambda E, nn=n, b=qb: E.copy(
                        RCPSB[0:1, nn * 512:(nn + 1) * 512],
                        PS[b][0:1, 0:N512]), inc=s_ex)
                p.wait("vector", s_ex, s_ex.total)
                p.wait("vector", s_vv, kmax_ok)
                p.op("vector", lambda E: E.tensor_scalar(
                    QRX[DHR:DHR + 1, :], RCPSB[0:1, :], -0.5,
                    KMAX[0:1, 1:2], ALU.mult, ALU.subtract), inc=s_vv)
                qrx_ok = s_vv.total

                p.wait("tensor", s_vv, qrx_ok)
                for tp in range(NT):
                    for n in range(2):
                        g = tp * 2 + n
                        bank = g % 2
                        if s_ex.total >= 1:
                            p.wait("tensor", s_ex, s_ex.total - 1)
                        p.op("tensor", lambda E, t_=tp, nn=n, b=bank: E.matmul(
                            PS[b][:, 0:N512], KCB[:, t_ * 128:(t_ + 1) * 128],
                            QCB[:, nn * 512:(nn + 1) * 512],
                            start=True, stop=False))
                        p.op("tensor", lambda E, t_=tp, nn=n, b=bank: E.matmul(
                            PS[b][:, 0:N512],
                            KRX[0:DHR + 1, t_ * 128:(t_ + 1) * 128],
                            QRX[0:DHR + 1, nn * 512:(nn + 1) * 512],
                            start=False, stop=True), inc=s_mm)
                        mmv = s_mm.total
                        p.wait("scalar", s_mm, mmv)
                        p.op("scalar", lambda E, t_=tp, nn=n, b=bank: E.activation(
                            PT[:, t_ % 2, nn * 512:(nn + 1) * 512],
                            PS[b][:, 0:N512], AF.Exp, scale=SCALE), inc=s_ex)
                    p.wait("tensor", s_ex, s_ex.total)
                    for n in range(2):
                        p.op("tensor", lambda E, t_=tp, nn=n: E.matmul(
                            PS[2 + nn][:, 0:N512], VCT[:, t_, :],
                            PT[:, t_ % 2, nn * 512:(nn + 1) * 512],
                            start=(t_ == 0), stop=(t_ == NT - 1)))
                        p.op("tensor", lambda E, t_=tp, nn=n: E.matmul(
                            PS[4 + nn][0:1, 0:N512], ONESC.ap(),
                            PT[:, t_ % 2, nn * 512:(nn + 1) * 512],
                            start=(t_ == 0), stop=(t_ == NT - 1)),
                            inc=(s_mm if (tp == NT - 1 and n == 1) else None))
                mmv = s_mm.total
                p.wait("vector", s_mm, mmv)
                p.op("vector", lambda E: E.tensor_copy(SUMS[0:1, 0:512],
                                                       PS[4][0:1, 0:N512]))
                p.op("vector", lambda E: E.tensor_copy(SUMS[0:1, 512:1024],
                                                       PS[5][0:1, 0:N512]))
                p.emit("vector", lambda E: E.drain())
                def _recip(E):
                    with nc.allow_low_precision(
                            reason="softmax 1/sum broadcast operand in bf16; "
                                   "P and V operands are bf16 anyway"):
                        return E.reciprocal(RCPSB[0:1, :], SUMS[0:1, :])
                p.op("vector", _recip, inc=s_vv)
                rs_ok = s_vv.total
                p.wait("tensor", s_vv, rs_ok)
                for n in range(2):
                    p.op("tensor", lambda E, nn=n: E.matmul(
                        PS[6 + nn][:, 0:N512], ONESR[0:1, 0:128],
                        RCPSB[0:1, nn * 512:(nn + 1) * 512],
                        start=True, stop=True),
                        inc=(s_mm if n == 1 else None))
                mmv = s_mm.total
                p.wait("vector", s_mm, mmv)
                for n in range(2):
                    p.op("vector", lambda E, nn=n: E.tensor_copy(
                        RCPB[:, nn * 512:(nn + 1) * 512],
                        PS[6 + nn][:, 0:N512]), inc=s_vv)
                p.emit("vector", lambda E: E.drain())
                for n in range(2):
                    p.op("vector", lambda E, nn=n, hh=h, to=toff: E.tensor_tensor(
                        OT[:, hh, to + nn * 512:to + (nn + 1) * 512],
                        PS[2 + nn][:, 0:N512],
                        RCPB[:, nn * 512:(nn + 1) * 512], ALU.mult),
                        inc=s_fb)
                p.wait("tensor", s_fb, s_fb.total)
        if stop in ("h1", "attn"):
            return
        p.wait("gpsimd", s_mm, s_mm.total)
        dma("gpsimd", WOB, WO.rearrange("(k p) d -> p k d", p=128), dma_w)
        dma("gpsimd", BORB2, BOR, dma_w)
        w_wo = dma_w.total
        p.wait("tensor", dma_w, w_wo)
        for t in range(NT):
            b0 = 2 + 2 * (t % 2)
            for n in range(2):
                for k in range(NK):
                    p.op("tensor", lambda E, kk=k, nn=n, tt=t, b=b0: E.matmul(
                        PS[b + nn][:, 0:N512], OT[:, kk, tt * 128:(tt + 1) * 128],
                        WOB[:, kk, nn * 512:(nn + 1) * 512],
                        start=(kk == 0), stop=False))
                p.op("tensor", lambda E, nn=n, b=b0: E.matmul(
                    PS[b + nn][:, 0:N512], ONESR[0:1, 0:128],
                    BORB2[0:1, nn * 512:(nn + 1) * 512],
                    start=False, stop=True),
                    inc=(s_mm if n == 1 else None))
            mmv = s_mm.total
            p.wait("vector", s_mm, mmv)
            for n in range(2):
                p.op("vector", lambda E, tt=t, nn=n, b=b0: E.tensor_tensor(
                    X[:, tt, nn * 512:(nn + 1) * 512],
                    X[:, tt, nn * 512:(nn + 1) * 512],
                    PS[b + nn][:, 0:N512], ALU.add),
                    inc=(s_fb if n == 1 else None))
            if t % 2 == 1:
                p.wait("tensor", s_fb, s_fb.total)
        WO_DONE = s_fb.total
        if stop == "wo":
            return
        p.wait("gpsimd", s_mm, s_mm.total)
        dma("gpsimd", ESELN, ESEL, dma_w)
        w_esel = dma_w.total
        val_st = {"j": 0}

        def val_chunks(n_c):
            for _ in range(n_c):
                kk = val_st["j"]
                dma("gpsimd", VALB[:, 2 * kk:2 * kk + 2, :],
                    VALUES.rearrange("e p v -> p e v")[:, 2 * kk:2 * kk + 2, :],
                    dma_w)
                val_st["j"] = kk + 1

        if stop == "ffnw":
            val_chunks(8)
            p.wait("sync", dma_w, dma_w.total)
            return
        p.wait("tensor", dma_w, w_esel)
        p.wait("tensor", s_fb, WO_DONE)
        for k in range(NK):
            bank = k % 2
            p.wait("tensor", s_vv, s_vv.total)
            p.op("tensor", lambda E, kk=k, b=bank: E.transpose(
                PS[b].ap().bitcast(BF16)[:, 0:E1],
                ESELN[0:E1, kk * 128:(kk + 1) * 128], IDB[0:E1, 0:E1]),
                inc=s_mm)
            mmv = s_mm.total
            p.wait("vector", s_mm, mmv)
            p.op("vector", lambda E, kk=k, b=bank: E.tensor_copy(
                ESELTB[:, kk, :], PS[b].ap().bitcast(BF16)[:, 0:E1]),
                inc=s_vv)
        es_ok = s_vv.total
        if stop == "eselt":
            return
        p.wait("tensor", s_vv, es_ok)
        for n in range(4):
            bank = n % 2
            if s_ex.total >= 1:
                p.wait("tensor", s_ex, s_ex.total - 1)
            for k in range(NK):
                p.op("tensor", lambda E, kk=k, nn=n, b=bank: E.matmul(
                    PS[b][0:E1, 0:N512], ESELTB[:, kk, :],
                    XNT[:, kk, nn * 512:(nn + 1) * 512],
                    start=(kk == 0), stop=(kk == NK - 1)),
                    inc=(s_mm if k == NK - 1 else None))
            mmv = s_mm.total
            p.wait("scalar", s_mm, mmv)
            p.op("scalar", lambda E, nn=n, b=bank: E.activation(
                SELTB[0:E1, nn * 512:(nn + 1) * 512], PS[b][0:E1, 0:N512],
                AF.Sigmoid), inc=s_ex)
        SW = BIGT.ap().rearrange("p (e t) -> p e t", e=E1)
        key_hist = []
        if stop == "sel":
            return
        for ht in range(2):
            toff = ht * 1024
            for e in range(E1):
                gk = len(key_hist)
                ring = gk % 2
                if gk >= 2:
                    p.wait("gpsimd", s_mm, key_hist[gk - 2])
                dma("gpsimd", KEYR[:, ring, :, :],
                    KEYS[e, :, :].rearrange("(k p) hh -> p k hh", p=128), dma_w)
                w_blk = dma_w.total
                if ht == 0 and e < 8:
                    val_chunks(1)
                    if e == 7:
                        w_val = dma_w.total
                p.wait("scalar", s_ex, s_ex.total)
                p.op("scalar", lambda E, ee=e: E.dma_start(
                    out=WROW[0:1, 0:T], in_=SELTB[ee:ee + 1, :]),
                    inc=s_ev, inc_n=16)
                sr_ok = s_ev.total
                p.wait("tensor", s_ev, sr_ok)
                p.wait("tensor", s_vv, s_vv.total)
                for n in range(2):
                    p.op("tensor", lambda E, nn=n, to=toff: E.matmul(
                        PS[6 + nn][:, 0:N512], ONESR[0:1, 0:128],
                        WROW[0:1, to + nn * 512:to + (nn + 1) * 512],
                        start=True, stop=True),
                        inc=(s_mm if n == 1 else None))
                mmv = s_mm.total
                p.wait("vector", s_mm, mmv)
                for n in range(2):
                    p.op("vector", lambda E, nn=n: E.tensor_copy(
                        SELBH[:, nn * 512:(nn + 1) * 512],
                        PS[6 + nn][:, 0:N512]), inc=(s_vv if n == 1 else None))
                selb_ok = s_vv.total
                p.wait("tensor", dma_w, w_blk)
                for n in range(2):
                    g = e * 2 + n
                    bank = g % 2
                    if s_ex.total >= 1:
                        p.wait("tensor", s_ex, s_ex.total - 1)
                    for k in range(NK):
                        p.op("tensor", lambda E, kk=k, rr=ring, nn=n, b=bank,
                             to=toff: E.matmul(
                                 PS[b][:, 0:N512], KEYR[:, rr, kk, :],
                                 XNT[:, kk, to + nn * 512:to + (nn + 1) * 512],
                                 start=(kk == 0), stop=(kk == NK - 1)),
                             inc=(s_mm if k == NK - 1 else None))
                    mmv = s_mm.total
                    p.wait("scalar", s_mm, mmv)
                    p.op("scalar", lambda E, ee=e, nn=n, b=bank: E.activation(
                        SW[:, ee, nn * 512:(nn + 1) * 512], PS[b][:, 0:N512],
                        AF.Relu), inc=s_ex)
                    p.wait("vector", s_ex, s_ex.total)
                    p.wait("vector", s_vv, selb_ok)
                    p.op("vector", lambda E, ee=e, nn=n: E.tensor_tensor(
                        SW[:, ee, nn * 512:(nn + 1) * 512],
                        SW[:, ee, nn * 512:(nn + 1) * 512],
                        SELBH[:, nn * 512:(nn + 1) * 512], ALU.mult),
                        inc=s_vv)
                key_hist.append(s_mm.total)
            if stop == "keysh0":
                return
            p.wait("tensor", s_vv, s_vv.total)
            p.wait("tensor", dma_w, w_val)
            for tl in range(8):
                t = ht * 8 + tl
                b0 = 2 + 2 * (tl % 2)
                for n in range(2):
                    for e in range(E1):
                        p.op("tensor", lambda E, ee=e, nn=n, tt=tl, b=b0:
                             E.matmul(
                                 PS[b + nn][:, 0:N512],
                                 SW[:, ee, tt * 128:(tt + 1) * 128],
                                 VALB[:, ee, nn * 512:(nn + 1) * 512],
                                 start=(ee == 0), stop=(ee == E1 - 1)),
                             inc=(s_mm if (e == E1 - 1 and n == 1) else None))
                mmv = s_mm.total
                p.wait("vector", s_mm, mmv)
                for n in range(2):
                    p.op("vector", lambda E, tt=t, nn=n, b=b0: E.tensor_tensor(
                        X[:, tt, nn * 512:(nn + 1) * 512],
                        X[:, tt, nn * 512:(nn + 1) * 512],
                        PS[b + nn][:, 0:N512], ALU.add),
                        inc=(s_fb if n == 1 else None))
                fbv = s_fb.total
                if tl % 2 == 1:
                    p.wait("tensor", s_fb, fbv)
                if stop == "valh0" and tl == 7:
                    return
                p.wait("sync", s_fb, fbv)
                p.op("sync", lambda E, tt=t: E.dma_start(
                    out=OUT.rearrange("(n p) d -> p n d", p=128)[:, tt, :],
                    in_=X[:, tt, :]), inc=s_st, inc_n=16)
        return

    OT = BIGT.ap().rearrange("p (hh t) -> p hh t", hh=NHEAD)
    L0_STOPS = ("mamba", "rw1", "logits", "top2", "wbc", "ew1h0", "e1",
                "ew2dma", "wrow")
    if stop not in L0_STOPS:
        l1_layer()
    L1_DONE = s_st.total

    if stop != "":
        p.wait("sync", s_fb, s_fb.total)
        p.wait("sync", s_gp, s_gp.total)
        p.wait("sync", s_vv, s_vv.total)
        p.wait("sync", s_ex, s_ex.total)
        p.op("sync", lambda E: E.dma_start(
            out=OUT.rearrange("(n p) d -> p n d", p=128), in_=X.ap()),
            inc=s_st, inc_n=16)
    p.wait("sync", s_st, s_st.total)

    import contextlib
    with contextlib.ExitStack() as stack:
        for s in p.sems.values():
            s.h = stack.enter_context(nc.semaphore(s.name))
        block = stack.enter_context(nc.Block())
        for name in Plan.ENGINES:
            getattr(block, name)(
                (lambda nm: lambda E: [fn(E) for fn in p.items[nm]])(name))
    return nc, tap_outs


def _prep_inputs(inputs, core):
    g = inputs
    d = {}
    d["x"] = np.ascontiguousarray(g["L0_attn_x"][core])
    d["idm"] = np.eye(128, dtype=np.float32)
    d["jm"] = np.eye(128, dtype=np.float32)[::-1].copy()
    d["win"] = g["L0_attn_W_in"]
    d["binT"] = np.ascontiguousarray(g["L0_attn_b_in"].reshape(1, DI))
    d["convw"] = np.ascontiguousarray(
        g["L0_attn_conv_w"][:, 0, :].reshape(NM, 128, 4).transpose(1, 0, 2))
    d["convb"] = np.ascontiguousarray(g["L0_attn_conv_b"].reshape(NM, 128).T)
    d["amat"] = g["L0_attn_A"]
    d["bp"] = g["L0_attn_Bp"].reshape(DS, 1)
    d["cp"] = g["L0_attn_Cp"].reshape(DS, 1)
    d["dp"] = g["L0_attn_Dp"].reshape(1, 1)
    d["wout"] = g["L0_attn_W_out"]
    d["boutr"] = g["L0_attn_b_out"].reshape(1, D)
    d["rw1"] = g["L0_ffn_rw1"]
    d["rb1c"] = np.ascontiguousarray(g["L0_ffn_rb1"].reshape(NK, 128).T)
    d["rw2"] = g["L0_ffn_rw2"]
    d["rb2c"] = g["L0_ffn_rb2"].reshape(E0, 1)
    d["ew1"] = g["L0_ffn_ew1"]
    d["eb1c"] = np.ascontiguousarray(
        g["L0_ffn_eb1"].reshape(E0, NM, 128).transpose(2, 0, 1))
    d["ew2"] = g["L0_ffn_ew2"]
    d["eb2r"] = g["L0_ffn_eb2"]
    d["wdq"] = g["L1_attn_W_DQ"]
    d["bdqc"] = g["L1_attn_b_DQ"].reshape(DC, 1)
    d["wuq"] = np.vstack([g["L1_attn_W_UQ"], g["L1_attn_b_UQ"][None]])
    d["wdkv"] = g["L1_attn_W_DKV"]
    d["bdkvc"] = g["L1_attn_b_DKV"].reshape(DC, 1)
    d["wuk"] = np.vstack([g["L1_attn_W_UK"], g["L1_attn_b_UK"][None]])
    d["wuv"] = np.vstack([g["L1_attn_W_UV"], g["L1_attn_b_UV"][None]])
    d["wqr"] = np.vstack([g["L1_attn_W_QR"], g["L1_attn_b_QR"][None]])
    d["wkr"] = g["L1_attn_W_KR"]
    d["bkrc"] = g["L1_attn_b_KR"].reshape(DHR, 1)
    d["wo"] = g["L1_attn_W_O"]
    d["bor"] = g["L1_attn_b_O"].reshape(1, D)
    d["esel"] = g["L1_ffn_expert_sel"]
    d["keys"] = g["L1_ffn_keys"]
    d["values"] = g["L1_ffn_values"]
    return {k: np.ascontiguousarray(np.asarray(v).astype(np.float32, copy=False))
            for k, v in d.items()}


def kernel(**inputs):
    nc, _ = build_nc()
    in_maps = [_prep_inputs(inputs, c) for c in range(8)]
    res = run_bass_kernel_spmd(nc, in_maps, core_ids=list(range(8)))
    return np.stack([r["out"] for r in res.results], axis=0)


# version 10
# speedup vs baseline: 1.5666x; 1.1636x over previous
import os
import numpy as np
import concourse.bass as bass
import concourse.mybir as mybir
from concourse.bass_utils import run_bass_kernel_spmd

F32 = mybir.dt.float32
BF16 = mybir.dt.bfloat16
AF = mybir.ActivationFunctionType
ALU = mybir.AluOpType
AX = mybir.AxisListType

T = 2048
NT = 16
D = 1024
NK = 8
DI = 2048
NM = 16
DS = 16
CH = 128
NCH = 16
E0, H0 = 8, 2048
E1, H1 = 16, 128
NHEAD, DK, DC, DHR = 8, 128, 32, 32
SCALE = 1.0 / float(np.sqrt(DK + DHR))
LN_EPS = 1e-5
N512 = 512

ARENA_BYTES = 49152


class Sem:
    def __init__(self, name):
        self.name = name
        self.total = 0
        self.h = None

    def inc(self, n=1):
        self.total += n
        return self.total


class Plan:
    ENGINES = ("tensor", "vector", "scalar", "gpsimd", "sync")

    def __init__(self):
        self.items = {e: [] for e in self.ENGINES}
        self.sems = {}

    def sem(self, name):
        s = self.sems.get(name)
        if s is None:
            s = self.sems[name] = Sem(name)
        return s

    def emit(self, eng, fn):
        self.items[eng].append(fn)

    def wait(self, eng, sem, val=None):
        v = sem.total if val is None else val
        if v > 0:
            self.emit(eng, lambda E, s=sem, vv=v: E.wait_ge(s.h, vv))

    def op(self, eng, fn, inc=None, inc_n=1):
        if inc is not None:
            inc.inc(inc_n)
            self.emit(eng, lambda E, f=fn, s=inc, n=inc_n: f(E).then_inc(s.h, n))
        else:
            self.emit(eng, fn)


def build_nc(debug_taps=(), stop=None):
    stop = stop if stop is not None else os.environ.get("MK_STOP", "")
    nc = bass.Bass("TRN2", target_bir_lowering=False, debug=False)
    p = Plan()
    taps = set(debug_taps)

    def din(name, shape):
        return nc.dram_tensor(name, list(shape), F32, kind="ExternalInput").ap()

    X_hbm = din("x", (T, D))
    ID_hbm = din("idm", (128, 128))
    J_hbm = din("jm", (128, 128))
    WIN = din("win", (D, DI))
    BINT = din("binT", (1, DI))
    CONVW = din("convw", (128, NM, 4))
    CONVB = din("convb", (128, NM))
    A_in = din("amat", (DS, DS))
    BP = din("bp", (DS, 1))
    CP = din("cp", (DS, 1))
    DP = din("dp", (1, 1))
    WOUT = din("wout", (DI, D))
    BOUT = din("boutr", (1, D))
    RW1 = din("rw1", (D, D))
    RB1C = din("rb1c", (128, NK))
    RW2 = din("rw2", (D, E0))
    RB2C = din("rb2c", (E0, 1))
    EW1 = din("ew1", (E0, D, H0))
    EB1C = din("eb1c", (128, E0, NM))
    EW2 = din("ew2", (E0, H0, D))
    EB2R = din("eb2r", (E0, D))
    WDQ = din("wdq", (D, DC))
    BDQC = din("bdqc", (DC, 1))
    WUQ = din("wuq", (DC + 1, NHEAD * DK))
    WDKV = din("wdkv", (D, DC))
    BDKVC = din("bdkvc", (DC, 1))
    WUK = din("wuk", (DC + 1, NHEAD * DK))
    WUV = din("wuv", (DC + 1, NHEAD * DK))
    WQR = din("wqr", (DC + 1, NHEAD * DHR))
    WKR = din("wkr", (D, DHR))
    BKRC = din("bkrc", (DHR, 1))
    WO = din("wo", (NHEAD * DK, D))
    BOR = din("bor", (1, D))
    ESEL = din("esel", (E1, D))
    KEYS = din("keys", (E1, D, H1))
    VALUES = din("values", (E1, H1, D))

    OUT = nc.dram_tensor("out", [T, D], F32, kind="ExternalOutput").ap()

    tap_outs = {}

    def tap(name, shape, dtype=F32):
        if name in taps:
            tap_outs[name] = nc.dram_tensor(
                "tap_" + name, list(shape), dtype, kind="ExternalOutput").ap()
            return tap_outs[name]
        return None

    sb = nc.alloc_sbuf_tensor

    X = sb("X", [128, NT, D], F32)
    XNT = sb("XNT", [128, NK, T], BF16)
    BIGT = sb("BIG", [128, 16384], BF16)
    ARENA = sb("ARENA", [128, ARENA_BYTES // 2], BF16)

    ARB = ARENA.ap()
    ARF = ARENA.ap().bitcast(F32)

    def arn_b(off_bytes, *shape):
        n = int(np.prod(shape))
        assert off_bytes % 2 == 0 and off_bytes + 2 * n <= ARENA_BYTES, (off_bytes, shape)
        v = ARB[:, off_bytes // 2: off_bytes // 2 + n]
        if len(shape) > 1:
            dims = " ".join(f"d{i}" for i in range(len(shape)))
            v = v.rearrange(f"p ({dims}) -> p {dims}",
                            **{f"d{i}": s for i, s in enumerate(shape)})
        return v

    def arn_f(off_bytes, *shape):
        n = int(np.prod(shape))
        assert off_bytes % 4 == 0 and off_bytes + 4 * n <= ARENA_BYTES, (off_bytes, shape)
        v = ARF[:, off_bytes // 4: off_bytes // 4 + n]
        if len(shape) > 1:
            dims = " ".join(f"d{i}" for i in range(len(shape)))
            v = v.rearrange(f"p ({dims}) -> p {dims}",
                            **{f"d{i}": s for i, s in enumerate(shape)})
        return v

    IDB = sb("IDB", [128, 128], BF16)
    IDF = sb("IDF", [128, 128], F32)
    JB = sb("JB", [128, 128], BF16)
    ONESC = sb("ONESC", [128, 1], BF16)
    ONESR = sb("ONESR", [1, N512], BF16)

    LN_SUM = sb("LN_SUM", [128, NT], F32)
    LN_SQ = sb("LN_SQ", [128, NT], F32)
    LN_MEAN = sb("LN_MEAN", [128, NT], F32)
    LN_RSTD = sb("LN_RSTD", [128, NT], F32)
    SCR8 = sb("SCR8", [128, D], F32)
    XNB = sb("XNB", [128, D], BF16)

    PQH = sb("PQH", [DS, 16 * DS], BF16)
    AMS = sb("AMS", [DS, DS], F32)
    ATS = sb("ATS", [DS, DS], F32)
    ATB = sb("ATB", [DS, DS], BF16)
    BPC = sb("BPC", [DS, 1], BF16)
    CPC = sb("CPC", [DS, 1], BF16)
    DPS = sb("DPS", [1, 1], F32)
    BT_CH = sb("BT_CH", [DS, 128], BF16)
    CT_CH = sb("CT_CH", [DS, 128], BF16)
    BTT = sb("BTT", [128, DS], BF16)
    WB_M = sb("WB_M", [128, DS], BF16)
    WCT = sb("WCT", [DS, 128], BF16)
    ATLT = sb("ATLT", [DS, DS], BF16)
    KROW = sb("KROW", [1, 128], F32)
    KROWB = sb("KROWB", [1, 128], BF16)
    TTOEP = sb("TTOEP", [128, 128], BF16)
    HT = sb("HT", [DS, 2, D], BF16)

    BINB = sb("BINB", [1, DI], BF16)
    CONVWS = sb("CONVWS", [128, NM, 4], F32)
    CONVBS = sb("CONVBS", [128, NM], F32)
    RB1S = sb("RB1S", [128, NK], F32)
    RW2B = sb("RW2B", [128, NK, E0], BF16)
    RB2S = sb("RB2S", [E0, 1], F32)
    LGTM = sb("LGTM", [128, NT, E0], F32)
    TOP_M1 = sb("TOP_M1", [128, NT], F32)
    TOP_M1N = sb("TOP_M1N", [128, NT], F32)
    TOP_M2 = sb("TOP_M2", [128, NT], F32)
    TOP_E1 = sb("TOP_E1", [128, NT, E0], F32)
    TOP_SCR = sb("TOP_SCR", [128, E0], F32)
    TOP_WS = sb("TOP_WS", [128, NT], F32)
    WTM = sb("WTM", [128, NT, E0], BF16)
    EB1S = sb("EB1S", [128, E0, NM], F32)

    TOP_MSK = sb("TOP_MSK", [128, NT, E0], F32)
    WROW = sb("WROW", [1, T], BF16)

    BDQS = sb("BDQS", [DC, 1], F32)
    BDKVS = sb("BDKVS", [DC, 1], F32)
    BKRS = sb("BKRS", [DHR, 1], F32)
    KMAX = sb("KMAX", [1, 4], F32)
    IND33 = sb("IND33", [1, 64], BF16)

    WIN_H = arn_b(0, NK, D)
    WOUT_H = arn_b(16384, NK, D)
    UC = arn_b(32768, 2, NK, 132)
    VCF = arn_b(36992, NK, 128)
    VC = arn_b(39040, 2, D)
    SYC = arn_b(43136, NK, 128)
    BOUTB = arn_b(45184, D)[0:1, :]

    PS = [nc.alloc_psum_tensor(f"PS{i}", [128, N512], F32) for i in range(8)]

    dma_x = p.sem("dma_x")
    dma_c = p.sem("dma_c")
    dma_w = p.sem("dma_w")
    s_mm = p.sem("s_mm")
    s_ev = p.sem("s_ev")
    s_vv = p.sem("s_vv")
    s_pr = p.sem("s_pr")
    s_tr = p.sem("s_tr")
    s_st = p.sem("s_st")
    s_gp = p.sem("s_gp")
    s_fb = p.sem("s_fb")
    s_ex = p.sem("s_ex")

    def dma(eng, out_ap, in_ap, sem):
        p.op(eng, lambda E, o=out_ap, i=in_ap: E.dma_start(out=o, in_=i),
             inc=sem, inc_n=16)

    dma("sync", X.ap(), X_hbm.rearrange("(n p) d -> p n d", p=128), dma_x)

    dma("sync", IDF.ap(), ID_hbm, dma_c)
    dma("gpsimd", IDB.ap(), ID_hbm, dma_c)
    dma("gpsimd", JB.ap(), J_hbm, dma_c)
    dma("sync", CONVWS.ap(), CONVW, dma_c)
    dma("sync", CONVBS.ap(), CONVB, dma_c)
    dma("sync", AMS.ap(), A_in, dma_c)
    dma("gpsimd", BPC.ap(), BP, dma_c)
    dma("gpsimd", CPC.ap(), CP, dma_c)
    dma("sync", DPS.ap(), DP, dma_c)
    dma("gpsimd", BINB.ap(), BINT, dma_c)
    dma("gpsimd", BOUTB, BOUT, dma_c)
    dma("sync", RB1S.ap(), RB1C, dma_c)
    dma("gpsimd", RW2B.ap(), RW2.rearrange("(k p) e -> p k e", p=128), dma_c)
    dma("sync", RB2S.ap(), RB2C, dma_c)
    dma("sync", EB1S.ap(), EB1C, dma_c)
    dma("sync", BDQS.ap(), BDQC, dma_c)
    dma("sync", BDKVS.ap(), BDKVC, dma_c)
    dma("sync", BKRS.ap(), BKRC, dma_c)
    CONST_N = dma_c.total

    p.emit("vector", lambda E: E.memset(ONESC.ap(), 1.0))
    p.emit("vector", lambda E: E.memset(ONESR.ap(), 1.0))
    p.emit("vector", lambda E: E.memset(TTOEP.ap(), 0.0))
    p.op("vector", lambda E: E.memset(HT.ap(), 0.0), inc=s_vv)
    V_MEMSET = s_vv.total

    def ssm_prep():
        p.wait("scalar", dma_c, CONST_N)
        p.op("scalar", lambda E: E.activation(ATS.ap(), AMS.ap(), AF.Tanh),
             inc=s_pr)
        p.wait("vector", s_pr, s_pr.total)
        p.op("vector", lambda E: E.tensor_copy(ATB.ap(), ATS.ap()))
        p.op("vector", lambda E: E.tensor_copy(PQH[0:DS, 0:DS], ATS.ap()),
             inc=s_pr)
        p.wait("tensor", s_pr, s_pr.total)
        p.wait("tensor", dma_c, CONST_N)
        p.op("tensor", lambda E: E.transpose(PS[0].ap().bitcast(BF16)[0:DS, 0:DS],
                                             ATB.ap(), IDB[0:DS, 0:DS]), inc=s_mm)
        p.wait("vector", s_mm, s_mm.total)
        p.op("vector", lambda E: E.tensor_copy(
            PQH[0:DS, DS:2 * DS], PS[0].ap().bitcast(BF16)[0:DS, 0:DS]), inc=s_pr)

        def slotP(j):
            return PQH[0:DS, (2 * j) * DS:(2 * j + 1) * DS]

        def slotQ(j):
            return PQH[0:DS, (2 * j + 1) * DS:(2 * j + 2) * DS]

        for j in range(7):
            p.wait("tensor", s_pr, s_pr.total)
            p.op("tensor", lambda E, a=slotQ(j), b=slotP(j): E.matmul(
                PS[1][0:DS, 0:DS], a, b, start=True, stop=True))
            p.op("tensor", lambda E, a=slotP(j), b=slotQ(j): E.matmul(
                PS[1][0:DS, DS:2 * DS], a, b, start=True,
                stop=True), inc=s_mm)
            p.wait("vector", s_mm, s_mm.total)
            p.op("vector", lambda E, d=slotP(j + 1): E.tensor_copy(
                d, PS[1][0:DS, 0:DS]))
            p.op("vector", lambda E, d=slotQ(j + 1): E.tensor_copy(
                d, PS[1][0:DS, DS:2 * DS]), inc=s_pr)
        P128, Q128 = slotP(7), slotQ(7)
        p.wait("vector", s_pr, s_pr.total)
        p.op("vector", lambda E: E.tensor_copy(ATLT.ap(), Q128))
        p.op("vector", lambda E: E.tensor_copy(BT_CH[0:DS, 0:1], BPC.ap()))
        p.op("vector", lambda E: E.tensor_copy(CT_CH[0:DS, 0:1], CPC.ap()),
             inc=s_pr)
        for j in range(7):
            m = 1 << j
            p.wait("tensor", s_pr, s_pr.total)
            p.op("tensor", lambda E, q=slotQ(j), mm=m: E.matmul(
                PS[1][0:DS, 0:mm], q, BT_CH[0:DS, 0:mm],
                start=True, stop=True))
            p.op("tensor", lambda E, pp=slotP(j), mm=m: E.matmul(
                PS[1][0:DS, 128:128 + mm], pp, CT_CH[0:DS, 0:mm],
                start=True, stop=True), inc=s_mm)
            p.wait("vector", s_mm, s_mm.total)
            p.op("vector", lambda E, mm=m: E.tensor_copy(
                BT_CH[0:DS, mm:2 * mm], PS[1][0:DS, 0:mm]))
            p.op("vector", lambda E, mm=m: E.tensor_copy(
                CT_CH[0:DS, mm:2 * mm], PS[1][0:DS, 128:128 + mm]),
                inc=s_pr)

        p.wait("tensor", s_pr, s_pr.total)
        p.op("tensor", lambda E: E.matmul(PS[1][0:1, 0:128], BPC.ap(),
                                          CT_CH[0:DS, 0:128], start=True,
                                          stop=True), inc=s_mm)
        p.wait("vector", s_mm, s_mm.total)
        p.op("vector", lambda E: E.tensor_copy(KROW.ap(), PS[1][0:1, 0:128]),
             inc=s_vv)
        p.wait("scalar", s_vv, s_vv.total)
        p.op("scalar", lambda E: E.add(KROW[0:1, 0:1], KROW[0:1, 0:1],
                                       DPS.ap()), inc=s_ev)
        p.wait("vector", s_ev, s_ev.total)
        p.op("vector", lambda E: E.tensor_copy(KROWB.ap(), KROW.ap()),
             inc=s_pr)
        p.wait("scalar", s_pr, s_pr.total)
        p.wait("scalar", s_vv, V_MEMSET)
        for s in range(128):
            p.op("scalar",
                 lambda E, ss=s: E.dma_start(out=TTOEP[ss:ss + 1, ss:128],
                                             in_=KROWB[0:1, 0:128 - ss]),
                 inc=s_pr, inc_n=16)
        t_ready = s_pr.total

        p.emit("vector", lambda E: E.drain())
        p.op("vector", lambda E: E.tensor_copy(WCT[0:DS, 0:127],
                                               CT_CH[0:DS, 1:128]), inc=s_pr)
        p.wait("tensor", s_pr, s_pr.total)
        p.op("tensor", lambda E: E.matmul(PS[1][0:DS, 0:1],
                                          P128, CPC.ap(), start=True,
                                          stop=True), inc=s_mm)
        p.wait("vector", s_mm, s_mm.total)
        p.op("vector", lambda E: E.tensor_copy(
            WCT[0:DS, 127:128], PS[1][0:DS, 0:1]), inc=s_pr)
        p.wait("tensor", s_pr, s_pr.total)
        p.op("tensor", lambda E: E.transpose(PS[2].ap().bitcast(BF16)[:, 0:DS],
                                             BT_CH[0:DS, 0:128],
                                             IDB[0:DS, 0:DS]), inc=s_mm)
        p.wait("vector", s_mm, s_mm.total)
        p.op("vector", lambda E: E.tensor_copy(
            BTT.ap(), PS[2].ap().bitcast(BF16)[:, 0:DS]), inc=s_pr)
        p.wait("tensor", s_pr, s_pr.total)
        p.op("tensor", lambda E: E.matmul(PS[2][:, 0:DS],
                                          JB.ap(), BTT.ap(), start=True,
                                          stop=True), inc=s_mm)
        p.wait("vector", s_mm, s_mm.total)
        p.op("vector", lambda E: E.tensor_copy(
            WB_M.ap(), PS[2][:, 0:DS]), inc=s_pr)
        return t_ready

    T_READY = ssm_prep()
    PREP_DONE = s_pr.total

    def tap_store(name, sbuf_ap, shape, dtype=F32, wait=()):
        t_ap = tap(name, shape, dtype)
        if t_ap is None:
            return
        for sem, val in wait:
            p.wait("sync", sem, val)
        p.op("sync", lambda E, o=t_ap, i=sbuf_ap: E.dma_start(out=o, in_=i),
             inc=s_st, inc_n=16)

    tap_store("ttoep", TTOEP.ap(), (128, 128), BF16, wait=[(s_pr, T_READY)])
    tap_store("wb", WB_M.ap(), (128, DS), BF16, wait=[(s_pr, PREP_DONE)])
    tap_store("wct", WCT.ap(), (DS, 128), BF16, wait=[(s_pr, PREP_DONE)])
    tap_store("atlt", ATLT.ap(), (DS, DS), BF16, wait=[(s_pr, PREP_DONE)])
    tap_store("krow", KROW.ap(), (1, 128), F32, wait=[(s_pr, PREP_DONE)])

    def layernorm(add_xn_to_x):
        p.wait("scalar", dma_x, 16)
        p.wait("vector", dma_x, 16)
        p.op("vector", lambda E: E.memset(LN_SUM.ap(), 0.0))
        p.op("vector", lambda E: E.memset(LN_MEAN.ap(), 0.0), inc=s_vv)
        p.op("vector", lambda E: E.memset(LN_SQ.ap(), 0.0), inc=s_vv)
        p.wait("scalar", s_vv, s_vv.total)
        for t in range(NT):
            p.op("scalar", lambda E, tt=t: E.activation(
                SCR8.ap(), X[:, tt, :], AF.Square,
                accum_out=LN_SQ[:, tt:tt + 1]), inc=s_ev)
            p.op("vector", lambda E, tt=t: E.tensor_scalar(
                XNB.ap().bitcast(F32)[:, 0:512], X[:, tt, 0:512], 1.0, 0.0,
                ALU.mult, ALU.add, accum_out=LN_SUM[:, tt:tt + 1]), inc=s_vv)
            p.op("vector", lambda E, tt=t: E.tensor_scalar(
                XNB.ap().bitcast(F32)[:, 0:512], X[:, tt, 512:1024], 1.0, 0.0,
                ALU.mult, ALU.add, accum_out=LN_MEAN[:, tt:tt + 1]), inc=s_vv)
        p.wait("vector", s_ev, s_ev.total)
        p.op("vector", lambda E: E.tensor_tensor(
            LN_SUM.ap(), LN_SUM.ap(), LN_MEAN.ap(), ALU.add))
        p.emit("vector", lambda E: E.drain())
        p.op("vector", lambda E: E.tensor_scalar(
            LN_MEAN.ap(), LN_SUM.ap(), 1.0 / D, 0.0, ALU.mult, ALU.add))
        p.op("vector", lambda E: E.tensor_scalar(
            LN_SQ.ap(), LN_SQ.ap(), 1.0 / D, 0.0, ALU.mult, ALU.add))
        p.emit("vector", lambda E: E.drain())
        p.op("vector", lambda E: E.tensor_tensor(
            LN_SUM.ap(), LN_MEAN.ap(), LN_MEAN.ap(), ALU.mult))
        p.emit("vector", lambda E: E.drain())
        p.op("vector", lambda E: E.tensor_tensor(
            LN_SQ.ap(), LN_SQ.ap(), LN_SUM.ap(), ALU.subtract))
        p.emit("vector", lambda E: E.drain())
        p.op("vector", lambda E: E.tensor_scalar(
            LN_SQ.ap(), LN_SQ.ap(), LN_EPS, 0.0, ALU.add, ALU.add))
        p.emit("vector", lambda E: E.drain())
        p.op("vector", lambda E: E.reciprocal(LN_SUM.ap(), LN_SQ.ap()),
             inc=s_vv)
        p.wait("scalar", s_vv, s_vv.total)
        p.op("scalar", lambda E: E.activation(LN_RSTD.ap(), LN_SUM.ap(),
                                              AF.Sqrt), inc=s_ev)
        p.wait("vector", s_ev, s_ev.total)
        for t in range(NT):
            if t >= 1:
                p.wait("vector", s_gp, s_gp.total)
                p.wait("vector", s_tr, s_tr.total)
            p.op("vector", lambda E, tt=t: E.tensor_scalar(
                SCR8.ap(), X[:, tt, :], LN_MEAN[:, tt:tt + 1],
                LN_RSTD[:, tt:tt + 1], ALU.subtract, ALU.mult))
            p.op("vector", lambda E: E.tensor_copy(XNB.ap(), SCR8.ap()),
                 inc=s_vv)
            if add_xn_to_x:
                p.wait("gpsimd", s_vv, s_vv.total)
                p.op("gpsimd", lambda E, tt=t: E.tensor_tensor(
                    X[:, tt, :], X[:, tt, :], SCR8.ap(), ALU.add), inc=s_gp)
            else:
                s_gp.inc(0)
            p.wait("tensor", s_vv, s_vv.total)
            for k in range(NK):
                bank = 6 + (k // 4)
                off = (k % 4) * 128
                p.op("tensor", lambda E, kk=k, b=bank, o=off: E.transpose(
                    PS[b].ap().bitcast(BF16)[:, o:o + 128],
                    XNB[:, kk * 128:(kk + 1) * 128], IDB.ap()),
                    inc=(s_mm if k in (3, 7) else None))
            mmv = s_mm.total
            p.wait("vector", s_mm, mmv - 1)
            for k in range(4):
                p.op("vector", lambda E, tt=t, kk=k: E.tensor_copy(
                    XNT[:, kk, tt * 128:(tt + 1) * 128],
                    PS[6].ap().bitcast(BF16)[:, kk * 128:(kk + 1) * 128]))
            p.wait("vector", s_mm, mmv)
            for k in range(4):
                p.op("vector", lambda E, tt=t, kk=k: E.tensor_copy(
                    XNT[:, 4 + kk, tt * 128:(tt + 1) * 128],
                    PS[7].ap().bitcast(BF16)[:, kk * 128:(kk + 1) * 128]),
                    inc=(s_tr if k == 3 else None))
        return s_tr.total, s_gp.total

    XNT_L0, GP_L0 = layernorm(add_xn_to_x=True)
    tap_store("ln_mean", LN_MEAN.ap(), (128, NT), F32, wait=[(s_tr, s_tr.total)])
    tap_store("ln_rstd", LN_RSTD.ap(), (128, NT), F32, wait=[(s_tr, s_tr.total)])
    tap_store("ln_sum", LN_SUM.ap(), (128, NT), F32, wait=[(s_tr, s_tr.total)])
    tap_store("ln_sq", LN_SQ.ap(), (128, NT), F32, wait=[(s_tr, s_tr.total)])
    tap_store("x0", X[:, 0, :], (128, D), F32, wait=[(dma_x, 16)])
    tap_store("xnt", XNT.ap(), (128, NK, T), BF16, wait=[(s_tr, XNT_L0)])


    def stream_mamba_weights(half):
        dma("gpsimd", WIN_H,
            WIN[:, half * 1024:(half + 1) * 1024].rearrange(
                "(k p) m -> p k m", p=128), dma_w)
        dma("gpsimd", WOUT_H,
            WOUT[half * 1024:(half + 1) * 1024, :].rearrange(
                "(k p) m -> p k m", p=128), dma_w)
        return dma_w.total

    tap_xu = tap("xu", (128, NCH, NK, 132), BF16)
    tap_vc = tap("vc", (128, NCH, D), BF16)
    tap_sy = tap("sy", (128, NCH, NK, 128), BF16)

    def mamba_half(half):
        p.wait("gpsimd", s_fb, s_fb.total)
        wv = stream_mamba_weights(half)
        p.wait("vector", s_fb, s_fb.total)
        p.op("vector", lambda E: E.memset(HT[0:DS, 0, :], 0.0))
        p.op("vector", lambda E: E.memset(UC[:, 0, :, 0:4], 0.0))
        p.op("vector", lambda E: E.memset(UC[:, 1, :, 0:4], 0.0), inc=s_vv)
        zs = s_vv.total

        p.wait("tensor", dma_w, wv - 16)
        p.wait("tensor", s_tr, XNT_L0)
        p.wait("tensor", s_pr, PREP_DONE)
        p.wait("tensor", s_vv, zs)
        p.wait("scalar", s_vv, zs)

        cv_done = {}
        for c in range(NCH):
            cur, prv = c % 2, 1 - c % 2
            for mg in range(2):
                bank = mg
                for mi in range(4):
                    m = mg * 4 + mi
                    for k in range(NK):
                        p.op("tensor", lambda E, kk=k, mm=m, b=bank, o=mi * 128, cc=c:
                             E.matmul(PS[b][:, o:o + 128],
                                      WIN_H[:, kk, mm * 128:(mm + 1) * 128],
                                      XNT[:, kk, cc * 128:(cc + 1) * 128],
                                      start=(kk == 0), stop=False))
                    gm0 = (half * NK + m) * 128
                    p.op("tensor", lambda E, b=bank, o=mi * 128, g0=gm0:
                         E.matmul(PS[b][:, o:o + 128],
                                  BINB[0:1, g0:g0 + 128],
                                  ONESR[0:1, 0:128], start=False, stop=True),
                         inc=(s_mm if mi == 3 else None))
            mm_inp = s_mm.total
            if c >= 2:
                p.wait("scalar", s_vv, cv_done[c - 2])
            p.wait("scalar", s_mm, mm_inp - 1)
            p.op("scalar", lambda E, cu=cur: E.copy(
                UC[:, cu, 0:4, 4:132],
                PS[0].ap().rearrange("p (m n) -> p m n", n=128)))
            p.wait("scalar", s_mm, mm_inp)
            p.op("scalar", lambda E, cu=cur: E.copy(
                UC[:, cu, 4:8, 4:132],
                PS[1].ap().rearrange("p (m n) -> p m n", n=128)), inc=s_ev)
            ev_u = s_ev.total
            p.wait("vector", s_ev, ev_u)
            if c > 0:
                p.op("vector", lambda E, cu=cur, pv=prv: E.tensor_copy(
                    UC[:, cu, :, 0:4], UC[:, pv, :, 128:132]))
            for m in range(NK):
                g = half * NK + m
                p.op("vector", lambda E, mm=m, gg=g, cu=cur: E.tensor_scalar(
                    VCF[:, mm, :], UC[:, cu, mm, 1:129], CONVWS[:, gg, 0:1],
                    CONVBS[:, gg:gg + 1], ALU.mult, ALU.add))
            for k in range(1, 4):
                for m in range(NK):
                    g = half * NK + m
                    p.op("vector",
                         lambda E, mm=m, gg=g, kk=k, cu=cur: E.scalar_tensor_tensor(
                             VCF[:, mm, :], UC[:, cu, mm, 1 + kk:129 + kk],
                             CONVWS[:, gg, kk:kk + 1], VCF[:, mm, :],
                             ALU.mult, ALU.add),
                         inc=(s_vv if (m == NK - 1 and k == 3) else None))
            cv_done[c] = s_vv.total
            if tap_xu is not None and half == 0:
                p.wait("sync", s_vv, cv_done[c])
                p.op("sync", lambda E, cc=c, cu=cur: E.dma_start(
                    out=tap_xu[:, cc, :, :], in_=UC[:, cu, :, :]),
                    inc=s_st, inc_n=16)
            p.wait("tensor", s_vv, cv_done[c])
            for m in range(NK):
                bank = 2 + (m // 4)
                off = (m % 4) * 128
                p.op("tensor", lambda E, mm=m, b=bank, o=off: E.transpose(
                    PS[b].ap().bitcast(BF16)[:, o:o + 128],
                    VCF[:, mm, :], IDB.ap()),
                    inc=(s_mm if m in (3, 7) else None))
            mm_tr = s_mm.total
            p.wait("vector", s_mm, mm_tr - 1)
            p.op("vector", lambda E, cu=cur: E.tensor_copy(
                VC[:, cu, 0:512], PS[2].ap().bitcast(BF16)[:, 0:512]))
            p.wait("vector", s_mm, mm_tr)
            p.op("vector", lambda E, cu=cur: E.tensor_copy(
                VC[:, cu, 512:1024], PS[3].ap().bitcast(BF16)[:, 0:512]),
                inc=s_vv)
            vc_ok = s_vv.total
            if tap_vc is not None and half == 0:
                p.wait("sync", s_vv, vc_ok)
                p.op("sync", lambda E, cc=c, cu=cur: E.dma_start(
                    out=tap_vc[:, cc, :], in_=VC[:, cu, :]), inc=s_st, inc_n=16)
            p.wait("tensor", s_vv, vc_ok)
            hcur, hnxt = c % 2, 1 - c % 2
            for m in range(NK):
                bank = 4 + (m // 4)
                off = (m % 4) * 128
                p.op("tensor", lambda E, mm=m, b=bank, o=off, cu=cur: E.matmul(
                    PS[b][:, o:o + 128],
                    VC[:, cu, mm * 128:(mm + 1) * 128], TTOEP.ap(),
                    start=True, stop=False))
                p.op("tensor", lambda E, mm=m, b=bank, o=off, hc=hcur: E.matmul(
                    PS[b][:, o:o + 128],
                    HT[0:DS, hc, mm * 128:(mm + 1) * 128], WCT.ap(),
                    start=False, stop=True),
                    inc=(s_mm if m in (3, 7) else None))
            mm_y = s_mm.total
            for n in range(2):
                p.op("tensor", lambda E, nn=n, hc=hcur: E.matmul(
                    PS[6][0:DS, 0:N512], ATLT.ap(),
                    HT[0:DS, hc, nn * 512:(nn + 1) * 512],
                    start=True, stop=False))
                p.op("tensor", lambda E, nn=n, cu=cur: E.matmul(
                    PS[6][0:DS, 0:N512], WB_M.ap(),
                    VC[:, cu, nn * 512:(nn + 1) * 512],
                    start=False, stop=True), inc=s_mm)
                mm_st = s_mm.total
                p.wait("vector", s_mm, mm_st)
                p.op("vector", lambda E, nn=n, hn=hnxt: E.tensor_copy(
                    HT[0:DS, hn, nn * 512:(nn + 1) * 512],
                    PS[6][0:DS, 0:N512]), inc=s_vv)
                p.wait("tensor", s_vv, s_vv.total)
            p.wait("scalar", s_mm, mm_y - 1)
            p.op("scalar", lambda E: E.activation(
                SYC[:, 0:4, :],
                PS[4].ap().rearrange("p (m n) -> p m n", n=128), AF.Silu))
            p.wait("scalar", s_mm, mm_y)
            p.op("scalar", lambda E: E.activation(
                SYC[:, 4:8, :],
                PS[5].ap().rearrange("p (m n) -> p m n", n=128), AF.Silu),
                inc=s_ev)
            sy_ok = s_ev.total
            if tap_sy is not None and half == 0:
                p.wait("sync", s_ev, sy_ok)
                p.op("sync", lambda E, cc=c: E.dma_start(
                    out=tap_sy[:, cc, :, :], in_=SYC), inc=s_st, inc_n=16)
            p.wait("tensor", s_ev, sy_ok)
            if c == 0:
                p.wait("tensor", dma_w, wv)
            for n in range(2):
                for m in range(NK):
                    last = (m == NK - 1) and (half == 1)
                    p.op("tensor", lambda E, mm=m, nn=n, st_=last: E.matmul(
                        PS[7][:, 0:N512],
                        SYC[:, mm, :], WOUT_H[:, mm, nn * 512:(nn + 1) * 512],
                        start=(mm == 0), stop=st_),
                        inc=(s_mm if last else None))
                if half == 0:
                    p.op("tensor", lambda E, nn=n: E.matmul(
                        PS[7][:, 0:N512], ONESR[0:1, 0:128],
                        BOUTB[0:1, nn * 512:(nn + 1) * 512],
                        start=False, stop=True), inc=s_mm)
                mm_o = s_mm.total
                p.wait("vector", s_mm, mm_o)
                p.op("vector", lambda E, cc=c, nn=n: E.tensor_tensor(
                    X[:, cc, nn * 512:(nn + 1) * 512],
                    X[:, cc, nn * 512:(nn + 1) * 512],
                    PS[7][:, 0:N512], ALU.add), inc=s_fb)
                p.wait("tensor", s_fb, s_fb.total)
        return

    p.wait("vector", s_gp, GP_L0)
    mamba_half(0)
    mamba_half(1)
    MAMBA_DONE_V = s_fb.total

    tap_store("x_after_mamba", X.ap(), (128, NT, D), F32,
              wait=[(s_fb, MAMBA_DONE_V)])

    RW1B = arn_b(0, NK, D)
    LGT = arn_f(16384, T)[0:E0, :]
    EW2RES = arn_b(0, NM, D)
    WBC = arn_b(32768, T)
    EW1R = arn_b(36864, 2, NK, 256)
    WTR = arn_b(45056, T)[0:E0, :]
    EB2RB = arn_b(16384, D)[0:E0, :]

    R1T = BIGT.ap().rearrange("p (m t) -> p m t", m=NK)
    EHW = BIGT.ap().rearrange("p (m t) -> p m t", m=NM)

    def moe_l0(stop=""):
        p.wait("gpsimd", s_fb, s_fb.total)
        dma("gpsimd", RW1B, RW1.rearrange("(k p) m -> p k m", p=128), dma_w)
        w_rw1 = dma_w.total
        p.wait("tensor", dma_w, w_rw1)
        p.wait("tensor", s_fb, s_fb.total)
        p.wait("tensor", s_ev, s_ev.total)
        p.wait("tensor", s_vv, s_vv.total)
        for m8 in range(NK):
            for n in range(4):
                bank = (m8 * 4 + n) % 2
                for k in range(NK):
                    p.op("tensor", lambda E, kk=k, mm=m8, nn=n, b=bank: E.matmul(
                        PS[b][:, 0:N512], RW1B[:, kk, mm * 128:(mm + 1) * 128],
                        XNT[:, kk, nn * 512:(nn + 1) * 512],
                        start=(kk == 0), stop=(kk == NK - 1)),
                        inc=(s_mm if k == NK - 1 else None))
                mmv = s_mm.total
                p.wait("scalar", s_mm, mmv)
                p.op("scalar", lambda E, mm=m8, nn=n, b=bank: E.activation(
                    R1T[:, mm, nn * 512:(nn + 1) * 512], PS[b][:, 0:N512],
                    AF.Silu, bias=RB1S[:, mm:mm + 1]), inc=s_ev)
                p.wait("tensor", s_ev, s_ev.total)
        if stop == "rw1":
            return
        for n in range(4):
            for k in range(NK):
                p.op("tensor", lambda E, kk=k, nn=n: E.matmul(
                    PS[2][0:E0, 0:N512], RW2B[:, kk, :],
                    R1T[:, kk, nn * 512:(nn + 1) * 512],
                    start=(kk == 0), stop=(kk == NK - 1)),
                    inc=(s_mm if k == NK - 1 else None))
            mmv = s_mm.total
            p.wait("scalar", s_mm, mmv)
            p.op("scalar", lambda E, nn=n: E.activation(
                LGT[0:E0, nn * 512:(nn + 1) * 512], PS[2][0:E0, 0:N512],
                AF.Identity, bias=RB2S.ap()), inc=s_ev)
            p.wait("tensor", s_ev, s_ev.total)
        for t in range(NT):
            bank = t % 2
            p.wait("tensor", s_vv, s_vv.total)
            p.op("tensor", lambda E, tt=t, b=bank: E.transpose(
                PS[b][:, 0:E0], LGT[0:E0, tt * 128:(tt + 1) * 128],
                IDF[0:E0, 0:E0]), inc=s_mm)
            mmv = s_mm.total
            p.wait("vector", s_mm, mmv)
            p.op("vector", lambda E, tt=t, b=bank: E.tensor_copy(
                LGTM[:, tt, :], PS[b][:, 0:E0]), inc=s_vv)
        LG_OK = s_vv.total
        if "lgtm" in taps:
            tap_store("lgtm", LGTM.ap(), (128, NT, E0), F32, wait=[(s_vv, LG_OK)])
        if stop == "logits":
            return
        p.op("vector", lambda E: E.reduce_max(
            TOP_M1.ap(), LGTM.ap(), axis=AX.X))
        p.emit("vector", lambda E: E.drain())
        p.op("vector", lambda E: E.tensor_scalar(
            TOP_M1N.ap(), TOP_M1.ap(), -1.0, 0.0, ALU.mult, ALU.add),
            inc=s_vv)
        for t in range(NT):
            p.op("vector", lambda E, tt=t: E.tensor_scalar(
                TOP_MSK[:, tt, :], LGTM[:, tt, :], TOP_M1[:, tt:tt + 1], None,
                ALU.is_ge))
        for t in range(NT):
            p.op("vector", lambda E, tt=t: E.scalar_tensor_tensor(
                TOP_MSK[:, tt, :], TOP_MSK[:, tt, :], -1e30, LGTM[:, tt, :],
                ALU.mult, ALU.add))
        for t in range(NT):
            p.op("vector", lambda E, tt=t: E.reduce_max(
                TOP_M2[:, tt:tt + 1], TOP_MSK[:, tt, :], axis=AX.X),
                inc=(s_vv if t == NT - 1 else None))
        M2_OK = s_vv.total
        p.wait("scalar", s_vv, M2_OK)
        for t in range(NT):
            p.op("scalar", lambda E, tt=t: E.activation(
                TOP_E1[:, tt, :], LGTM[:, tt, :], AF.Exp,
                bias=TOP_M1N[:, tt:tt + 1]), inc=(s_ev if t == NT - 1 else None))
        E1_OK = s_ev.total
        for t in range(NT):
            p.op("vector", lambda E, tt=t: E.tensor_scalar(
                TOP_MSK[:, tt, :], LGTM[:, tt, :], TOP_M2[:, tt:tt + 1], None,
                ALU.is_ge))
        p.wait("vector", s_ev, E1_OK)
        for t in range(NT):
            p.op("vector", lambda E, tt=t: E.tensor_tensor(
                TOP_E1[:, tt, :], TOP_E1[:, tt, :], TOP_MSK[:, tt, :],
                ALU.mult))
        for t in range(NT):
            p.op("vector", lambda E, tt=t: E.reduce_sum(
                TOP_WS[:, tt:tt + 1], TOP_E1[:, tt, :], axis=AX.X))
        p.emit("vector", lambda E: E.drain())
        p.op("vector", lambda E: E.reciprocal(TOP_M1.ap(), TOP_WS.ap()))
        p.emit("vector", lambda E: E.drain())
        for t in range(NT):
            p.op("vector", lambda E, tt=t: E.tensor_scalar(
                WTM[:, tt, :], TOP_E1[:, tt, :], TOP_M1[:, tt:tt + 1], None,
                ALU.mult), inc=(s_vv if t == NT - 1 else None))
        W_OK = s_vv.total
        p.wait("tensor", s_vv, W_OK)
        for t in range(NT):
            bank = t % 2
            p.wait("tensor", s_vv, s_vv.total)
            p.op("tensor", lambda E, tt=t, b=bank: E.transpose(
                PS[b].ap().bitcast(BF16)[0:E0, 0:128], WTM[:, tt, :],
                IDB.ap()), inc=s_mm)
            mmv = s_mm.total
            p.wait("vector", s_mm, mmv)
            p.op("vector", lambda E, tt=t, b=bank: E.tensor_copy(
                WTR[0:E0, tt * 128:(tt + 1) * 128],
                PS[b].ap().bitcast(BF16)[0:E0, 0:128]), inc=s_vv)
        WT_OK = s_vv.total
        if "wtm" in taps:
            tap_store("wtm", WTM.ap(), (128, NT, E0), BF16, wait=[(s_vv, W_OK)])
        if stop == "top2":
            return
        p.wait("gpsimd", s_mm, s_mm.total)
        dma("gpsimd", EB2RB, EB2R, dma_c)
        c_eb2 = dma_c.total
        p.wait("tensor", dma_c, c_eb2)
        p.wait("tensor", s_vv, WT_OK)
        for t in range(NT):
            b0 = 2 + 2 * (t % 2)
            for n in range(2):
                p.op("tensor", lambda E, tt=t, nn=n, b=b0: E.matmul(
                    PS[b + nn][:, 0:N512],
                    WTR[0:E0, tt * 128:(tt + 1) * 128],
                    EB2RB[0:E0, nn * 512:(nn + 1) * 512],
                    start=True, stop=True),
                    inc=(s_mm if n == 1 else None))
            mmv = s_mm.total
            p.wait("vector", s_mm, mmv)
            for n in range(2):
                p.op("vector", lambda E, tt=t, nn=n, b=b0: E.tensor_tensor(
                    X[:, tt, nn * 512:(nn + 1) * 512],
                    X[:, tt, nn * 512:(nn + 1) * 512],
                    PS[b + nn][:, 0:N512], ALU.add),
                    inc=(s_fb if n == 1 else None))
            if t % 2 == 1:
                p.wait("tensor", s_fb, s_fb.total)
        ew2_free_mm = s_mm.total
        ew1_hist = []
        n_exp = 1 if stop == "e1" else E0
        for e in range(n_exp):
            ew2_st = {"j": 0, "waited": False}

            def ew2_chunks(n_c):
                if not ew2_st["waited"]:
                    p.wait("gpsimd", s_mm, ew2_free_mm)
                    if e == 0:
                        p.wait("gpsimd", s_ev, s_ev.total)
                    ew2_st["waited"] = True
                for _ in range(n_c):
                    kk = ew2_st["j"]
                    dma("gpsimd", EW2RES[:, kk, :],
                        EW2[e, kk * 128:(kk + 1) * 128, :], dma_w)
                    ew2_st["j"] = kk + 1

            if stop == "ew2dma":
                ew2_chunks(NM)
                p.wait("sync", dma_w, dma_w.total)
                return
            p.wait("vector", s_vv, WT_OK)
            p.wait("vector", s_fb, s_fb.total)
            p.wait("scalar", s_vv, s_vv.total)
            p.op("scalar", lambda E, ee=e: E.dma_start(
                out=WROW.ap(), in_=WTR[ee:ee + 1, :]), inc=s_ev, inc_n=16)
            wr_ok = s_ev.total
            if stop == "wrow":
                p.wait("sync", s_ev, wr_ok)
                return
            p.wait("tensor", s_ev, wr_ok)
            for n in range(4):
                b = 2 + (n % 2)
                if n >= 2:
                    p.wait("tensor", s_vv, s_vv.total - 1)
                p.op("tensor", lambda E, nn=n, bb=b: E.matmul(
                    PS[bb][:, 0:N512], ONESR[0:1, 0:128],
                    WROW[0:1, nn * 512:(nn + 1) * 512],
                    start=True, stop=True), inc=s_mm)
                mmv = s_mm.total
                p.wait("vector", s_mm, mmv)
                p.op("vector", lambda E, nn=n, bb=b: E.tensor_copy(
                    WBC[:, nn * 512:(nn + 1) * 512], PS[bb][:, 0:N512]),
                    inc=s_vv)
            wbc_ok = s_vv.total
            if stop == "wbc":
                return
            for ht in range(2):
                for j in range(8):
                    gj = len(ew1_hist)
                    ring = gj % 2
                    if gj >= 2:
                        p.wait("gpsimd", s_mm, ew1_hist[gj - 2])
                    dma("gpsimd", EW1R[:, ring, :, :],
                        EW1[e, :, j * 256:(j + 1) * 256].rearrange(
                            "(k p) h -> p k h", p=128), dma_w)
                    w_blk = dma_w.total
                    if ht == 0 and j >= 2:
                        ew2_chunks({2: 3, 3: 3, 4: 3, 5: 3, 6: 2, 7: 2}[j])
                        if j == 7:
                            w_ew2 = dma_w.total
                    p.wait("tensor", dma_w, w_blk)
                    if j == 0:
                        p.wait("tensor", s_vv, wbc_ok)
                    for mi in range(2):
                        m = 2 * j + mi
                        for n in range(2):
                            bank = (2 * m + n) % 2
                            if s_ex.total >= 1:
                                p.wait("tensor", s_ex, s_ex.total - 1)
                            for k in range(NK):
                                p.op("tensor", lambda E, kk=k, rr=ring, nn=n,
                                     b=bank, h0=ht, mo=mi * 128: E.matmul(
                                         PS[b][:, 0:N512],
                                         EW1R[:, rr, kk, mo:mo + 128],
                                         XNT[:, kk,
                                             h0 * 1024 + nn * 512:
                                             h0 * 1024 + (nn + 1) * 512],
                                         start=(kk == 0), stop=(kk == NK - 1)),
                                     inc=(s_mm if k == NK - 1 else None))
                            mmv = s_mm.total
                            p.wait("scalar", s_mm, mmv)
                            p.op("scalar", lambda E, mm=m, nn=n, b=bank, ee=e:
                                 E.activation(
                                     EHW[:, mm, nn * 512:(nn + 1) * 512],
                                     PS[b][:, 0:N512], AF.Silu,
                                     bias=EB1S[:, ee, mm:mm + 1]), inc=s_ex)
                            p.wait("vector", s_ex, s_ex.total)
                            p.op("vector", lambda E, mm=m, nn=n, h0=ht:
                                 E.tensor_tensor(
                                     EHW[:, mm, nn * 512:(nn + 1) * 512],
                                     EHW[:, mm, nn * 512:(nn + 1) * 512],
                                     WBC[:, h0 * 1024 + nn * 512:
                                         h0 * 1024 + (nn + 1) * 512],
                                     ALU.mult), inc=s_vv)
                    ew1_hist.append(s_mm.total)
                if stop == "ew1h0":
                    return
                p.wait("tensor", dma_w, w_ew2)
                p.wait("tensor", s_vv, s_vv.total)
                for tl in range(8):
                    t = ht * 8 + tl
                    b0 = 2 + 2 * (tl % 2)
                    for n in range(2):
                        for k in range(NM):
                            p.op("tensor", lambda E, kk=k, nn=n, tt=tl, b=b0:
                                 E.matmul(
                                     PS[b + nn][:, 0:N512],
                                     EHW[:, kk, tt * 128:(tt + 1) * 128],
                                     EW2RES[:, kk, nn * 512:(nn + 1) * 512],
                                     start=(kk == 0), stop=False))
                        p.items["tensor"].pop()
                        p.op("tensor", lambda E, nn=n, tt=tl, b=b0: E.matmul(
                            PS[b + nn][:, 0:N512],
                            EHW[:, NM - 1, tt * 128:(tt + 1) * 128],
                            EW2RES[:, NM - 1, nn * 512:(nn + 1) * 512],
                            start=False, stop=True),
                            inc=(s_mm if n == 1 else None))
                    mmv = s_mm.total
                    p.wait("vector", s_mm, mmv)
                    for n in range(2):
                        p.op("vector", lambda E, tt=t, nn=n, b=b0: E.tensor_tensor(
                            X[:, tt, nn * 512:(nn + 1) * 512],
                            X[:, tt, nn * 512:(nn + 1) * 512],
                            PS[b + nn][:, 0:N512], ALU.add),
                            inc=(s_fb if n == 1 else None))
                    if tl % 2 == 1:
                        p.wait("tensor", s_fb, s_fb.total)
            ew2_free_mm = s_mm.total
        return

    if stop != "mamba":
        moe_l0(stop)
    MOE_DONE = s_fb.total
    tap_store("x_after_l0", X.ap(), (128, NT, D), F32,
              wait=[(s_fb, MOE_DONE)])

    WDQB = arn_b(0, NK, DC)
    WDKVB = arn_b(512, NK, DC)
    WKRB = arn_b(1024, NK, DHR)
    WUQB = arn_b(1536, NHEAD * DK)[0:DC + 1, :]
    WUKB = arn_b(3584, NHEAD * DK)[0:DC + 1, :]
    WUVB = arn_b(5632, NHEAD * DK)[0:DC + 1, :]
    WQRB = arn_b(7680, NHEAD * DHR)[0:DC + 1, :]
    WOBH = arn_b(8192, NK, N512)
    CQT = arn_b(8192, T)[0:DC + 1, :]
    KX67 = arn_b(12288, T)
    WEFF = arn_b(16384, NHEAD, D)
    CKVTM = arn_b(32768, NT, 64)
    WUQT = arn_b(34816, NHEAD, 64)
    WUKT = arn_b(35840, NHEAD, 64)
    WUVT = arn_b(36864, NHEAD, 64)
    MHB = arn_b(37888, NHEAD, 64)
    RCPR = arn_b(38912, 1024)[0:1, :]
    BORB2 = arn_b(38912, D)[0:1, :]
    PT6 = arn_b(40960, 6, N512)
    RCPB33 = arn_b(47104, 1024)[0:DC, :]
    KN2R = arn_b(47104, 1024)[0:1, :]
    KN2R96 = arn_b(47104, 1024)
    KEYR = arn_b(0, 2, NK, H1)
    ESELN = arn_b(4096, D)[0:E1, :]
    SELTB = arn_b(4096, T)[0:E1, :]
    VALB = arn_b(12288, E1, D)
    SELBH = arn_b(8192, 1024)
    ESELTB = arn_b(45056, NK, E1)
    Q67A = BIGT.ap()[0:64, :].rearrange("p (h t) -> p h t", h=NHEAD)
    UH = BIGT.ap()[64:96, :].rearrange("p (h t) -> p h t", h=NHEAD)
    UHC = BIGT.ap()[0:32, :].rearrange("p (h t) -> p h t", h=NHEAD)

    def l1_layer():
        p.wait("vector", s_fb, s_fb.total)
        p.wait("scalar", s_fb, s_fb.total)
        p.wait("tensor", s_fb, s_fb.total)
        xnt_l1, _ = layernorm(add_xn_to_x=False)
        if stop == "ln1":
            return

        p.wait("gpsimd", s_mm, s_mm.total)
        dma("gpsimd", WDQB, WDQ.rearrange("(k p) c -> p k c", p=128), dma_w)
        dma("gpsimd", WDKVB, WDKV.rearrange("(k p) c -> p k c", p=128), dma_w)
        dma("gpsimd", WKRB, WKR.rearrange("(k p) c -> p k c", p=128), dma_w)
        dma("gpsimd", WUQB, WUQ, dma_w)
        dma("gpsimd", WUKB, WUK, dma_w)
        dma("gpsimd", WUVB, WUV, dma_w)
        dma("gpsimd", WQRB, WQR, dma_w)
        dma("gpsimd", BORB2, BOR, dma_w)
        w_attn = dma_w.total
        dma("gpsimd", WOBH,
            WO[:, 0:512].rearrange("(k p) d -> p k d", p=128), dma_w)
        w_wo0 = dma_w.total
        if stop == "attw":
            p.wait("sync", dma_w, w_wo0)
            return

        p.op("vector", lambda E: E.memset(IND33.ap(), 0.0))
        p.op("vector", lambda E: E.memset(IND33[0:1, 32:33], 1.0), inc=s_vv)
        ind_ok = s_vv.total
        if stop == "attind":
            p.wait("sync", s_vv, ind_ok)
            return

        p.wait("tensor", dma_w, w_attn)
        p.wait("tensor", s_ex, s_ex.total)
        p.wait("tensor", s_vv, s_vv.total)
        p.wait("tensor", s_fb, s_fb.total)
        bank_gate = {}

        def bank_wait(b):
            g = bank_gate.get(b)
            if g is not None:
                p.wait("tensor", g[0], g[1])

        tr_srcs = ((WUQB, WUQT),) if stop == "atttr1" else (
            (WUQB, WUQT), (WUKB, WUKT), (WUVB, WUVT))
        for src, dst in tr_srcs:
            for h in range(NHEAD):
                bank = h % 2
                bank_wait(bank)
                p.op("tensor", lambda E, s=src, hh=h, b=bank: E.transpose(
                    PS[b].ap().bitcast(BF16)[:, 0:DC + 1],
                    s[0:DC + 1, hh * 128:(hh + 1) * 128],
                    IDB[0:DC + 1, 0:DC + 1]), inc=s_mm)
                mmv = s_mm.total
                p.wait("vector", s_mm, mmv)
                p.op("vector", lambda E, d=dst, hh=h, b=bank: E.tensor_copy(
                    d[:, hh, 0:DC + 1], PS[b].ap().bitcast(BF16)[:, 0:DC + 1]),
                    inc=s_vv)
                bank_gate[bank] = (s_vv, s_vv.total)
        if stop == "atttr1":
            p.wait("sync", s_vv, s_vv.total)
            return
        p.wait("tensor", s_vv, s_vv.total)
        for h in range(NHEAD):
            bank = 2 + h % 2
            bank_wait(bank)
            p.op("tensor", lambda E, hh=h, b=bank: E.matmul(
                PS[b][0:DC + 1, 0:DC + 1], WUQT[:, hh, 0:DC + 1],
                WUKT[:, hh, 0:DC + 1], start=True, stop=True), inc=s_mm)
            mmv = s_mm.total
            p.wait("vector", s_mm, mmv)
            p.op("vector", lambda E, hh=h, b=bank: E.tensor_copy(
                MHB[0:DC + 1, hh, 0:DC + 1], PS[b][0:DC + 1, 0:DC + 1]),
                inc=s_vv)
            bank_gate[bank] = (s_vv, s_vv.total)
        if stop == "atttr":
            p.wait("sync", s_vv, s_vv.total)
            return
        for n in range(2):
            p.op_noop = None
            p.wait("tensor", dma_w, w_wo0 if n == 0 else w_wo1)
            if n == 0:
                p.wait("tensor", s_vv, ind_ok)
            for h in range(NHEAD):
                bank = 4 + h % 2
                bank_wait(bank)
                if h == 0:
                    p.wait("tensor", s_vv, s_vv.total)
                if h == 0:
                    p.op("tensor", lambda E, b=bank: E.matmul(
                        PS[b][0:DC + 1, 0:N512], WUVT[:, 0, 0:DC + 1],
                        WOBH[:, 0, :], start=True, stop=False))
                    p.op("tensor", lambda E, nn=n, b=bank: E.matmul(
                        PS[b][0:DC + 1, 0:N512], IND33[0:1, 0:DC + 1],
                        BORB2[0:1, nn * 512:(nn + 1) * 512],
                        start=False, stop=True), inc=s_mm)
                else:
                    p.op("tensor", lambda E, hh=h, b=bank: E.matmul(
                        PS[b][0:DC + 1, 0:N512], WUVT[:, hh, 0:DC + 1],
                        WOBH[:, hh, :], start=True, stop=True), inc=s_mm)
                mmv = s_mm.total
                p.wait("vector", s_mm, mmv)
                p.op("vector", lambda E, hh=h, nn=n, b=bank: E.tensor_copy(
                    WEFF[0:DC + 1, hh, nn * 512:(nn + 1) * 512],
                    PS[b][0:DC + 1, 0:N512]), inc=s_vv)
                bank_gate[bank] = (s_vv, s_vv.total)
            if n == 0:
                weff_n0 = s_mm.total
                p.wait("gpsimd", s_mm, weff_n0)
                dma("gpsimd", WOBH,
                    WO[:, 512:1024].rearrange("(k p) d -> p k d", p=128),
                    dma_w)
                w_wo1 = dma_w.total
        weff_mm = s_mm.total
        WEFF_OK = s_vv.total
        if stop == "weff":
            p.wait("sync", s_vv, WEFF_OK)
            return

        p.wait("vector", s_mm, weff_mm)
        p.op("vector", lambda E: E.memset(CQT[DC:DC + 1, :], 1.0))
        p.op("vector", lambda E: E.memset(CKVTM[:, :, 32:33], 1.0), inc=s_vv)
        ones_ok = s_vv.total
        p.wait("tensor", s_tr, xnt_l1)
        p.wait("tensor", s_vv, ones_ok)

        def proj32(dst, lhs_w, bias_ap, nm):
            for n in range(4):
                bank = n % 2
                if s_ex.total >= 1:
                    p.wait("tensor", s_ex, s_ex.total - 1)
                for k in range(NK):
                    p.op("tensor", lambda E, kk=k, nn=n, b=bank, w=lhs_w,
                         m=nm: E.matmul(
                             PS[b][0:m, 0:N512], w[:, kk, :],
                             XNT[:, kk, nn * 512:(nn + 1) * 512],
                             start=(kk == 0), stop=(kk == NK - 1)),
                         inc=(s_mm if k == NK - 1 else None))
                mmv = s_mm.total
                p.wait("scalar", s_mm, mmv)
                p.op("scalar", lambda E, nn=n, b=bank, d=dst, ba=bias_ap, m=nm:
                     E.activation(d[0:m, nn * 512:(nn + 1) * 512],
                                  PS[b][0:m, 0:N512], AF.Identity, bias=ba),
                     inc=s_ex)

        proj32(CQT, WDQB, BDQS.ap(), DC)
        proj32(KX67, WDKVB, BDKVS.ap(), DC)
        proj32(KX67[32:64, :], WKRB, BKRS.ap(), DHR)
        p.wait("tensor", s_ex, s_ex.total)
        proj_ex = s_ex.total

        for tp in range(NT):
            bank = tp % 2
            bank_wait(bank)
            p.op("tensor", lambda E, t_=tp, b=bank: E.transpose(
                PS[b].ap().bitcast(BF16)[:, 0:DC],
                KX67[0:DC, t_ * 128:(t_ + 1) * 128],
                IDB[0:DC, 0:DC]), inc=s_mm)
            mmv = s_mm.total
            p.wait("vector", s_mm, mmv)
            p.op("vector", lambda E, t_=tp, b=bank: E.tensor_copy(
                CKVTM[:, t_, 0:DC], PS[b].ap().bitcast(BF16)[:, 0:DC]),
                inc=s_vv)
            bank_gate[bank] = (s_vv, s_vv.total)
        n_heads = 1 if stop == "h1" else NHEAD
        for h in range(n_heads):
            for part in range(2):
                for n in range(4):
                    bank = 2 * part + n % 2
                    bank_wait(bank)
                    if part == 0:
                        p.op("tensor", lambda E, hh=h, nn=n, b=bank: E.matmul(
                            PS[b][0:DC, 0:N512], MHB[0:DC + 1, hh, 0:DC],
                            CQT[0:DC + 1, nn * 512:(nn + 1) * 512],
                            start=True, stop=True), inc=s_mm)
                    else:
                        p.op("tensor", lambda E, hh=h, nn=n, b=bank: E.matmul(
                            PS[b][0:DHR, 0:N512],
                            WQRB[0:DC + 1, hh * DHR:(hh + 1) * DHR],
                            CQT[0:DC + 1, nn * 512:(nn + 1) * 512],
                            start=True, stop=True), inc=s_mm)
                    mmv = s_mm.total
                    p.wait("vector", s_mm, mmv)
                    if part == 0:
                        p.op("vector", lambda E, hh=h, nn=n, b=bank:
                             E.tensor_copy(
                                 Q67A[0:32, hh, nn * 512:(nn + 1) * 512],
                                 PS[b][0:DC, 0:N512]), inc=s_vv)
                    else:
                        p.op("vector", lambda E, hh=h, nn=n, b=bank:
                             E.tensor_copy(
                                 Q67A[32:64, hh, nn * 512:(nn + 1) * 512],
                                 PS[b][0:DHR, 0:N512]), inc=s_vv)
                    bank_gate[bank] = (s_vv, s_vv.total)
        PREP_V = s_vv.total
        if stop == "attprep":
            p.wait("sync", s_vv, PREP_V)
            return

        exp_of_bank = {}
        pv_free_v = PREP_V
        for h in range(n_heads):
            for hh in range(2):
                toff = hh * 1024
                p.wait("tensor", s_vv, pv_free_v)
                groups = [list(range(3 * g, min(3 * g + 3, NT)))
                          for g in range(6)]
                for tps in groups:
                    for tp in tps:
                        for n in range(2):
                            slot = (2 * tp + n) % 6
                            if slot in exp_of_bank:
                                p.wait("tensor", s_ex, exp_of_bank[slot])
                            p.op("tensor", lambda E, t_=tp, nn=n, sl=slot,
                                 to=toff, h_=h: E.matmul(
                                     PS[sl][:, 0:N512],
                                     KX67[0:64, t_ * 128:(t_ + 1) * 128],
                                     Q67A[0:64, h_,
                                          to + nn * 512:to + (nn + 1) * 512],
                                     start=True, stop=True),
                                 inc=(s_mm if (tp == tps[-1] and n == 1)
                                      else None))
                    sc_mm = s_mm.total
                    p.wait("scalar", s_mm, sc_mm)
                    for tp in tps:
                        for n in range(2):
                            slot = (2 * tp + n) % 6
                            p.op("scalar", lambda E, sl=slot: E.activation(
                                PT6[:, sl, :], PS[sl][:, 0:N512], AF.Exp,
                                scale=SCALE), inc=s_ex)
                            exp_of_bank[slot] = s_ex.total
                    for tp in tps:
                        for n in range(2):
                            slot = (2 * tp + n) % 6
                            p.wait("tensor", s_ex, exp_of_bank[slot])
                            p.op("tensor", lambda E, t_=tp, nn=n, sl=slot:
                                 E.matmul(
                                     PS[6 + nn][0:DC + 1, 0:N512],
                                     CKVTM[:, t_, 0:DC + 1], PT6[:, sl, :],
                                     start=(t_ == 0), stop=(t_ == NT - 1)),
                                 inc=(s_mm if (tp == NT - 1 and n == 1)
                                      else None))
                pv_mm = s_mm.total
                p.wait("vector", s_mm, pv_mm)

                def _recip(E, nn):
                    with nc.allow_low_precision(
                            reason="softmax 1/sum broadcast operand in bf16; "
                                   "P and V operands are bf16 anyway"):
                        return E.reciprocal(
                            RCPR[0:1, nn * 512:(nn + 1) * 512],
                            PS[6 + nn][32:33, 0:N512])
                for n in range(2):
                    p.op("vector", lambda E, nn=n: _recip(E, nn),
                         inc=(s_vv if n == 1 else None))
                rs_ok = s_vv.total
                p.wait("tensor", s_vv, rs_ok)
                for n in range(2):
                    p.op("tensor", lambda E, nn=n: E.matmul(
                        PS[4 + nn][0:DC, 0:N512], ONESR[0:1, 0:DC],
                        RCPR[0:1, nn * 512:(nn + 1) * 512],
                        start=True, stop=True),
                        inc=(s_mm if n == 1 else None))
                mmv = s_mm.total
                p.wait("vector", s_mm, mmv)
                for n in range(2):
                    p.op("vector", lambda E, nn=n: E.tensor_copy(
                        RCPB33[0:DC, nn * 512:(nn + 1) * 512],
                        PS[4 + nn][0:DC, 0:N512]))
                p.emit("vector", lambda E: E.drain())
                for n in range(2):
                    p.op("vector", lambda E, h_=h, nn=n, to=toff:
                         E.tensor_tensor(
                             UH[0:32, h_, to + nn * 512:to + (nn + 1) * 512],
                             PS[6 + nn][0:DC, 0:N512],
                             RCPB33[0:DC, nn * 512:(nn + 1) * 512],
                             ALU.mult), inc=s_vv)
                pv_free_v = s_vv.total
        if stop in ("h1", "attn"):
            p.wait("sync", s_vv, s_vv.total)
            return

        p.wait("vector", s_vv, s_vv.total)
        p.wait("vector", s_mm, s_mm.total)
        p.op("vector", lambda E: E.tensor_copy(
            KN2R96[32:33, 0:1024], WEFF[32:33, 0, :]))
        for h2 in range(1, NHEAD):
            p.emit("vector", lambda E: E.drain())
            p.op("vector", lambda E, hh=h2: E.tensor_tensor(
                KN2R96[32:33, 0:1024], KN2R96[32:33, 0:1024],
                WEFF[32:33, hh, :], ALU.add))
        p.emit("vector", lambda E: E.drain())
        p.op("vector", lambda E: E.tensor_copy(
            KN2R[0:1, 0:1024], KN2R96[32:33, 0:1024]))
        for h2 in range(NHEAD):
            p.op("vector", lambda E, hh=h2: E.tensor_copy(
                UHC[0:32, hh, :], UH[0:32, hh, :]),
                inc=(s_vv if h2 == NHEAD - 1 else None))
        w32_ok = s_vv.total
        p.wait("tensor", s_vv, w32_ok)
        for t in range(NT):
            b0 = 2 + 2 * (t % 2)
            for n in range(2):
                for h in range(NHEAD):
                    p.op("tensor", lambda E, hh=h, nn=n, tt=t, b=b0: E.matmul(
                        PS[b + nn][:, 0:N512],
                        UHC[0:32, hh, tt * 128:(tt + 1) * 128],
                        WEFF[0:DC, hh, nn * 512:(nn + 1) * 512],
                        start=(hh == 0), stop=False))
                p.op("tensor", lambda E, nn=n, b=b0: E.matmul(
                    PS[b + nn][:, 0:N512], ONESR[0:1, 0:128],
                    KN2R[0:1, nn * 512:(nn + 1) * 512],
                    start=False, stop=True),
                    inc=(s_mm if n == 1 else None))
            mmv = s_mm.total
            p.wait("vector", s_mm, mmv)
            for n in range(2):
                p.op("vector", lambda E, tt=t, nn=n, b=b0: E.tensor_tensor(
                    X[:, tt, nn * 512:(nn + 1) * 512],
                    X[:, tt, nn * 512:(nn + 1) * 512],
                    PS[b + nn][:, 0:N512], ALU.add),
                    inc=(s_fb if n == 1 else None))
            if t % 2 == 1:
                p.wait("tensor", s_fb, s_fb.total)
        WO_DONE = s_fb.total
        if stop == "wo":
            return
        p.wait("gpsimd", s_mm, s_mm.total)
        dma("gpsimd", ESELN, ESEL, dma_w)
        w_esel = dma_w.total
        val_st = {"j": 0}

        def val_chunks(n_c):
            for _ in range(n_c):
                kk = val_st["j"]
                dma("gpsimd", VALB[:, 2 * kk:2 * kk + 2, :],
                    VALUES.rearrange("e p v -> p e v")[:, 2 * kk:2 * kk + 2, :],
                    dma_w)
                val_st["j"] = kk + 1

        if stop == "ffnw":
            val_chunks(8)
            p.wait("sync", dma_w, dma_w.total)
            return
        p.wait("tensor", dma_w, w_esel)
        p.wait("tensor", s_fb, WO_DONE)
        for k in range(NK):
            bank = k % 2
            p.wait("tensor", s_vv, s_vv.total)
            p.op("tensor", lambda E, kk=k, b=bank: E.transpose(
                PS[b].ap().bitcast(BF16)[:, 0:E1],
                ESELN[0:E1, kk * 128:(kk + 1) * 128], IDB[0:E1, 0:E1]),
                inc=s_mm)
            mmv = s_mm.total
            p.wait("vector", s_mm, mmv)
            p.op("vector", lambda E, kk=k, b=bank: E.tensor_copy(
                ESELTB[:, kk, :], PS[b].ap().bitcast(BF16)[:, 0:E1]),
                inc=s_vv)
        es_ok = s_vv.total
        if stop == "eselt":
            return
        p.wait("tensor", s_vv, es_ok)
        for n in range(4):
            bank = n % 2
            if s_ex.total >= 1:
                p.wait("tensor", s_ex, s_ex.total - 1)
            for k in range(NK):
                p.op("tensor", lambda E, kk=k, nn=n, b=bank: E.matmul(
                    PS[b][0:E1, 0:N512], ESELTB[:, kk, :],
                    XNT[:, kk, nn * 512:(nn + 1) * 512],
                    start=(kk == 0), stop=(kk == NK - 1)),
                    inc=(s_mm if k == NK - 1 else None))
            mmv = s_mm.total
            p.wait("scalar", s_mm, mmv)
            p.op("scalar", lambda E, nn=n, b=bank: E.activation(
                SELTB[0:E1, nn * 512:(nn + 1) * 512], PS[b][0:E1, 0:N512],
                AF.Sigmoid), inc=s_ex)
        SW = BIGT.ap().rearrange("p (e t) -> p e t", e=E1)
        key_hist = []
        if stop == "sel":
            return
        for ht in range(2):
            toff = ht * 1024
            for e in range(E1):
                gk = len(key_hist)
                ring = gk % 2
                if gk >= 2:
                    p.wait("gpsimd", s_mm, key_hist[gk - 2])
                dma("gpsimd", KEYR[:, ring, :, :],
                    KEYS[e, :, :].rearrange("(k p) hh -> p k hh", p=128), dma_w)
                w_blk = dma_w.total
                if ht == 0 and e < 8:
                    val_chunks(1)
                    if e == 7:
                        w_val = dma_w.total
                p.wait("scalar", s_ex, s_ex.total)
                p.op("scalar", lambda E, ee=e: E.dma_start(
                    out=WROW[0:1, 0:T], in_=SELTB[ee:ee + 1, :]),
                    inc=s_ev, inc_n=16)
                sr_ok = s_ev.total
                p.wait("tensor", s_ev, sr_ok)
                p.wait("tensor", s_vv, s_vv.total)
                for n in range(2):
                    p.op("tensor", lambda E, nn=n, to=toff: E.matmul(
                        PS[6 + nn][:, 0:N512], ONESR[0:1, 0:128],
                        WROW[0:1, to + nn * 512:to + (nn + 1) * 512],
                        start=True, stop=True),
                        inc=(s_mm if n == 1 else None))
                mmv = s_mm.total
                p.wait("vector", s_mm, mmv)
                for n in range(2):
                    p.op("vector", lambda E, nn=n: E.tensor_copy(
                        SELBH[:, nn * 512:(nn + 1) * 512],
                        PS[6 + nn][:, 0:N512]), inc=(s_vv if n == 1 else None))
                selb_ok = s_vv.total
                p.wait("tensor", dma_w, w_blk)
                for n in range(2):
                    g = e * 2 + n
                    bank = g % 2
                    if s_ex.total >= 1:
                        p.wait("tensor", s_ex, s_ex.total - 1)
                    for k in range(NK):
                        p.op("tensor", lambda E, kk=k, rr=ring, nn=n, b=bank,
                             to=toff: E.matmul(
                                 PS[b][:, 0:N512], KEYR[:, rr, kk, :],
                                 XNT[:, kk, to + nn * 512:to + (nn + 1) * 512],
                                 start=(kk == 0), stop=(kk == NK - 1)),
                             inc=(s_mm if k == NK - 1 else None))
                    mmv = s_mm.total
                    p.wait("scalar", s_mm, mmv)
                    p.op("scalar", lambda E, ee=e, nn=n, b=bank: E.activation(
                        SW[:, ee, nn * 512:(nn + 1) * 512], PS[b][:, 0:N512],
                        AF.Relu), inc=s_ex)
                    p.wait("vector", s_ex, s_ex.total)
                    p.wait("vector", s_vv, selb_ok)
                    p.op("vector", lambda E, ee=e, nn=n: E.tensor_tensor(
                        SW[:, ee, nn * 512:(nn + 1) * 512],
                        SW[:, ee, nn * 512:(nn + 1) * 512],
                        SELBH[:, nn * 512:(nn + 1) * 512], ALU.mult),
                        inc=s_vv)
                key_hist.append(s_mm.total)
            if stop == "keysh0":
                return
            p.wait("tensor", s_vv, s_vv.total)
            p.wait("tensor", dma_w, w_val)
            for tl in range(8):
                t = ht * 8 + tl
                b0 = 2 + 2 * (tl % 2)
                for n in range(2):
                    for e in range(E1):
                        p.op("tensor", lambda E, ee=e, nn=n, tt=tl, b=b0:
                             E.matmul(
                                 PS[b + nn][:, 0:N512],
                                 SW[:, ee, tt * 128:(tt + 1) * 128],
                                 VALB[:, ee, nn * 512:(nn + 1) * 512],
                                 start=(ee == 0), stop=(ee == E1 - 1)),
                             inc=(s_mm if (e == E1 - 1 and n == 1) else None))
                mmv = s_mm.total
                p.wait("vector", s_mm, mmv)
                for n in range(2):
                    p.op("vector", lambda E, tt=t, nn=n, b=b0: E.tensor_tensor(
                        X[:, tt, nn * 512:(nn + 1) * 512],
                        X[:, tt, nn * 512:(nn + 1) * 512],
                        PS[b + nn][:, 0:N512], ALU.add),
                        inc=(s_fb if n == 1 else None))
                fbv = s_fb.total
                if tl % 2 == 1:
                    p.wait("tensor", s_fb, fbv)
                if stop == "valh0" and tl == 7:
                    return
                p.wait("sync", s_fb, fbv)
                p.op("sync", lambda E, tt=t: E.dma_start(
                    out=OUT.rearrange("(n p) d -> p n d", p=128)[:, tt, :],
                    in_=X[:, tt, :]), inc=s_st, inc_n=16)
        return

    L0_STOPS = ("mamba", "rw1", "logits", "top2", "wbc", "ew1h0", "e1",
                "ew2dma", "wrow")
    if stop not in L0_STOPS:
        l1_layer()
    L1_DONE = s_st.total

    if stop != "":
        p.wait("sync", s_fb, s_fb.total)
        p.wait("sync", s_gp, s_gp.total)
        p.wait("sync", s_vv, s_vv.total)
        p.wait("sync", s_ex, s_ex.total)
        p.op("sync", lambda E: E.dma_start(
            out=OUT.rearrange("(n p) d -> p n d", p=128), in_=X.ap()),
            inc=s_st, inc_n=16)
    p.wait("sync", s_st, s_st.total)

    import contextlib
    with contextlib.ExitStack() as stack:
        for s in p.sems.values():
            s.h = stack.enter_context(nc.semaphore(s.name))
        block = stack.enter_context(nc.Block())
        for name in Plan.ENGINES:
            getattr(block, name)(
                (lambda nm: lambda E: [fn(E) for fn in p.items[nm]])(name))
    return nc, tap_outs


def _prep_inputs(inputs, core):
    g = inputs
    d = {}
    d["x"] = np.ascontiguousarray(g["L0_attn_x"][core])
    d["idm"] = np.eye(128, dtype=np.float32)
    d["jm"] = np.eye(128, dtype=np.float32)[::-1].copy()
    d["win"] = g["L0_attn_W_in"]
    d["binT"] = np.ascontiguousarray(g["L0_attn_b_in"].reshape(1, DI))
    d["convw"] = np.ascontiguousarray(
        g["L0_attn_conv_w"][:, 0, :].reshape(NM, 128, 4).transpose(1, 0, 2))
    d["convb"] = np.ascontiguousarray(g["L0_attn_conv_b"].reshape(NM, 128).T)
    d["amat"] = g["L0_attn_A"]
    d["bp"] = g["L0_attn_Bp"].reshape(DS, 1)
    d["cp"] = g["L0_attn_Cp"].reshape(DS, 1)
    d["dp"] = g["L0_attn_Dp"].reshape(1, 1)
    d["wout"] = g["L0_attn_W_out"]
    d["boutr"] = g["L0_attn_b_out"].reshape(1, D)
    d["rw1"] = g["L0_ffn_rw1"]
    d["rb1c"] = np.ascontiguousarray(g["L0_ffn_rb1"].reshape(NK, 128).T)
    d["rw2"] = g["L0_ffn_rw2"]
    d["rb2c"] = g["L0_ffn_rb2"].reshape(E0, 1)
    d["ew1"] = g["L0_ffn_ew1"]
    d["eb1c"] = np.ascontiguousarray(
        g["L0_ffn_eb1"].reshape(E0, NM, 128).transpose(2, 0, 1))
    d["ew2"] = g["L0_ffn_ew2"]
    d["eb2r"] = g["L0_ffn_eb2"]
    d["wdq"] = g["L1_attn_W_DQ"]
    d["bdqc"] = g["L1_attn_b_DQ"].reshape(DC, 1)
    d["wuq"] = np.vstack([g["L1_attn_W_UQ"], g["L1_attn_b_UQ"][None]])
    d["wdkv"] = g["L1_attn_W_DKV"]
    d["bdkvc"] = g["L1_attn_b_DKV"].reshape(DC, 1)
    d["wuk"] = np.vstack([g["L1_attn_W_UK"], g["L1_attn_b_UK"][None]])
    d["wuv"] = np.vstack([g["L1_attn_W_UV"], g["L1_attn_b_UV"][None]])
    d["wqr"] = np.vstack([g["L1_attn_W_QR"], g["L1_attn_b_QR"][None]])
    d["wkr"] = g["L1_attn_W_KR"]
    d["bkrc"] = g["L1_attn_b_KR"].reshape(DHR, 1)
    d["wo"] = g["L1_attn_W_O"]
    d["bor"] = g["L1_attn_b_O"].reshape(1, D)
    d["esel"] = g["L1_ffn_expert_sel"]
    d["keys"] = g["L1_ffn_keys"]
    d["values"] = g["L1_ffn_values"]
    return {k: np.ascontiguousarray(np.asarray(v).astype(np.float32, copy=False))
            for k, v in d.items()}


def kernel(**inputs):
    nc, _ = build_nc()
    in_maps = [_prep_inputs(inputs, c) for c in range(8)]
    res = run_bass_kernel_spmd(nc, in_maps, core_ids=list(range(8)))
    return np.stack([r["out"] for r in res.results], axis=0)


# version 11 (final)
# speedup vs baseline: 1.6034x; 1.0235x over previous
import os
import numpy as np
import concourse.bass as bass
import concourse.mybir as mybir
from concourse.bass_utils import run_bass_kernel_spmd

F32 = mybir.dt.float32
BF16 = mybir.dt.bfloat16
AF = mybir.ActivationFunctionType
ALU = mybir.AluOpType
AX = mybir.AxisListType

T = 2048
NT = 16
D = 1024
NK = 8
DI = 2048
NM = 16
DS = 16
CH = 128
NCH = 16
E0, H0 = 8, 2048
E1, H1 = 16, 128
NHEAD, DK, DC, DHR = 8, 128, 32, 32
SCALE = 1.0 / float(np.sqrt(DK + DHR))
LN_EPS = 1e-5
N512 = 512

ARENA_BYTES = 49152


class Sem:
    def __init__(self, name):
        self.name = name
        self.total = 0
        self.h = None

    def inc(self, n=1):
        self.total += n
        return self.total


class Plan:
    ENGINES = ("tensor", "vector", "scalar", "gpsimd", "sync")

    def __init__(self):
        self.items = {e: [] for e in self.ENGINES}
        self.sems = {}

    def sem(self, name):
        s = self.sems.get(name)
        if s is None:
            s = self.sems[name] = Sem(name)
        return s

    def emit(self, eng, fn):
        self.items[eng].append(fn)

    def wait(self, eng, sem, val=None):
        v = sem.total if val is None else val
        if v > 0:
            self.emit(eng, lambda E, s=sem, vv=v: E.wait_ge(s.h, vv))

    def op(self, eng, fn, inc=None, inc_n=1):
        if inc is not None:
            inc.inc(inc_n)
            self.emit(eng, lambda E, f=fn, s=inc, n=inc_n: f(E).then_inc(s.h, n))
        else:
            self.emit(eng, fn)


def build_nc(debug_taps=(), stop=None):
    stop = stop if stop is not None else os.environ.get("MK_STOP", "")
    nc = bass.Bass("TRN2", target_bir_lowering=False, debug=False)
    p = Plan()
    taps = set(debug_taps)

    def din(name, shape):
        return nc.dram_tensor(name, list(shape), F32, kind="ExternalInput").ap()

    X_hbm = din("x", (T, D))
    ID_hbm = din("idm", (128, 128))
    J_hbm = din("jm", (128, 128))
    WIN = din("win", (D, DI))
    BINT = din("binT", (1, DI))
    BINC = din("binc", (128, NM))
    CONVW = din("convw", (128, NM, 4))
    CONVB = din("convb", (128, NM))
    A_in = din("amat", (DS, DS))
    BP = din("bp", (DS, 1))
    CP = din("cp", (DS, 1))
    DP = din("dp", (1, 1))
    WOUT = din("wout", (DI, D))
    BOUT = din("boutr", (1, D))
    RW1 = din("rw1", (D, D))
    RB1C = din("rb1c", (128, NK))
    RW2 = din("rw2", (D, E0))
    RB2C = din("rb2c", (E0, 1))
    EW1 = din("ew1", (E0, D, H0))
    EB1C = din("eb1c", (128, E0, NM))
    EW2 = din("ew2", (E0, H0, D))
    EB2R = din("eb2r", (E0, D))
    WDQ = din("wdq", (D, DC))
    BDQC = din("bdqc", (DC, 1))
    WUQ = din("wuq", (DC + 1, NHEAD * DK))
    WDKV = din("wdkv", (D, DC))
    BDKVC = din("bdkvc", (DC, 1))
    WUK = din("wuk", (DC + 1, NHEAD * DK))
    WUV = din("wuv", (DC + 1, NHEAD * DK))
    WQR = din("wqr", (DC + 1, NHEAD * DHR))
    WKR = din("wkr", (D, DHR))
    BKRC = din("bkrc", (DHR, 1))
    WO = din("wo", (NHEAD * DK, D))
    BOR = din("bor", (1, D))
    ESEL = din("esel", (E1, D))
    KEYS = din("keys", (E1, D, H1))
    VALUES = din("values", (E1, H1, D))

    OUT = nc.dram_tensor("out", [T, D], F32, kind="ExternalOutput").ap()

    tap_outs = {}

    def tap(name, shape, dtype=F32):
        if name in taps:
            tap_outs[name] = nc.dram_tensor(
                "tap_" + name, list(shape), dtype, kind="ExternalOutput").ap()
            return tap_outs[name]
        return None

    sb = nc.alloc_sbuf_tensor

    X = sb("X", [128, NT, D], F32)
    XNT = sb("XNT", [128, NK, T], BF16)
    BIGT = sb("BIG", [128, 16384], BF16)
    ARENA = sb("ARENA", [128, ARENA_BYTES // 2], BF16)

    ARB = ARENA.ap()
    ARF = ARENA.ap().bitcast(F32)

    def arn_b(off_bytes, *shape):
        n = int(np.prod(shape))
        assert off_bytes % 2 == 0 and off_bytes + 2 * n <= ARENA_BYTES, (off_bytes, shape)
        v = ARB[:, off_bytes // 2: off_bytes // 2 + n]
        if len(shape) > 1:
            dims = " ".join(f"d{i}" for i in range(len(shape)))
            v = v.rearrange(f"p ({dims}) -> p {dims}",
                            **{f"d{i}": s for i, s in enumerate(shape)})
        return v

    def arn_f(off_bytes, *shape):
        n = int(np.prod(shape))
        assert off_bytes % 4 == 0 and off_bytes + 4 * n <= ARENA_BYTES, (off_bytes, shape)
        v = ARF[:, off_bytes // 4: off_bytes // 4 + n]
        if len(shape) > 1:
            dims = " ".join(f"d{i}" for i in range(len(shape)))
            v = v.rearrange(f"p ({dims}) -> p {dims}",
                            **{f"d{i}": s for i, s in enumerate(shape)})
        return v

    IDB = sb("IDB", [128, 128], BF16)
    IDF = sb("IDF", [128, 128], F32)
    JB = sb("JB", [128, 128], BF16)
    ONESC = sb("ONESC", [128, 1], BF16)
    ONESR = sb("ONESR", [1, N512], BF16)

    LN_SUM = sb("LN_SUM", [128, NT], F32)
    LN_SQ = sb("LN_SQ", [128, NT], F32)
    LN_MEAN = sb("LN_MEAN", [128, NT], F32)
    LN_RSTD = sb("LN_RSTD", [128, NT], F32)
    SCR8 = sb("SCR8", [128, D], F32)
    XNB = sb("XNB", [128, D], BF16)

    PQH = sb("PQH", [DS, 16 * DS], BF16)
    AMS = sb("AMS", [DS, DS], F32)
    ATS = sb("ATS", [DS, DS], F32)
    ATB = sb("ATB", [DS, DS], BF16)
    BPC = sb("BPC", [DS, 1], BF16)
    CPC = sb("CPC", [DS, 1], BF16)
    DPS = sb("DPS", [1, 1], F32)
    BT_CH = sb("BT_CH", [DS, 128], BF16)
    CT_CH = sb("CT_CH", [DS, 128], BF16)
    BTT = sb("BTT", [128, DS], BF16)
    WB_M = sb("WB_M", [128, DS], BF16)
    WCT = sb("WCT", [DS, 128], BF16)
    ATLT = sb("ATLT", [DS, DS], BF16)
    KROW = sb("KROW", [1, 128], F32)
    KROWB = sb("KROWB", [1, 128], BF16)
    TTOEP = sb("TTOEP", [128, 128], BF16)
    HT = sb("HT", [DS, 2, D], BF16)

    BINB = sb("BINB", [1, DI], BF16)
    BINCS = sb("BINCS", [128, NM], F32)
    CONVWS = sb("CONVWS", [128, NM, 4], F32)
    CONVBS = sb("CONVBS", [128, NM], F32)
    RB1S = sb("RB1S", [128, NK], F32)
    RW2B = sb("RW2B", [128, NK, E0], BF16)
    RB2S = sb("RB2S", [E0, 1], F32)
    LGTM = sb("LGTM", [128, NT, E0], F32)
    TOP_M1 = sb("TOP_M1", [128, NT], F32)
    TOP_M1N = sb("TOP_M1N", [128, NT], F32)
    TOP_M2 = sb("TOP_M2", [128, NT], F32)
    TOP_E1 = sb("TOP_E1", [128, NT, E0], F32)
    TOP_SCR = sb("TOP_SCR", [128, E0], F32)
    TOP_WS = sb("TOP_WS", [128, NT], F32)
    WTM = sb("WTM", [128, NT, E0], BF16)
    EB1S = sb("EB1S", [128, E0, NM], F32)

    TOP_MSK = sb("TOP_MSK", [128, NT, E0], F32)
    WROW = sb("WROW", [1, T], BF16)

    BDQS = sb("BDQS", [DC, 1], F32)
    BDKVS = sb("BDKVS", [DC, 1], F32)
    BKRS = sb("BKRS", [DHR, 1], F32)
    KMAX = sb("KMAX", [1, 4], F32)
    IND33 = sb("IND33", [1, 64], BF16)

    WIN_H = arn_b(0, NK, D)
    WOUT_H = arn_b(16384, NK, D)
    UC = arn_b(32768, 2, NK, 132)
    VCF = arn_b(36992, NK, 128)
    VC = arn_b(39040, 2, D)
    SYC = arn_b(43136, NK, 128)
    BOUTB = arn_b(45184, D)[0:1, :]

    PS = [nc.alloc_psum_tensor(f"PS{i}", [128, N512], F32) for i in range(8)]

    dma_x = p.sem("dma_x")
    dma_c = p.sem("dma_c")
    dma_w = p.sem("dma_w")
    s_mm = p.sem("s_mm")
    s_ev = p.sem("s_ev")
    s_vv = p.sem("s_vv")
    s_pr = p.sem("s_pr")
    s_tr = p.sem("s_tr")
    s_st = p.sem("s_st")
    s_gp = p.sem("s_gp")
    s_fb = p.sem("s_fb")
    s_ex = p.sem("s_ex")

    def dma(eng, out_ap, in_ap, sem):
        p.op(eng, lambda E, o=out_ap, i=in_ap: E.dma_start(out=o, in_=i),
             inc=sem, inc_n=16)

    dma("sync", X.ap(), X_hbm.rearrange("(n p) d -> p n d", p=128), dma_x)

    dma("sync", IDF.ap(), ID_hbm, dma_c)
    dma("gpsimd", IDB.ap(), ID_hbm, dma_c)
    dma("gpsimd", JB.ap(), J_hbm, dma_c)
    dma("sync", CONVWS.ap(), CONVW, dma_c)
    dma("sync", CONVBS.ap(), CONVB, dma_c)
    dma("sync", AMS.ap(), A_in, dma_c)
    dma("gpsimd", BPC.ap(), BP, dma_c)
    dma("gpsimd", CPC.ap(), CP, dma_c)
    dma("sync", DPS.ap(), DP, dma_c)
    dma("sync", BINCS.ap(), BINC, dma_c)
    dma("gpsimd", BOUTB, BOUT, dma_c)
    dma("sync", RB1S.ap(), RB1C, dma_c)
    dma("gpsimd", RW2B.ap(), RW2.rearrange("(k p) e -> p k e", p=128), dma_c)
    dma("sync", RB2S.ap(), RB2C, dma_c)
    dma("sync", EB1S.ap(), EB1C, dma_c)
    dma("sync", BDQS.ap(), BDQC, dma_c)
    dma("sync", BDKVS.ap(), BDKVC, dma_c)
    dma("sync", BKRS.ap(), BKRC, dma_c)
    CONST_N = dma_c.total

    p.emit("vector", lambda E: E.memset(ONESC.ap(), 1.0))
    p.emit("vector", lambda E: E.memset(ONESR.ap(), 1.0))
    p.emit("vector", lambda E: E.memset(TTOEP.ap(), 0.0))
    p.op("vector", lambda E: E.memset(HT.ap(), 0.0), inc=s_vv)
    V_MEMSET = s_vv.total

    def ssm_prep():
        p.wait("scalar", dma_c, CONST_N)
        p.op("scalar", lambda E: E.activation(ATS.ap(), AMS.ap(), AF.Tanh),
             inc=s_pr)
        p.wait("vector", s_pr, s_pr.total)
        p.op("vector", lambda E: E.tensor_copy(ATB.ap(), ATS.ap()))
        p.op("vector", lambda E: E.tensor_copy(PQH[0:DS, 0:DS], ATS.ap()),
             inc=s_pr)
        p.wait("tensor", s_pr, s_pr.total)
        p.wait("tensor", dma_c, CONST_N)
        p.op("tensor", lambda E: E.transpose(PS[0].ap().bitcast(BF16)[0:DS, 0:DS],
                                             ATB.ap(), IDB[0:DS, 0:DS]), inc=s_mm)
        p.wait("vector", s_mm, s_mm.total)
        p.op("vector", lambda E: E.tensor_copy(
            PQH[0:DS, DS:2 * DS], PS[0].ap().bitcast(BF16)[0:DS, 0:DS]), inc=s_pr)

        def slotP(j):
            return PQH[0:DS, (2 * j) * DS:(2 * j + 1) * DS]

        def slotQ(j):
            return PQH[0:DS, (2 * j + 1) * DS:(2 * j + 2) * DS]

        for j in range(7):
            p.wait("tensor", s_pr, s_pr.total)
            p.op("tensor", lambda E, a=slotQ(j), b=slotP(j): E.matmul(
                PS[1][0:DS, 0:DS], a, b, start=True, stop=True))
            p.op("tensor", lambda E, a=slotP(j), b=slotQ(j): E.matmul(
                PS[1][0:DS, DS:2 * DS], a, b, start=True,
                stop=True), inc=s_mm)
            p.wait("vector", s_mm, s_mm.total)
            p.op("vector", lambda E, d=slotP(j + 1): E.tensor_copy(
                d, PS[1][0:DS, 0:DS]))
            p.op("vector", lambda E, d=slotQ(j + 1): E.tensor_copy(
                d, PS[1][0:DS, DS:2 * DS]), inc=s_pr)
        P128, Q128 = slotP(7), slotQ(7)
        p.wait("vector", s_pr, s_pr.total)
        p.op("vector", lambda E: E.tensor_copy(ATLT.ap(), Q128))
        p.op("vector", lambda E: E.tensor_copy(BT_CH[0:DS, 0:1], BPC.ap()))
        p.op("vector", lambda E: E.tensor_copy(CT_CH[0:DS, 0:1], CPC.ap()),
             inc=s_pr)
        for j in range(7):
            m = 1 << j
            p.wait("tensor", s_pr, s_pr.total)
            p.op("tensor", lambda E, q=slotQ(j), mm=m: E.matmul(
                PS[1][0:DS, 0:mm], q, BT_CH[0:DS, 0:mm],
                start=True, stop=True))
            p.op("tensor", lambda E, pp=slotP(j), mm=m: E.matmul(
                PS[1][0:DS, 128:128 + mm], pp, CT_CH[0:DS, 0:mm],
                start=True, stop=True), inc=s_mm)
            p.wait("vector", s_mm, s_mm.total)
            p.op("vector", lambda E, mm=m: E.tensor_copy(
                BT_CH[0:DS, mm:2 * mm], PS[1][0:DS, 0:mm]))
            p.op("vector", lambda E, mm=m: E.tensor_copy(
                CT_CH[0:DS, mm:2 * mm], PS[1][0:DS, 128:128 + mm]),
                inc=s_pr)

        p.wait("tensor", s_pr, s_pr.total)
        p.op("tensor", lambda E: E.matmul(PS[1][0:1, 0:128], BPC.ap(),
                                          CT_CH[0:DS, 0:128], start=True,
                                          stop=True), inc=s_mm)
        p.wait("vector", s_mm, s_mm.total)
        p.op("vector", lambda E: E.tensor_copy(KROW.ap(), PS[1][0:1, 0:128]),
             inc=s_vv)
        p.wait("scalar", s_vv, s_vv.total)
        p.op("scalar", lambda E: E.add(KROW[0:1, 0:1], KROW[0:1, 0:1],
                                       DPS.ap()), inc=s_ev)
        p.wait("vector", s_ev, s_ev.total)
        p.op("vector", lambda E: E.tensor_copy(KROWB.ap(), KROW.ap()),
             inc=s_pr)
        p.wait("scalar", s_pr, s_pr.total)
        p.wait("scalar", s_vv, V_MEMSET)
        for s in range(128):
            p.op("scalar",
                 lambda E, ss=s: E.dma_start(out=TTOEP[ss:ss + 1, ss:128],
                                             in_=KROWB[0:1, 0:128 - ss]),
                 inc=s_pr, inc_n=16)
        t_ready = s_pr.total

        p.emit("vector", lambda E: E.drain())
        p.op("vector", lambda E: E.tensor_copy(WCT[0:DS, 0:127],
                                               CT_CH[0:DS, 1:128]), inc=s_pr)
        p.wait("tensor", s_pr, s_pr.total)
        p.op("tensor", lambda E: E.matmul(PS[1][0:DS, 0:1],
                                          P128, CPC.ap(), start=True,
                                          stop=True), inc=s_mm)
        p.wait("vector", s_mm, s_mm.total)
        p.op("vector", lambda E: E.tensor_copy(
            WCT[0:DS, 127:128], PS[1][0:DS, 0:1]), inc=s_pr)
        p.wait("tensor", s_pr, s_pr.total)
        p.op("tensor", lambda E: E.transpose(PS[2].ap().bitcast(BF16)[:, 0:DS],
                                             BT_CH[0:DS, 0:128],
                                             IDB[0:DS, 0:DS]), inc=s_mm)
        p.wait("vector", s_mm, s_mm.total)
        p.op("vector", lambda E: E.tensor_copy(
            BTT.ap(), PS[2].ap().bitcast(BF16)[:, 0:DS]), inc=s_pr)
        p.wait("tensor", s_pr, s_pr.total)
        p.op("tensor", lambda E: E.matmul(PS[2][:, 0:DS],
                                          JB.ap(), BTT.ap(), start=True,
                                          stop=True), inc=s_mm)
        p.wait("vector", s_mm, s_mm.total)
        p.op("vector", lambda E: E.tensor_copy(
            WB_M.ap(), PS[2][:, 0:DS]), inc=s_pr)
        return t_ready

    T_READY = ssm_prep()
    PREP_DONE = s_pr.total

    def tap_store(name, sbuf_ap, shape, dtype=F32, wait=()):
        t_ap = tap(name, shape, dtype)
        if t_ap is None:
            return
        for sem, val in wait:
            p.wait("sync", sem, val)
        p.op("sync", lambda E, o=t_ap, i=sbuf_ap: E.dma_start(out=o, in_=i),
             inc=s_st, inc_n=16)

    tap_store("ttoep", TTOEP.ap(), (128, 128), BF16, wait=[(s_pr, T_READY)])
    tap_store("wb", WB_M.ap(), (128, DS), BF16, wait=[(s_pr, PREP_DONE)])
    tap_store("wct", WCT.ap(), (DS, 128), BF16, wait=[(s_pr, PREP_DONE)])
    tap_store("atlt", ATLT.ap(), (DS, DS), BF16, wait=[(s_pr, PREP_DONE)])
    tap_store("krow", KROW.ap(), (1, 128), F32, wait=[(s_pr, PREP_DONE)])

    def layernorm(add_xn_to_x):
        p.wait("scalar", dma_x, 16)
        p.wait("vector", dma_x, 16)
        p.op("vector", lambda E: E.memset(LN_SUM.ap(), 0.0))
        p.op("vector", lambda E: E.memset(LN_MEAN.ap(), 0.0), inc=s_vv)
        p.op("vector", lambda E: E.memset(LN_SQ.ap(), 0.0), inc=s_vv)
        p.wait("scalar", s_vv, s_vv.total)
        for t in range(NT):
            p.op("scalar", lambda E, tt=t: E.activation(
                SCR8.ap(), X[:, tt, :], AF.Square,
                accum_out=LN_SQ[:, tt:tt + 1]), inc=s_ev)
            p.op("vector", lambda E, tt=t: E.tensor_scalar(
                XNB.ap().bitcast(F32)[:, 0:512], X[:, tt, 0:512], 1.0, 0.0,
                ALU.mult, ALU.add, accum_out=LN_SUM[:, tt:tt + 1]), inc=s_vv)
            p.op("vector", lambda E, tt=t: E.tensor_scalar(
                XNB.ap().bitcast(F32)[:, 0:512], X[:, tt, 512:1024], 1.0, 0.0,
                ALU.mult, ALU.add, accum_out=LN_MEAN[:, tt:tt + 1]), inc=s_vv)
        p.wait("vector", s_ev, s_ev.total)
        p.op("vector", lambda E: E.tensor_tensor(
            LN_SUM.ap(), LN_SUM.ap(), LN_MEAN.ap(), ALU.add))
        p.emit("vector", lambda E: E.drain())
        p.op("vector", lambda E: E.tensor_scalar(
            LN_MEAN.ap(), LN_SUM.ap(), 1.0 / D, 0.0, ALU.mult, ALU.add))
        p.op("vector", lambda E: E.tensor_scalar(
            LN_SQ.ap(), LN_SQ.ap(), 1.0 / D, 0.0, ALU.mult, ALU.add))
        p.emit("vector", lambda E: E.drain())
        p.op("vector", lambda E: E.tensor_tensor(
            LN_SUM.ap(), LN_MEAN.ap(), LN_MEAN.ap(), ALU.mult))
        p.emit("vector", lambda E: E.drain())
        p.op("vector", lambda E: E.tensor_tensor(
            LN_SQ.ap(), LN_SQ.ap(), LN_SUM.ap(), ALU.subtract))
        p.emit("vector", lambda E: E.drain())
        p.op("vector", lambda E: E.tensor_scalar(
            LN_SQ.ap(), LN_SQ.ap(), LN_EPS, 0.0, ALU.add, ALU.add))
        p.emit("vector", lambda E: E.drain())
        p.op("vector", lambda E: E.reciprocal(LN_SUM.ap(), LN_SQ.ap()),
             inc=s_vv)
        p.wait("scalar", s_vv, s_vv.total)
        p.op("scalar", lambda E: E.activation(LN_RSTD.ap(), LN_SUM.ap(),
                                              AF.Sqrt), inc=s_ev)
        p.wait("vector", s_ev, s_ev.total)
        for t in range(NT):
            if t >= 1:
                p.wait("vector", s_gp, s_gp.total)
                p.wait("vector", s_tr, s_tr.total)
            p.op("vector", lambda E, tt=t: E.tensor_scalar(
                SCR8.ap(), X[:, tt, :], LN_MEAN[:, tt:tt + 1],
                LN_RSTD[:, tt:tt + 1], ALU.subtract, ALU.mult))
            p.op("vector", lambda E: E.tensor_copy(XNB.ap(), SCR8.ap()),
                 inc=s_vv)
            if add_xn_to_x:
                p.wait("gpsimd", s_vv, s_vv.total)
                p.op("gpsimd", lambda E, tt=t: E.tensor_tensor(
                    X[:, tt, :], X[:, tt, :], SCR8.ap(), ALU.add), inc=s_gp)
            else:
                s_gp.inc(0)
            p.wait("tensor", s_vv, s_vv.total)
            for k in range(NK):
                bank = 6 + (k // 4)
                off = (k % 4) * 128
                p.op("tensor", lambda E, kk=k, b=bank, o=off: E.transpose(
                    PS[b].ap().bitcast(BF16)[:, o:o + 128],
                    XNB[:, kk * 128:(kk + 1) * 128], IDB.ap()),
                    inc=(s_mm if k in (3, 7) else None))
            mmv = s_mm.total
            p.wait("vector", s_mm, mmv - 1)
            for k in range(4):
                p.op("vector", lambda E, tt=t, kk=k: E.tensor_copy(
                    XNT[:, kk, tt * 128:(tt + 1) * 128],
                    PS[6].ap().bitcast(BF16)[:, kk * 128:(kk + 1) * 128]))
            p.wait("vector", s_mm, mmv)
            for k in range(4):
                p.op("vector", lambda E, tt=t, kk=k: E.tensor_copy(
                    XNT[:, 4 + kk, tt * 128:(tt + 1) * 128],
                    PS[7].ap().bitcast(BF16)[:, kk * 128:(kk + 1) * 128]),
                    inc=(s_tr if k == 3 else None))
        return s_tr.total, s_gp.total

    XNT_L0, GP_L0 = layernorm(add_xn_to_x=True)
    tap_store("ln_mean", LN_MEAN.ap(), (128, NT), F32, wait=[(s_tr, s_tr.total)])
    tap_store("ln_rstd", LN_RSTD.ap(), (128, NT), F32, wait=[(s_tr, s_tr.total)])
    tap_store("ln_sum", LN_SUM.ap(), (128, NT), F32, wait=[(s_tr, s_tr.total)])
    tap_store("ln_sq", LN_SQ.ap(), (128, NT), F32, wait=[(s_tr, s_tr.total)])
    tap_store("x0", X[:, 0, :], (128, D), F32, wait=[(dma_x, 16)])
    tap_store("xnt", XNT.ap(), (128, NK, T), BF16, wait=[(s_tr, XNT_L0)])


    def stream_mamba_weights(half):
        dma("gpsimd", WIN_H,
            WIN[:, half * 1024:(half + 1) * 1024].rearrange(
                "(k p) m -> p k m", p=128), dma_w)
        dma("gpsimd", WOUT_H,
            WOUT[half * 1024:(half + 1) * 1024, :].rearrange(
                "(k p) m -> p k m", p=128), dma_w)
        return dma_w.total

    tap_xu = tap("xu", (128, NCH, NK, 132), BF16)
    tap_vc = tap("vc", (128, NCH, D), BF16)
    tap_sy = tap("sy", (128, NCH, NK, 128), BF16)

    def mamba_half(half):
        p.wait("gpsimd", s_fb, s_fb.total)
        wv = stream_mamba_weights(half)
        p.wait("vector", s_fb, s_fb.total)
        p.op("vector", lambda E: E.memset(HT[0:DS, 0, :], 0.0))
        p.op("vector", lambda E: E.memset(UC[:, 0, :, 0:4], 0.0))
        p.op("vector", lambda E: E.memset(UC[:, 1, :, 0:4], 0.0), inc=s_vv)
        zs = s_vv.total

        p.wait("tensor", dma_w, wv - 16)
        p.wait("tensor", s_tr, XNT_L0)
        p.wait("tensor", s_pr, PREP_DONE)
        p.wait("tensor", s_vv, zs)
        p.wait("scalar", s_vv, zs)

        cv_done = {}
        for c in range(NCH):
            cur, prv = c % 2, 1 - c % 2
            for mg in range(2):
                bank = mg
                for mi in range(4):
                    m = mg * 4 + mi
                    for k in range(NK):
                        p.op("tensor", lambda E, kk=k, mm=m, b=bank, o=mi * 128, cc=c:
                             E.matmul(PS[b][:, o:o + 128],
                                      WIN_H[:, kk, mm * 128:(mm + 1) * 128],
                                      XNT[:, kk, cc * 128:(cc + 1) * 128],
                                      start=(kk == 0), stop=(kk == NK - 1)),
                             inc=(s_mm if (mi == 3 and k == NK - 1) else None))
            mm_inp = s_mm.total
            if c >= 2:
                p.wait("scalar", s_vv, cv_done[c - 2])
            p.wait("scalar", s_mm, mm_inp - 1)
            for mi in range(4):
                p.op("scalar", lambda E, cu=cur, m_=mi, g=half * NK + mi:
                     E.activation(UC[:, cu, m_, 4:132],
                                  PS[0][:, m_ * 128:(m_ + 1) * 128],
                                  AF.Identity, bias=BINCS[:, g:g + 1]))
            p.wait("scalar", s_mm, mm_inp)
            for mi in range(4):
                p.op("scalar", lambda E, cu=cur, m_=mi, g=half * NK + 4 + mi:
                     E.activation(UC[:, cu, 4 + m_, 4:132],
                                  PS[1][:, m_ * 128:(m_ + 1) * 128],
                                  AF.Identity, bias=BINCS[:, g:g + 1]),
                     inc=(s_ev if mi == 3 else None))
            ev_u = s_ev.total
            p.wait("vector", s_ev, ev_u)
            if c > 0:
                p.op("vector", lambda E, cu=cur, pv=prv: E.tensor_copy(
                    UC[:, cu, :, 0:4], UC[:, pv, :, 128:132]))
            for m in range(NK):
                g = half * NK + m
                p.op("vector", lambda E, mm=m, gg=g, cu=cur: E.tensor_scalar(
                    VCF[:, mm, :], UC[:, cu, mm, 1:129], CONVWS[:, gg, 0:1],
                    CONVBS[:, gg:gg + 1], ALU.mult, ALU.add))
            for k in range(1, 4):
                for m in range(NK):
                    g = half * NK + m
                    p.op("vector",
                         lambda E, mm=m, gg=g, kk=k, cu=cur: E.scalar_tensor_tensor(
                             VCF[:, mm, :], UC[:, cu, mm, 1 + kk:129 + kk],
                             CONVWS[:, gg, kk:kk + 1], VCF[:, mm, :],
                             ALU.mult, ALU.add),
                         inc=(s_vv if (m == NK - 1 and k == 3) else None))
            cv_done[c] = s_vv.total
            if tap_xu is not None and half == 0:
                p.wait("sync", s_vv, cv_done[c])
                p.op("sync", lambda E, cc=c, cu=cur: E.dma_start(
                    out=tap_xu[:, cc, :, :], in_=UC[:, cu, :, :]),
                    inc=s_st, inc_n=16)
            p.wait("tensor", s_vv, cv_done[c])
            for m in range(NK):
                bank = 2 + (m // 4)
                off = (m % 4) * 128
                p.op("tensor", lambda E, mm=m, b=bank, o=off: E.transpose(
                    PS[b].ap().bitcast(BF16)[:, o:o + 128],
                    VCF[:, mm, :], IDB.ap()),
                    inc=(s_mm if m in (3, 7) else None))
            mm_tr = s_mm.total
            p.wait("vector", s_mm, mm_tr - 1)
            p.op("vector", lambda E, cu=cur: E.tensor_copy(
                VC[:, cu, 0:512], PS[2].ap().bitcast(BF16)[:, 0:512]))
            p.wait("vector", s_mm, mm_tr)
            p.op("vector", lambda E, cu=cur: E.tensor_copy(
                VC[:, cu, 512:1024], PS[3].ap().bitcast(BF16)[:, 0:512]),
                inc=s_vv)
            vc_ok = s_vv.total
            if tap_vc is not None and half == 0:
                p.wait("sync", s_vv, vc_ok)
                p.op("sync", lambda E, cc=c, cu=cur: E.dma_start(
                    out=tap_vc[:, cc, :], in_=VC[:, cu, :]), inc=s_st, inc_n=16)
            p.wait("tensor", s_vv, vc_ok)
            hcur, hnxt = c % 2, 1 - c % 2
            for m in range(NK):
                bank = 4 + (m // 4)
                off = (m % 4) * 128
                p.op("tensor", lambda E, mm=m, b=bank, o=off, cu=cur: E.matmul(
                    PS[b][:, o:o + 128],
                    VC[:, cu, mm * 128:(mm + 1) * 128], TTOEP.ap(),
                    start=True, stop=False))
                p.op("tensor", lambda E, mm=m, b=bank, o=off, hc=hcur: E.matmul(
                    PS[b][:, o:o + 128],
                    HT[0:DS, hc, mm * 128:(mm + 1) * 128], WCT.ap(),
                    start=False, stop=True),
                    inc=(s_mm if m in (3, 7) else None))
            mm_y = s_mm.total
            for n in range(2):
                p.op("tensor", lambda E, nn=n, hc=hcur: E.matmul(
                    PS[6][0:DS, 0:N512], ATLT.ap(),
                    HT[0:DS, hc, nn * 512:(nn + 1) * 512],
                    start=True, stop=False))
                p.op("tensor", lambda E, nn=n, cu=cur: E.matmul(
                    PS[6][0:DS, 0:N512], WB_M.ap(),
                    VC[:, cu, nn * 512:(nn + 1) * 512],
                    start=False, stop=True), inc=s_mm)
                mm_st = s_mm.total
                p.wait("vector", s_mm, mm_st)
                p.op("vector", lambda E, nn=n, hn=hnxt: E.tensor_copy(
                    HT[0:DS, hn, nn * 512:(nn + 1) * 512],
                    PS[6][0:DS, 0:N512]), inc=s_vv)
                p.wait("tensor", s_vv, s_vv.total)
            p.wait("scalar", s_mm, mm_y - 1)
            p.op("scalar", lambda E: E.activation(
                SYC[:, 0:4, :],
                PS[4].ap().rearrange("p (m n) -> p m n", n=128), AF.Silu))
            p.wait("scalar", s_mm, mm_y)
            p.op("scalar", lambda E: E.activation(
                SYC[:, 4:8, :],
                PS[5].ap().rearrange("p (m n) -> p m n", n=128), AF.Silu),
                inc=s_ev)
            sy_ok = s_ev.total
            if tap_sy is not None and half == 0:
                p.wait("sync", s_ev, sy_ok)
                p.op("sync", lambda E, cc=c: E.dma_start(
                    out=tap_sy[:, cc, :, :], in_=SYC), inc=s_st, inc_n=16)
            p.wait("tensor", s_ev, sy_ok)
            if c == 0:
                p.wait("tensor", dma_w, wv)
            for n in range(2):
                for m in range(NK):
                    last = (m == NK - 1) and (half == 1)
                    p.op("tensor", lambda E, mm=m, nn=n, st_=last: E.matmul(
                        PS[7][:, 0:N512],
                        SYC[:, mm, :], WOUT_H[:, mm, nn * 512:(nn + 1) * 512],
                        start=(mm == 0), stop=st_),
                        inc=(s_mm if last else None))
                if half == 0:
                    p.op("tensor", lambda E, nn=n: E.matmul(
                        PS[7][:, 0:N512], ONESR[0:1, 0:128],
                        BOUTB[0:1, nn * 512:(nn + 1) * 512],
                        start=False, stop=True), inc=s_mm)
                mm_o = s_mm.total
                p.wait("vector", s_mm, mm_o)
                p.op("vector", lambda E, cc=c, nn=n: E.tensor_tensor(
                    X[:, cc, nn * 512:(nn + 1) * 512],
                    X[:, cc, nn * 512:(nn + 1) * 512],
                    PS[7][:, 0:N512], ALU.add), inc=s_fb)
                p.wait("tensor", s_fb, s_fb.total)
        return

    p.wait("vector", s_gp, GP_L0)
    mamba_half(0)
    mamba_half(1)
    MAMBA_DONE_V = s_fb.total

    tap_store("x_after_mamba", X.ap(), (128, NT, D), F32,
              wait=[(s_fb, MAMBA_DONE_V)])

    RW1B = arn_b(0, NK, D)
    LGT = arn_f(16384, T)[0:E0, :]
    EW2RES = arn_b(0, NM, D)
    WBC = arn_b(32768, T)
    EW1R = arn_b(36864, 2, NK, 256)
    WTR = arn_b(45056, T)[0:E0, :]
    EB2RB = arn_b(16384, D)[0:E0, :]

    R1T = BIGT.ap().rearrange("p (m t) -> p m t", m=NK)
    EHW = BIGT.ap().rearrange("p (m t) -> p m t", m=NM)

    def moe_l0(stop=""):
        p.wait("gpsimd", s_fb, s_fb.total)
        dma("gpsimd", RW1B, RW1.rearrange("(k p) m -> p k m", p=128), dma_w)
        w_rw1 = dma_w.total
        p.wait("tensor", dma_w, w_rw1)
        p.wait("tensor", s_fb, s_fb.total)
        p.wait("tensor", s_ev, s_ev.total)
        p.wait("tensor", s_vv, s_vv.total)
        for m8 in range(NK):
            for n in range(4):
                bank = (m8 * 4 + n) % 2
                for k in range(NK):
                    p.op("tensor", lambda E, kk=k, mm=m8, nn=n, b=bank: E.matmul(
                        PS[b][:, 0:N512], RW1B[:, kk, mm * 128:(mm + 1) * 128],
                        XNT[:, kk, nn * 512:(nn + 1) * 512],
                        start=(kk == 0), stop=(kk == NK - 1)),
                        inc=(s_mm if k == NK - 1 else None))
                mmv = s_mm.total
                p.wait("scalar", s_mm, mmv)
                p.op("scalar", lambda E, mm=m8, nn=n, b=bank: E.activation(
                    R1T[:, mm, nn * 512:(nn + 1) * 512], PS[b][:, 0:N512],
                    AF.Silu, bias=RB1S[:, mm:mm + 1]), inc=s_ev)
                p.wait("tensor", s_ev, s_ev.total)
        if stop == "rw1":
            return
        for n in range(4):
            for k in range(NK):
                p.op("tensor", lambda E, kk=k, nn=n: E.matmul(
                    PS[2][0:E0, 0:N512], RW2B[:, kk, :],
                    R1T[:, kk, nn * 512:(nn + 1) * 512],
                    start=(kk == 0), stop=(kk == NK - 1)),
                    inc=(s_mm if k == NK - 1 else None))
            mmv = s_mm.total
            p.wait("scalar", s_mm, mmv)
            p.op("scalar", lambda E, nn=n: E.activation(
                LGT[0:E0, nn * 512:(nn + 1) * 512], PS[2][0:E0, 0:N512],
                AF.Identity, bias=RB2S.ap()), inc=s_ev)
            p.wait("tensor", s_ev, s_ev.total)
        for t in range(NT):
            bank = t % 2
            p.wait("tensor", s_vv, s_vv.total)
            p.op("tensor", lambda E, tt=t, b=bank: E.transpose(
                PS[b][:, 0:E0], LGT[0:E0, tt * 128:(tt + 1) * 128],
                IDF[0:E0, 0:E0]), inc=s_mm)
            mmv = s_mm.total
            p.wait("vector", s_mm, mmv)
            p.op("vector", lambda E, tt=t, b=bank: E.tensor_copy(
                LGTM[:, tt, :], PS[b][:, 0:E0]), inc=s_vv)
        LG_OK = s_vv.total
        if "lgtm" in taps:
            tap_store("lgtm", LGTM.ap(), (128, NT, E0), F32, wait=[(s_vv, LG_OK)])
        if stop == "logits":
            return
        p.op("vector", lambda E: E.reduce_max(
            TOP_M1.ap(), LGTM.ap(), axis=AX.X))
        p.emit("vector", lambda E: E.drain())
        p.op("vector", lambda E: E.tensor_scalar(
            TOP_M1N.ap(), TOP_M1.ap(), -1.0, 0.0, ALU.mult, ALU.add),
            inc=s_vv)
        for t in range(NT):
            p.op("vector", lambda E, tt=t: E.tensor_scalar(
                TOP_MSK[:, tt, :], LGTM[:, tt, :], TOP_M1[:, tt:tt + 1], None,
                ALU.is_ge))
        for t in range(NT):
            p.op("vector", lambda E, tt=t: E.scalar_tensor_tensor(
                TOP_MSK[:, tt, :], TOP_MSK[:, tt, :], -1e30, LGTM[:, tt, :],
                ALU.mult, ALU.add))
        for t in range(NT):
            p.op("vector", lambda E, tt=t: E.reduce_max(
                TOP_M2[:, tt:tt + 1], TOP_MSK[:, tt, :], axis=AX.X),
                inc=(s_vv if t == NT - 1 else None))
        M2_OK = s_vv.total
        p.wait("scalar", s_vv, M2_OK)
        for t in range(NT):
            p.op("scalar", lambda E, tt=t: E.activation(
                TOP_E1[:, tt, :], LGTM[:, tt, :], AF.Exp,
                bias=TOP_M1N[:, tt:tt + 1]), inc=(s_ev if t == NT - 1 else None))
        E1_OK = s_ev.total
        for t in range(NT):
            p.op("vector", lambda E, tt=t: E.tensor_scalar(
                TOP_MSK[:, tt, :], LGTM[:, tt, :], TOP_M2[:, tt:tt + 1], None,
                ALU.is_ge))
        p.wait("vector", s_ev, E1_OK)
        for t in range(NT):
            p.op("vector", lambda E, tt=t: E.tensor_tensor(
                TOP_E1[:, tt, :], TOP_E1[:, tt, :], TOP_MSK[:, tt, :],
                ALU.mult))
        for t in range(NT):
            p.op("vector", lambda E, tt=t: E.reduce_sum(
                TOP_WS[:, tt:tt + 1], TOP_E1[:, tt, :], axis=AX.X))
        p.emit("vector", lambda E: E.drain())
        p.op("vector", lambda E: E.reciprocal(TOP_M1.ap(), TOP_WS.ap()))
        p.emit("vector", lambda E: E.drain())
        for t in range(NT):
            p.op("vector", lambda E, tt=t: E.tensor_scalar(
                WTM[:, tt, :], TOP_E1[:, tt, :], TOP_M1[:, tt:tt + 1], None,
                ALU.mult), inc=(s_vv if t == NT - 1 else None))
        W_OK = s_vv.total
        p.wait("tensor", s_vv, W_OK)
        for t in range(NT):
            bank = t % 2
            p.wait("tensor", s_vv, s_vv.total)
            p.op("tensor", lambda E, tt=t, b=bank: E.transpose(
                PS[b].ap().bitcast(BF16)[0:E0, 0:128], WTM[:, tt, :],
                IDB.ap()), inc=s_mm)
            mmv = s_mm.total
            p.wait("vector", s_mm, mmv)
            p.op("vector", lambda E, tt=t, b=bank: E.tensor_copy(
                WTR[0:E0, tt * 128:(tt + 1) * 128],
                PS[b].ap().bitcast(BF16)[0:E0, 0:128]), inc=s_vv)
        WT_OK = s_vv.total
        if "wtm" in taps:
            tap_store("wtm", WTM.ap(), (128, NT, E0), BF16, wait=[(s_vv, W_OK)])
        if stop == "top2":
            return
        p.wait("gpsimd", s_mm, s_mm.total)
        dma("gpsimd", EB2RB, EB2R, dma_c)
        c_eb2 = dma_c.total
        p.wait("tensor", dma_c, c_eb2)
        p.wait("tensor", s_vv, WT_OK)
        for t in range(NT):
            b0 = 2 + 2 * (t % 2)
            for n in range(2):
                p.op("tensor", lambda E, tt=t, nn=n, b=b0: E.matmul(
                    PS[b + nn][:, 0:N512],
                    WTR[0:E0, tt * 128:(tt + 1) * 128],
                    EB2RB[0:E0, nn * 512:(nn + 1) * 512],
                    start=True, stop=True),
                    inc=(s_mm if n == 1 else None))
            mmv = s_mm.total
            p.wait("vector", s_mm, mmv)
            for n in range(2):
                p.op("vector", lambda E, tt=t, nn=n, b=b0: E.tensor_tensor(
                    X[:, tt, nn * 512:(nn + 1) * 512],
                    X[:, tt, nn * 512:(nn + 1) * 512],
                    PS[b + nn][:, 0:N512], ALU.add),
                    inc=(s_fb if n == 1 else None))
            if t % 2 == 1:
                p.wait("tensor", s_fb, s_fb.total)
        ew2_free_mm = s_mm.total
        ew1_hist = []
        n_exp = 1 if stop == "e1" else E0
        for e in range(n_exp):
            ew2_st = {"j": 0, "waited": False}

            def ew2_chunks(n_c):
                if not ew2_st["waited"]:
                    p.wait("gpsimd", s_mm, ew2_free_mm)
                    if e == 0:
                        p.wait("gpsimd", s_ev, s_ev.total)
                    ew2_st["waited"] = True
                for _ in range(n_c):
                    kk = ew2_st["j"]
                    dma("gpsimd", EW2RES[:, kk, :],
                        EW2[e, kk * 128:(kk + 1) * 128, :], dma_w)
                    ew2_st["j"] = kk + 1

            if stop == "ew2dma":
                ew2_chunks(NM)
                p.wait("sync", dma_w, dma_w.total)
                return
            p.wait("vector", s_vv, WT_OK)
            p.wait("vector", s_fb, s_fb.total)
            p.wait("scalar", s_vv, s_vv.total)
            p.op("scalar", lambda E, ee=e: E.dma_start(
                out=WROW.ap(), in_=WTR[ee:ee + 1, :]), inc=s_ev, inc_n=16)
            wr_ok = s_ev.total
            if stop == "wrow":
                p.wait("sync", s_ev, wr_ok)
                return
            p.wait("tensor", s_ev, wr_ok)
            for n in range(4):
                b = 2 + (n % 2)
                if n >= 2:
                    p.wait("tensor", s_vv, s_vv.total - 1)
                p.op("tensor", lambda E, nn=n, bb=b: E.matmul(
                    PS[bb][:, 0:N512], ONESR[0:1, 0:128],
                    WROW[0:1, nn * 512:(nn + 1) * 512],
                    start=True, stop=True), inc=s_mm)
                mmv = s_mm.total
                p.wait("vector", s_mm, mmv)
                p.op("vector", lambda E, nn=n, bb=b: E.tensor_copy(
                    WBC[:, nn * 512:(nn + 1) * 512], PS[bb][:, 0:N512]),
                    inc=s_vv)
            wbc_ok = s_vv.total
            if stop == "wbc":
                return
            for ht in range(2):
                for j in range(8):
                    gj = len(ew1_hist)
                    ring = gj % 2
                    if gj >= 2:
                        p.wait("gpsimd", s_mm, ew1_hist[gj - 2])
                    dma("gpsimd", EW1R[:, ring, :, :],
                        EW1[e, :, j * 256:(j + 1) * 256].rearrange(
                            "(k p) h -> p k h", p=128), dma_w)
                    w_blk = dma_w.total
                    if ht == 0 and j >= 2:
                        ew2_chunks({2: 3, 3: 3, 4: 3, 5: 3, 6: 2, 7: 2}[j])
                        if j == 7:
                            w_ew2 = dma_w.total
                    p.wait("tensor", dma_w, w_blk)
                    if j == 0:
                        p.wait("tensor", s_vv, wbc_ok)
                    for mi in range(2):
                        m = 2 * j + mi
                        for n in range(2):
                            bank = (2 * m + n) % 2
                            if s_ex.total >= 1:
                                p.wait("tensor", s_ex, s_ex.total - 1)
                            for k in range(NK):
                                p.op("tensor", lambda E, kk=k, rr=ring, nn=n,
                                     b=bank, h0=ht, mo=mi * 128: E.matmul(
                                         PS[b][:, 0:N512],
                                         EW1R[:, rr, kk, mo:mo + 128],
                                         XNT[:, kk,
                                             h0 * 1024 + nn * 512:
                                             h0 * 1024 + (nn + 1) * 512],
                                         start=(kk == 0), stop=(kk == NK - 1)),
                                     inc=(s_mm if k == NK - 1 else None))
                            mmv = s_mm.total
                            p.wait("scalar", s_mm, mmv)
                            p.op("scalar", lambda E, mm=m, nn=n, b=bank, ee=e:
                                 E.activation(
                                     EHW[:, mm, nn * 512:(nn + 1) * 512],
                                     PS[b][:, 0:N512], AF.Silu,
                                     bias=EB1S[:, ee, mm:mm + 1]), inc=s_ex)
                            p.wait("vector", s_ex, s_ex.total)
                            p.op("vector", lambda E, mm=m, nn=n, h0=ht:
                                 E.tensor_tensor(
                                     EHW[:, mm, nn * 512:(nn + 1) * 512],
                                     EHW[:, mm, nn * 512:(nn + 1) * 512],
                                     WBC[:, h0 * 1024 + nn * 512:
                                         h0 * 1024 + (nn + 1) * 512],
                                     ALU.mult), inc=s_vv)
                    ew1_hist.append(s_mm.total)
                if stop == "ew1h0":
                    return
                p.wait("tensor", dma_w, w_ew2)
                p.wait("tensor", s_vv, s_vv.total)
                for tl in range(8):
                    t = ht * 8 + tl
                    b0 = 2 + 2 * (tl % 2)
                    for n in range(2):
                        for k in range(NM):
                            p.op("tensor", lambda E, kk=k, nn=n, tt=tl, b=b0:
                                 E.matmul(
                                     PS[b + nn][:, 0:N512],
                                     EHW[:, kk, tt * 128:(tt + 1) * 128],
                                     EW2RES[:, kk, nn * 512:(nn + 1) * 512],
                                     start=(kk == 0), stop=False))
                        p.items["tensor"].pop()
                        p.op("tensor", lambda E, nn=n, tt=tl, b=b0: E.matmul(
                            PS[b + nn][:, 0:N512],
                            EHW[:, NM - 1, tt * 128:(tt + 1) * 128],
                            EW2RES[:, NM - 1, nn * 512:(nn + 1) * 512],
                            start=False, stop=True),
                            inc=(s_mm if n == 1 else None))
                    mmv = s_mm.total
                    p.wait("vector", s_mm, mmv)
                    for n in range(2):
                        p.op("vector", lambda E, tt=t, nn=n, b=b0: E.tensor_tensor(
                            X[:, tt, nn * 512:(nn + 1) * 512],
                            X[:, tt, nn * 512:(nn + 1) * 512],
                            PS[b + nn][:, 0:N512], ALU.add),
                            inc=(s_fb if n == 1 else None))
                    if tl % 2 == 1:
                        p.wait("tensor", s_fb, s_fb.total)
            ew2_free_mm = s_mm.total
        return

    if stop != "mamba":
        moe_l0(stop)
    MOE_DONE = s_fb.total
    tap_store("x_after_l0", X.ap(), (128, NT, D), F32,
              wait=[(s_fb, MOE_DONE)])

    WDQB = arn_b(0, NK, DC)
    WDKVB = arn_b(512, NK, DC)
    WKRB = arn_b(1024, NK, DHR)
    WUQB = arn_b(1536, NHEAD * DK)[0:DC + 1, :]
    WUKB = arn_b(3584, NHEAD * DK)[0:DC + 1, :]
    WUVB = arn_b(5632, NHEAD * DK)[0:DC + 1, :]
    WQRB = arn_b(7680, NHEAD * DHR)[0:DC + 1, :]
    WOBH = arn_b(8192, NK, N512)
    CQT = arn_b(8192, T)[0:DC + 1, :]
    KX67 = arn_b(12288, T)
    WEFF = arn_b(16384, NHEAD, D)
    CKVTM = arn_b(32768, NT, 64)
    WUQT = arn_b(34816, NHEAD, 64)
    WUKT = arn_b(35840, NHEAD, 64)
    WUVT = arn_b(36864, NHEAD, 64)
    MHB = arn_b(37888, NHEAD, 64)
    RCPR = arn_b(38912, 1024)[0:1, :]
    BORB2 = arn_b(38912, D)[0:1, :]
    PT6 = arn_b(40960, 6, N512)
    RCPB33 = arn_b(47104, 1024)[0:DC, :]
    KN2R = arn_b(47104, 1024)[0:1, :]
    KN2R96 = arn_b(47104, 1024)
    KEYR = arn_b(0, 2, NK, H1)
    ESELN = arn_b(4096, D)[0:E1, :]
    SELTB = arn_b(4096, T)[0:E1, :]
    VALB = arn_b(12288, E1, D)
    SELBH = arn_b(8192, 1024)
    ESELTB = arn_b(45056, NK, E1)
    Q67A = BIGT.ap()[0:64, :].rearrange("p (h t) -> p h t", h=NHEAD)
    UH = BIGT.ap()[64:96, :].rearrange("p (h t) -> p h t", h=NHEAD)
    UHC = BIGT.ap()[0:32, :].rearrange("p (h t) -> p h t", h=NHEAD)

    def l1_layer():
        p.wait("vector", s_fb, s_fb.total)
        p.wait("scalar", s_fb, s_fb.total)
        p.wait("tensor", s_fb, s_fb.total)
        xnt_l1, _ = layernorm(add_xn_to_x=False)
        if stop == "ln1":
            return

        p.wait("gpsimd", s_mm, s_mm.total)
        dma("gpsimd", WDQB, WDQ.rearrange("(k p) c -> p k c", p=128), dma_w)
        dma("gpsimd", WDKVB, WDKV.rearrange("(k p) c -> p k c", p=128), dma_w)
        dma("gpsimd", WKRB, WKR.rearrange("(k p) c -> p k c", p=128), dma_w)
        dma("gpsimd", WUQB, WUQ, dma_w)
        dma("gpsimd", WUKB, WUK, dma_w)
        dma("gpsimd", WUVB, WUV, dma_w)
        dma("gpsimd", WQRB, WQR, dma_w)
        dma("gpsimd", BORB2, BOR, dma_w)
        w_attn = dma_w.total
        dma("gpsimd", WOBH,
            WO[:, 0:512].rearrange("(k p) d -> p k d", p=128), dma_w)
        w_wo0 = dma_w.total
        if stop == "attw":
            p.wait("sync", dma_w, w_wo0)
            return

        p.op("vector", lambda E: E.memset(IND33.ap(), 0.0))
        p.op("vector", lambda E: E.memset(IND33[0:1, 32:33], 1.0), inc=s_vv)
        ind_ok = s_vv.total
        if stop == "attind":
            p.wait("sync", s_vv, ind_ok)
            return

        p.wait("tensor", dma_w, w_attn)
        p.wait("tensor", s_ex, s_ex.total)
        p.wait("tensor", s_vv, s_vv.total)
        p.wait("tensor", s_fb, s_fb.total)
        bank_gate = {}

        def bank_wait(b):
            g = bank_gate.get(b)
            if g is not None:
                p.wait("tensor", g[0], g[1])

        tr_srcs = ((WUQB, WUQT),) if stop == "atttr1" else (
            (WUQB, WUQT), (WUKB, WUKT), (WUVB, WUVT))
        for src, dst in tr_srcs:
            for h in range(NHEAD):
                bank = h % 2
                bank_wait(bank)
                p.op("tensor", lambda E, s=src, hh=h, b=bank: E.transpose(
                    PS[b].ap().bitcast(BF16)[:, 0:DC + 1],
                    s[0:DC + 1, hh * 128:(hh + 1) * 128],
                    IDB[0:DC + 1, 0:DC + 1]), inc=s_mm)
                mmv = s_mm.total
                p.wait("vector", s_mm, mmv)
                p.op("vector", lambda E, d=dst, hh=h, b=bank: E.tensor_copy(
                    d[:, hh, 0:DC + 1], PS[b].ap().bitcast(BF16)[:, 0:DC + 1]),
                    inc=s_vv)
                bank_gate[bank] = (s_vv, s_vv.total)
        if stop == "atttr1":
            p.wait("sync", s_vv, s_vv.total)
            return
        p.wait("tensor", s_vv, s_vv.total)
        for h in range(NHEAD):
            bank = 2 + h % 2
            bank_wait(bank)
            p.op("tensor", lambda E, hh=h, b=bank: E.matmul(
                PS[b][0:DC + 1, 0:DC + 1], WUQT[:, hh, 0:DC + 1],
                WUKT[:, hh, 0:DC + 1], start=True, stop=True), inc=s_mm)
            mmv = s_mm.total
            p.wait("vector", s_mm, mmv)
            p.op("vector", lambda E, hh=h, b=bank: E.tensor_copy(
                MHB[0:DC + 1, hh, 0:DC + 1], PS[b][0:DC + 1, 0:DC + 1]),
                inc=s_vv)
            bank_gate[bank] = (s_vv, s_vv.total)
        if stop == "atttr":
            p.wait("sync", s_vv, s_vv.total)
            return
        for n in range(2):
            p.op_noop = None
            p.wait("tensor", dma_w, w_wo0 if n == 0 else w_wo1)
            if n == 0:
                p.wait("tensor", s_vv, ind_ok)
            for h in range(NHEAD):
                bank = 4 + h % 2
                bank_wait(bank)
                if h == 0:
                    p.wait("tensor", s_vv, s_vv.total)
                if h == 0:
                    p.op("tensor", lambda E, b=bank: E.matmul(
                        PS[b][0:DC + 1, 0:N512], WUVT[:, 0, 0:DC + 1],
                        WOBH[:, 0, :], start=True, stop=False))
                    p.op("tensor", lambda E, nn=n, b=bank: E.matmul(
                        PS[b][0:DC + 1, 0:N512], IND33[0:1, 0:DC + 1],
                        BORB2[0:1, nn * 512:(nn + 1) * 512],
                        start=False, stop=True), inc=s_mm)
                else:
                    p.op("tensor", lambda E, hh=h, b=bank: E.matmul(
                        PS[b][0:DC + 1, 0:N512], WUVT[:, hh, 0:DC + 1],
                        WOBH[:, hh, :], start=True, stop=True), inc=s_mm)
                mmv = s_mm.total
                p.wait("vector", s_mm, mmv)
                p.op("vector", lambda E, hh=h, nn=n, b=bank: E.tensor_copy(
                    WEFF[0:DC + 1, hh, nn * 512:(nn + 1) * 512],
                    PS[b][0:DC + 1, 0:N512]), inc=s_vv)
                bank_gate[bank] = (s_vv, s_vv.total)
            if n == 0:
                weff_n0 = s_mm.total
                p.wait("gpsimd", s_mm, weff_n0)
                dma("gpsimd", WOBH,
                    WO[:, 512:1024].rearrange("(k p) d -> p k d", p=128),
                    dma_w)
                w_wo1 = dma_w.total
        weff_mm = s_mm.total
        WEFF_OK = s_vv.total
        if stop == "weff":
            p.wait("sync", s_vv, WEFF_OK)
            return

        p.wait("vector", s_mm, weff_mm)
        p.op("vector", lambda E: E.memset(CQT[DC:DC + 1, :], 1.0))
        p.op("vector", lambda E: E.memset(CKVTM[:, :, 32:33], 1.0), inc=s_vv)
        ones_ok = s_vv.total
        p.wait("tensor", s_tr, xnt_l1)
        p.wait("tensor", s_vv, ones_ok)

        def proj32(dst, lhs_w, bias_ap, nm):
            for n in range(4):
                bank = n % 2
                if s_ex.total >= 1:
                    p.wait("tensor", s_ex, s_ex.total - 1)
                for k in range(NK):
                    p.op("tensor", lambda E, kk=k, nn=n, b=bank, w=lhs_w,
                         m=nm: E.matmul(
                             PS[b][0:m, 0:N512], w[:, kk, :],
                             XNT[:, kk, nn * 512:(nn + 1) * 512],
                             start=(kk == 0), stop=(kk == NK - 1)),
                         inc=(s_mm if k == NK - 1 else None))
                mmv = s_mm.total
                p.wait("scalar", s_mm, mmv)
                p.op("scalar", lambda E, nn=n, b=bank, d=dst, ba=bias_ap, m=nm:
                     E.activation(d[0:m, nn * 512:(nn + 1) * 512],
                                  PS[b][0:m, 0:N512], AF.Identity, bias=ba),
                     inc=s_ex)

        proj32(CQT, WDQB, BDQS.ap(), DC)
        proj32(KX67, WDKVB, BDKVS.ap(), DC)
        proj32(KX67[32:64, :], WKRB, BKRS.ap(), DHR)
        p.wait("tensor", s_ex, s_ex.total)
        proj_ex = s_ex.total

        for tp in range(NT):
            bank = tp % 2
            bank_wait(bank)
            p.op("tensor", lambda E, t_=tp, b=bank: E.transpose(
                PS[b].ap().bitcast(BF16)[:, 0:DC],
                KX67[0:DC, t_ * 128:(t_ + 1) * 128],
                IDB[0:DC, 0:DC]), inc=s_mm)
            mmv = s_mm.total
            p.wait("vector", s_mm, mmv)
            p.op("vector", lambda E, t_=tp, b=bank: E.tensor_copy(
                CKVTM[:, t_, 0:DC], PS[b].ap().bitcast(BF16)[:, 0:DC]),
                inc=s_vv)
            bank_gate[bank] = (s_vv, s_vv.total)
        n_heads = 1 if stop == "h1" else NHEAD
        for h in range(n_heads):
            for part in range(2):
                for n in range(4):
                    bank = 2 * part + n % 2
                    bank_wait(bank)
                    if part == 0:
                        p.op("tensor", lambda E, hh=h, nn=n, b=bank: E.matmul(
                            PS[b][0:DC, 0:N512], MHB[0:DC + 1, hh, 0:DC],
                            CQT[0:DC + 1, nn * 512:(nn + 1) * 512],
                            start=True, stop=True), inc=s_mm)
                    else:
                        p.op("tensor", lambda E, hh=h, nn=n, b=bank: E.matmul(
                            PS[b][0:DHR, 0:N512],
                            WQRB[0:DC + 1, hh * DHR:(hh + 1) * DHR],
                            CQT[0:DC + 1, nn * 512:(nn + 1) * 512],
                            start=True, stop=True), inc=s_mm)
                    mmv = s_mm.total
                    p.wait("vector", s_mm, mmv)
                    if part == 0:
                        p.op("vector", lambda E, hh=h, nn=n, b=bank:
                             E.tensor_copy(
                                 Q67A[0:32, hh, nn * 512:(nn + 1) * 512],
                                 PS[b][0:DC, 0:N512]), inc=s_vv)
                    else:
                        p.op("vector", lambda E, hh=h, nn=n, b=bank:
                             E.tensor_copy(
                                 Q67A[32:64, hh, nn * 512:(nn + 1) * 512],
                                 PS[b][0:DHR, 0:N512]), inc=s_vv)
                    bank_gate[bank] = (s_vv, s_vv.total)
        PREP_V = s_vv.total
        if stop == "attprep":
            p.wait("sync", s_vv, PREP_V)
            return

        exp_of_bank = {}
        pv_free_v = PREP_V
        for h in range(n_heads):
            for hh in range(2):
                toff = hh * 1024
                p.wait("tensor", s_vv, pv_free_v)
                groups = [list(range(3 * g, min(3 * g + 3, NT)))
                          for g in range(6)]
                for tps in groups:
                    for tp in tps:
                        for n in range(2):
                            slot = (2 * tp + n) % 6
                            if slot in exp_of_bank:
                                p.wait("tensor", s_ex, exp_of_bank[slot])
                            p.op("tensor", lambda E, t_=tp, nn=n, sl=slot,
                                 to=toff, h_=h: E.matmul(
                                     PS[sl][:, 0:N512],
                                     KX67[0:64, t_ * 128:(t_ + 1) * 128],
                                     Q67A[0:64, h_,
                                          to + nn * 512:to + (nn + 1) * 512],
                                     start=True, stop=True),
                                 inc=(s_mm if (tp == tps[-1] and n == 1)
                                      else None))
                    sc_mm = s_mm.total
                    p.wait("scalar", s_mm, sc_mm)
                    for tp in tps:
                        for n in range(2):
                            slot = (2 * tp + n) % 6
                            p.op("scalar", lambda E, sl=slot: E.activation(
                                PT6[:, sl, :], PS[sl][:, 0:N512], AF.Exp,
                                scale=SCALE), inc=s_ex)
                            exp_of_bank[slot] = s_ex.total
                    for tp in tps:
                        for n in range(2):
                            slot = (2 * tp + n) % 6
                            p.wait("tensor", s_ex, exp_of_bank[slot])
                            p.op("tensor", lambda E, t_=tp, nn=n, sl=slot:
                                 E.matmul(
                                     PS[6 + nn][0:DC + 1, 0:N512],
                                     CKVTM[:, t_, 0:DC + 1], PT6[:, sl, :],
                                     start=(t_ == 0), stop=(t_ == NT - 1)),
                                 inc=(s_mm if (tp == NT - 1 and n == 1)
                                      else None))
                pv_mm = s_mm.total
                p.wait("vector", s_mm, pv_mm)

                def _recip(E, nn):
                    with nc.allow_low_precision(
                            reason="softmax 1/sum broadcast operand in bf16; "
                                   "P and V operands are bf16 anyway"):
                        return E.reciprocal(
                            RCPR[0:1, nn * 512:(nn + 1) * 512],
                            PS[6 + nn][32:33, 0:N512])
                for n in range(2):
                    p.op("vector", lambda E, nn=n: _recip(E, nn),
                         inc=(s_vv if n == 1 else None))
                rs_ok = s_vv.total
                p.wait("tensor", s_vv, rs_ok)
                for n in range(2):
                    p.op("tensor", lambda E, nn=n: E.matmul(
                        PS[4 + nn][0:DC, 0:N512], ONESR[0:1, 0:DC],
                        RCPR[0:1, nn * 512:(nn + 1) * 512],
                        start=True, stop=True),
                        inc=(s_mm if n == 1 else None))
                mmv = s_mm.total
                p.wait("vector", s_mm, mmv)
                for n in range(2):
                    p.op("vector", lambda E, nn=n: E.tensor_copy(
                        RCPB33[0:DC, nn * 512:(nn + 1) * 512],
                        PS[4 + nn][0:DC, 0:N512]))
                p.emit("vector", lambda E: E.drain())
                for n in range(2):
                    p.op("vector", lambda E, h_=h, nn=n, to=toff:
                         E.tensor_tensor(
                             UH[0:32, h_, to + nn * 512:to + (nn + 1) * 512],
                             PS[6 + nn][0:DC, 0:N512],
                             RCPB33[0:DC, nn * 512:(nn + 1) * 512],
                             ALU.mult), inc=s_vv)
                pv_free_v = s_vv.total
        if stop in ("h1", "attn"):
            p.wait("sync", s_vv, s_vv.total)
            return

        p.wait("vector", s_vv, s_vv.total)
        p.wait("vector", s_mm, s_mm.total)
        p.op("vector", lambda E: E.tensor_copy(
            KN2R96[32:33, 0:1024], WEFF[32:33, 0, :]))
        for h2 in range(1, NHEAD):
            p.emit("vector", lambda E: E.drain())
            p.op("vector", lambda E, hh=h2: E.tensor_tensor(
                KN2R96[32:33, 0:1024], KN2R96[32:33, 0:1024],
                WEFF[32:33, hh, :], ALU.add))
        p.emit("vector", lambda E: E.drain())
        p.op("vector", lambda E: E.tensor_copy(
            KN2R[0:1, 0:1024], KN2R96[32:33, 0:1024]))
        for h2 in range(NHEAD):
            p.op("vector", lambda E, hh=h2: E.tensor_copy(
                UHC[0:32, hh, :], UH[0:32, hh, :]),
                inc=(s_vv if h2 == NHEAD - 1 else None))
        w32_ok = s_vv.total
        p.wait("tensor", s_vv, w32_ok)
        for t in range(NT):
            b0 = 2 + 2 * (t % 2)
            for n in range(2):
                for h in range(NHEAD):
                    p.op("tensor", lambda E, hh=h, nn=n, tt=t, b=b0: E.matmul(
                        PS[b + nn][:, 0:N512],
                        UHC[0:32, hh, tt * 128:(tt + 1) * 128],
                        WEFF[0:DC, hh, nn * 512:(nn + 1) * 512],
                        start=(hh == 0), stop=False))
                p.op("tensor", lambda E, nn=n, b=b0: E.matmul(
                    PS[b + nn][:, 0:N512], ONESR[0:1, 0:128],
                    KN2R[0:1, nn * 512:(nn + 1) * 512],
                    start=False, stop=True),
                    inc=(s_mm if n == 1 else None))
            mmv = s_mm.total
            p.wait("vector", s_mm, mmv)
            for n in range(2):
                p.op("vector", lambda E, tt=t, nn=n, b=b0: E.tensor_tensor(
                    X[:, tt, nn * 512:(nn + 1) * 512],
                    X[:, tt, nn * 512:(nn + 1) * 512],
                    PS[b + nn][:, 0:N512], ALU.add),
                    inc=(s_fb if n == 1 else None))
            if t % 2 == 1:
                p.wait("tensor", s_fb, s_fb.total)
        WO_DONE = s_fb.total
        if stop == "wo":
            return
        p.wait("gpsimd", s_mm, s_mm.total)
        dma("gpsimd", ESELN, ESEL, dma_w)
        w_esel = dma_w.total
        val_st = {"j": 0}

        def val_chunks(n_c):
            for _ in range(n_c):
                kk = val_st["j"]
                dma("gpsimd", VALB[:, 2 * kk:2 * kk + 2, :],
                    VALUES.rearrange("e p v -> p e v")[:, 2 * kk:2 * kk + 2, :],
                    dma_w)
                val_st["j"] = kk + 1

        if stop == "ffnw":
            val_chunks(8)
            p.wait("sync", dma_w, dma_w.total)
            return
        p.wait("tensor", dma_w, w_esel)
        p.wait("tensor", s_fb, WO_DONE)
        for k in range(NK):
            bank = k % 2
            p.wait("tensor", s_vv, s_vv.total)
            p.op("tensor", lambda E, kk=k, b=bank: E.transpose(
                PS[b].ap().bitcast(BF16)[:, 0:E1],
                ESELN[0:E1, kk * 128:(kk + 1) * 128], IDB[0:E1, 0:E1]),
                inc=s_mm)
            mmv = s_mm.total
            p.wait("vector", s_mm, mmv)
            p.op("vector", lambda E, kk=k, b=bank: E.tensor_copy(
                ESELTB[:, kk, :], PS[b].ap().bitcast(BF16)[:, 0:E1]),
                inc=s_vv)
        es_ok = s_vv.total
        if stop == "eselt":
            return
        p.wait("tensor", s_vv, es_ok)
        for n in range(4):
            bank = n % 2
            if s_ex.total >= 1:
                p.wait("tensor", s_ex, s_ex.total - 1)
            for k in range(NK):
                p.op("tensor", lambda E, kk=k, nn=n, b=bank: E.matmul(
                    PS[b][0:E1, 0:N512], ESELTB[:, kk, :],
                    XNT[:, kk, nn * 512:(nn + 1) * 512],
                    start=(kk == 0), stop=(kk == NK - 1)),
                    inc=(s_mm if k == NK - 1 else None))
            mmv = s_mm.total
            p.wait("scalar", s_mm, mmv)
            p.op("scalar", lambda E, nn=n, b=bank: E.activation(
                SELTB[0:E1, nn * 512:(nn + 1) * 512], PS[b][0:E1, 0:N512],
                AF.Sigmoid), inc=s_ex)
        SW = BIGT.ap().rearrange("p (e t) -> p e t", e=E1)
        key_hist = []
        if stop == "sel":
            return
        for ht in range(2):
            toff = ht * 1024
            for e in range(E1):
                gk = len(key_hist)
                ring = gk % 2
                if gk >= 2:
                    p.wait("gpsimd", s_mm, key_hist[gk - 2])
                dma("gpsimd", KEYR[:, ring, :, :],
                    KEYS[e, :, :].rearrange("(k p) hh -> p k hh", p=128), dma_w)
                w_blk = dma_w.total
                if ht == 0 and e < 8:
                    val_chunks(1)
                    if e == 7:
                        w_val = dma_w.total
                p.wait("scalar", s_ex, s_ex.total)
                p.op("scalar", lambda E, ee=e: E.dma_start(
                    out=WROW[0:1, 0:T], in_=SELTB[ee:ee + 1, :]),
                    inc=s_ev, inc_n=16)
                sr_ok = s_ev.total
                p.wait("tensor", s_ev, sr_ok)
                p.wait("tensor", s_vv, s_vv.total)
                for n in range(2):
                    p.op("tensor", lambda E, nn=n, to=toff: E.matmul(
                        PS[6 + nn][:, 0:N512], ONESR[0:1, 0:128],
                        WROW[0:1, to + nn * 512:to + (nn + 1) * 512],
                        start=True, stop=True),
                        inc=(s_mm if n == 1 else None))
                mmv = s_mm.total
                p.wait("vector", s_mm, mmv)
                for n in range(2):
                    p.op("vector", lambda E, nn=n: E.tensor_copy(
                        SELBH[:, nn * 512:(nn + 1) * 512],
                        PS[6 + nn][:, 0:N512]), inc=(s_vv if n == 1 else None))
                selb_ok = s_vv.total
                p.wait("tensor", dma_w, w_blk)
                for n in range(2):
                    g = e * 2 + n
                    bank = g % 2
                    if s_ex.total >= 1:
                        p.wait("tensor", s_ex, s_ex.total - 1)
                    for k in range(NK):
                        p.op("tensor", lambda E, kk=k, rr=ring, nn=n, b=bank,
                             to=toff: E.matmul(
                                 PS[b][:, 0:N512], KEYR[:, rr, kk, :],
                                 XNT[:, kk, to + nn * 512:to + (nn + 1) * 512],
                                 start=(kk == 0), stop=(kk == NK - 1)),
                             inc=(s_mm if k == NK - 1 else None))
                    mmv = s_mm.total
                    p.wait("scalar", s_mm, mmv)
                    p.op("scalar", lambda E, ee=e, nn=n, b=bank: E.activation(
                        SW[:, ee, nn * 512:(nn + 1) * 512], PS[b][:, 0:N512],
                        AF.Relu), inc=s_ex)
                    p.wait("vector", s_ex, s_ex.total)
                    p.wait("vector", s_vv, selb_ok)
                    p.op("vector", lambda E, ee=e, nn=n: E.tensor_tensor(
                        SW[:, ee, nn * 512:(nn + 1) * 512],
                        SW[:, ee, nn * 512:(nn + 1) * 512],
                        SELBH[:, nn * 512:(nn + 1) * 512], ALU.mult),
                        inc=s_vv)
                key_hist.append(s_mm.total)
            if stop == "keysh0":
                return
            p.wait("tensor", s_vv, s_vv.total)
            p.wait("tensor", dma_w, w_val)
            for tl in range(8):
                t = ht * 8 + tl
                b0 = 2 + 2 * (tl % 2)
                for n in range(2):
                    for e in range(E1):
                        p.op("tensor", lambda E, ee=e, nn=n, tt=tl, b=b0:
                             E.matmul(
                                 PS[b + nn][:, 0:N512],
                                 SW[:, ee, tt * 128:(tt + 1) * 128],
                                 VALB[:, ee, nn * 512:(nn + 1) * 512],
                                 start=(ee == 0), stop=(ee == E1 - 1)),
                             inc=(s_mm if (e == E1 - 1 and n == 1) else None))
                mmv = s_mm.total
                p.wait("vector", s_mm, mmv)
                for n in range(2):
                    p.op("vector", lambda E, tt=t, nn=n, b=b0: E.tensor_tensor(
                        X[:, tt, nn * 512:(nn + 1) * 512],
                        X[:, tt, nn * 512:(nn + 1) * 512],
                        PS[b + nn][:, 0:N512], ALU.add),
                        inc=(s_fb if n == 1 else None))
                fbv = s_fb.total
                if tl % 2 == 1:
                    p.wait("tensor", s_fb, fbv)
                if stop == "valh0" and tl == 7:
                    return
                p.wait("sync", s_fb, fbv)
                p.op("sync", lambda E, tt=t: E.dma_start(
                    out=OUT.rearrange("(n p) d -> p n d", p=128)[:, tt, :],
                    in_=X[:, tt, :]), inc=s_st, inc_n=16)
        return

    L0_STOPS = ("mamba", "rw1", "logits", "top2", "wbc", "ew1h0", "e1",
                "ew2dma", "wrow")
    if stop not in L0_STOPS:
        l1_layer()
    L1_DONE = s_st.total

    if stop != "":
        p.wait("sync", s_fb, s_fb.total)
        p.wait("sync", s_gp, s_gp.total)
        p.wait("sync", s_vv, s_vv.total)
        p.wait("sync", s_ex, s_ex.total)
        p.op("sync", lambda E: E.dma_start(
            out=OUT.rearrange("(n p) d -> p n d", p=128), in_=X.ap()),
            inc=s_st, inc_n=16)
    p.wait("sync", s_st, s_st.total)

    import contextlib
    with contextlib.ExitStack() as stack:
        for s in p.sems.values():
            s.h = stack.enter_context(nc.semaphore(s.name))
        block = stack.enter_context(nc.Block())
        for name in Plan.ENGINES:
            getattr(block, name)(
                (lambda nm: lambda E: [fn(E) for fn in p.items[nm]])(name))
    return nc, tap_outs


def _prep_inputs(inputs, core):
    g = inputs
    d = {}
    d["x"] = np.ascontiguousarray(g["L0_attn_x"][core])
    d["idm"] = np.eye(128, dtype=np.float32)
    d["jm"] = np.eye(128, dtype=np.float32)[::-1].copy()
    d["win"] = g["L0_attn_W_in"]
    d["binT"] = np.ascontiguousarray(g["L0_attn_b_in"].reshape(1, DI))
    d["binc"] = np.ascontiguousarray(
        g["L0_attn_b_in"].reshape(NM, 128).T)
    d["convw"] = np.ascontiguousarray(
        g["L0_attn_conv_w"][:, 0, :].reshape(NM, 128, 4).transpose(1, 0, 2))
    d["convb"] = np.ascontiguousarray(g["L0_attn_conv_b"].reshape(NM, 128).T)
    d["amat"] = g["L0_attn_A"]
    d["bp"] = g["L0_attn_Bp"].reshape(DS, 1)
    d["cp"] = g["L0_attn_Cp"].reshape(DS, 1)
    d["dp"] = g["L0_attn_Dp"].reshape(1, 1)
    d["wout"] = g["L0_attn_W_out"]
    d["boutr"] = g["L0_attn_b_out"].reshape(1, D)
    d["rw1"] = g["L0_ffn_rw1"]
    d["rb1c"] = np.ascontiguousarray(g["L0_ffn_rb1"].reshape(NK, 128).T)
    d["rw2"] = g["L0_ffn_rw2"]
    d["rb2c"] = g["L0_ffn_rb2"].reshape(E0, 1)
    d["ew1"] = g["L0_ffn_ew1"]
    d["eb1c"] = np.ascontiguousarray(
        g["L0_ffn_eb1"].reshape(E0, NM, 128).transpose(2, 0, 1))
    d["ew2"] = g["L0_ffn_ew2"]
    d["eb2r"] = g["L0_ffn_eb2"]
    d["wdq"] = g["L1_attn_W_DQ"]
    d["bdqc"] = g["L1_attn_b_DQ"].reshape(DC, 1)
    d["wuq"] = np.vstack([g["L1_attn_W_UQ"], g["L1_attn_b_UQ"][None]])
    d["wdkv"] = g["L1_attn_W_DKV"]
    d["bdkvc"] = g["L1_attn_b_DKV"].reshape(DC, 1)
    d["wuk"] = np.vstack([g["L1_attn_W_UK"], g["L1_attn_b_UK"][None]])
    d["wuv"] = np.vstack([g["L1_attn_W_UV"], g["L1_attn_b_UV"][None]])
    d["wqr"] = np.vstack([g["L1_attn_W_QR"], g["L1_attn_b_QR"][None]])
    d["wkr"] = g["L1_attn_W_KR"]
    d["bkrc"] = g["L1_attn_b_KR"].reshape(DHR, 1)
    d["wo"] = g["L1_attn_W_O"]
    d["bor"] = g["L1_attn_b_O"].reshape(1, D)
    d["esel"] = g["L1_ffn_expert_sel"]
    d["keys"] = g["L1_ffn_keys"]
    d["values"] = g["L1_ffn_values"]
    return {k: np.ascontiguousarray(np.asarray(v).astype(np.float32, copy=False))
            for k, v in d.items()}


def kernel(**inputs):
    nc, _ = build_nc()
    in_maps = [_prep_inputs(inputs, c) for c in range(8)]
    res = run_bass_kernel_spmd(nc, in_maps, core_ids=list(range(8)))
    return np.stack([r["out"] for r in res.results], axis=0)
